# version 15
# baseline (speedup 1.0000x reference)
import numpy as np
import contextlib
import concourse.bass as bass
import concourse.mybir as mybir
from concourse.bass_utils import run_bass_kernel_spmd

F32 = mybir.dt.float32
BF16 = mybir.dt.bfloat16
AF = mybir.ActivationFunctionType
ALU = mybir.AluOpType

C = 1024
S = 2048
HID = 4096
NCH = C // 128
NOC = HID // 128
NQ = 4
TQ = S // NQ
EPS_LN = 1e-5
EPS_ATTN = 1e-6
NHEAD = 16

ENGINES = ("sync", "tensor", "vector", "scalar", "gpsimd")
TT, TS, STT = "tensor_tensor", "tensor_scalar", "scalar_tensor_tensor"


class Emitter:
    def __init__(self):
        self.q = {e: [] for e in ENGINES}
        self.cnt = {}

    def new_sem(self, name):
        self.cnt[name] = 0

    def op(self, eng, meth, *args, inc=None, **kw):
        if inc is not None:
            self.cnt[inc[0]] += inc[1]
        self.q[eng].append(("op", meth, args, kw, inc))
        return self.cnt[inc[0]] if inc else None

    def dma(self, eng, out, in_, sem):
        return self.op(eng, "dma_start", inc=(sem, 16), out=out, in_=in_)

    def wait(self, eng, sem, val):
        assert val <= self.cnt[sem], f"wait {sem}>={val} > emitted {self.cnt[sem]}"
        if val > 0:
            self.q[eng].append(("wait", sem, val))

    def wait_dma(self, eng, sem, val):
        assert val <= self.cnt[sem], f"wait {sem}>={val} > emitted {self.cnt[sem]}"
        if val > 0:
            self.q[eng].append(("dwait", sem, val))

    def resolve_dwaits(self):
        for eng in ENGINES:
            for i, item in enumerate(self.q[eng]):
                if item[0] == "dwait":
                    _, sem, val = item
                    self.q[eng][i] = ("wait", sem, min(val + 16, self.cnt[sem]))

    def replay(self, eng_obj, eng_name, sems):
        for item in self.q[eng_name]:
            if item[0] == "wait":
                _, sem, val = item
                eng_obj.wait_ge(sems[sem], val)
            else:
                _, meth, args, kw, inc = item
                inst = getattr(eng_obj, meth)(*args, **kw)
                if inc is not None:
                    inst.then_inc(sems[inc[0]], inc[1])


def build_program(nq=NQ, stop_after=99):
    nc = bass.Bass("TRN2", target_bir_lowering=False, debug=False)

    xT = nc.dram_tensor("xT", [C, S], F32, kind="ExternalInput").ap()
    w1g = nc.dram_tensor("w1g", [8, 128, 4096], F32, kind="ExternalInput").ap()
    w2g = nc.dram_tensor("w2g", [8, 128, 4096], F32, kind="ExternalInput").ap()
    h1g = nc.dram_tensor("h1g", [32, 128, 4096], F32, kind="ExternalInput").ap()
    h2g = nc.dram_tensor("h2g", [32, 128, 4096], F32, kind="ExternalInput").ap()
    w3g = nc.dram_tensor("w3g", [8, 128, 4096], F32, kind="ExternalInput").ap()
    s1i = nc.dram_tensor("s1", [128, 32], F32, kind="ExternalInput").ap()
    s2i = nc.dram_tensor("s2", [128, 32], F32, kind="ExternalInput").ap()
    h1si = nc.dram_tensor("h1s", [128, 32], F32, kind="ExternalInput").ap()
    h2si = nc.dram_tensor("h2s", [128, 32], F32, kind="ExternalInput").ap()
    w3si = nc.dram_tensor("w3s", [128, 8], F32, kind="ExternalInput").ap()
    gami = nc.dram_tensor("gam", [128, 8], F32, kind="ExternalInput").ap()
    beti = nc.dram_tensor("bet", [128, 8], F32, kind="ExternalInput").ap()
    hfili = nc.dram_tensor("hfil", [128, 8, 4], F32, kind="ExternalInput").ap()
    gfili = nc.dram_tensor("gfil", [128, 8, 4], F32, kind="ExternalInput").ap()
    hsyni = nc.dram_tensor("hsyn", [128, 8, 4], F32, kind="ExternalInput").ap()
    gsyni = nc.dram_tensor("gsyn", [128, 8, 4], F32, kind="ExternalInput").ap()
    identi = nc.dram_tensor("ident", [128, 128], F32, kind="ExternalInput").ap()
    yT = nc.dram_tensor("yT", [C, S], F32, kind="ExternalOutput").ap()
    ao_scr = nc.dram_tensor("ao_scr", [NCH, 128, S], BF16, kind="Internal").ap()
    l0_scr = nc.dram_tensor("l0_scr", [NCH, 128, 1026], BF16, kind="Internal").ap()
    h0_scr = nc.dram_tensor("h0_scr", [NCH, 128, 1026], BF16, kind="Internal").ap()
    r1_scr = nc.dram_tensor("r1_scr", [NCH, 128, 1026], BF16, kind="Internal").ap()
    low_scr = nc.dram_tensor("low_scr", [NCH, 128, S], BF16, kind="Internal").ap()
    ws_s1 = nc.dram_tensor("ws_s1", [8, 2, 128, 4096], BF16, kind="Internal").ap()
    ws_s23 = nc.dram_tensor("ws_s23", [2, 32, 128, 4096], BF16, kind="Internal").ap()
    ws_s4 = nc.dram_tensor("ws_s4", [8, 128, 4096], BF16, kind="Internal").ap()

    em = Emitter()
    semnames = (["qC", "qA0", "qA1", "qA2", "qA3", "qA4", "qA5", "qA6", "qA7",
                 "qX0", "qX1", "qO0", "qO1",
                 "qF0", "qF1", "qLs0", "qLs1", "qAO0", "qAO1",
                 "tPE", "tACT", "tDVE", "tPOOL"]
                + [f"qW{i}" for i in range(4)] + [f"qS{i}" for i in range(4)]
                + [f"qT{i}" for i in range(4)])
    for sname in semnames:
        em.new_sem(sname)

    ctxs = []

    def sbuf(name, shape, dt):
        t = nc.sbuf_tensor(name, shape, dt)
        ctxs.append(t)
        return t.__enter__()

    def ps_(name, shape, dt=F32):
        t = nc.psum_tensor(name, shape, dt)
        ctxs.append(t)
        return t.__enter__()

    xn = sbuf("xn", [128, NCH * S], BF16)
    abuf = sbuf("abuf", [128, NOC * TQ], BF16)
    bbuf = sbuf("bbuf", [128, NOC * TQ], BF16)
    wb = sbuf("wb", [128, 4 * 4096], BF16)
    bcR = sbuf("bcR", [128, S], F32)
    bcB = sbuf("bcB", [128, S], F32)
    aren = sbuf("aren", [128, 16448], BF16)
    aob = sbuf("aob", [128, 4096], BF16)
    t1a = sbuf("t1a", [128, 2 * TQ], F32)
    t2a = sbuf("t2a", [128, 2 * TQ], F32)
    xq = sbuf("xq", [128, 2 * TQ], F32)
    aoq = sbuf("aoq", [128, 2 * TQ], BF16)
    outb = sbuf("outb", [128, 2 * TQ], F32)
    scl = sbuf("scl", [128, 152], F32)
    filt = sbuf("filt", [128, 128], F32)
    fcmb = sbuf("fcmb", [128, 80], F32)
    ident = sbuf("identsb", [128, 128], F32)
    identb = sbuf("identbb", [128, 128], BF16)
    onesc = sbuf("onesc", [128, 1], F32)
    cvals = sbuf("cvals", [128, 2], F32)
    dtail = sbuf("dtail", [128, 8], F32)
    dtailb = sbuf("dtailb", [128, 8], BF16)
    onesr = sbuf("onesr", [1, 128], F32)
    kvs = sbuf("kvs", [128, NCH * 65], BF16)

    ar = aren.ap()
    azA = ar[:, 0:4112].bitcast(F32)
    azB = ar[:, 4112:8224].bitcast(F32)
    R3 = 8224
    R4 = 12336
    low2v = ar[:, 0:4096]
    h1v = ar[:, 4112:8224]
    qfmv = ar[:, R3:R3 + 4096]
    vtb = [ar[:, R4 + i * 258: R4 + i * 258 + 130] for i in range(4)]
    qtb = [ar[:, R4 + i * 258 + 130: R4 + i * 258 + 258] for i in range(4)]
    attv = ar[:, R4:R4 + 4112]
    sb4 = [ar[:, R3 + i * 1028: R3 + i * 1028 + 1028] for i in range(4)]
    lowsl = [ar[:, 0:2048], ar[:, R4:R4 + 2048]]
    fyb = [ar[:, R3 + i * 1024: R3 + (i + 1) * 1024].bitcast(F32) for i in range(2)]
    fab = [ar[:, R3 + 2048 + i * 512: R3 + 2048 + (i + 1) * 512] for i in range(2)]
    aosl = [ar[:, R3 + i * 2048: R3 + (i + 1) * 2048] for i in range(2)]
    aof = aob.ap()[:, 0:4096].bitcast(F32)

    s1v, s2v = scl[:, 0:32], scl[:, 32:64]
    h1sv, h2sv = scl[:, 64:96], scl[:, 96:128]
    w3sv, gamv, betv = scl[:, 128:136], scl[:, 136:144], scl[:, 144:152]

    p = [ps_(f"p{i}", [128, 512], F32) for i in range(6)]
    ptr = [ps_(f"ptr{i}", [128, 128], BF16) for i in range(2)]

    wslot = [wb[:, i * 4096:(i + 1) * 4096] for i in range(4)]

    slot_free = {i: 0 for i in range(4)}
    slot_store = {i: 0 for i in range(4)}

    def _store_slot(slot, dst, load_cnt):
        em.wait("scalar", f"qW{slot}", load_cnt)
        em.dma("scalar", dst, wslot[slot][:, :], f"qS{slot}")
        slot_store[slot] = em.cnt[f"qS{slot}"]
        return slot_store[slot]

    def _wload(slot, src, q, dst_scr):
        em.wait("gpsimd", "tPE", slot_free[slot])
        em.wait("gpsimd", f"qS{slot}", slot_store[slot])
        if q == 0:
            v = em.dma("gpsimd", wslot[slot][:, :], src, f"qW{slot}")
            _store_slot(slot, dst_scr, v)
        else:
            v = em.dma("gpsimd", wslot[slot][:, :], dst_scr, f"qW{slot}")
        return v

    def _wload_direct(slot, src):
        em.wait("gpsimd", "tPE", slot_free[slot])
        em.wait("gpsimd", f"qS{slot}", slot_store[slot])
        return em.dma("gpsimd", wslot[slot][:, :], src, f"qW{slot}")

    def load_s1(g, q):
        ready = []
        for wi in (0, 1):
            slot = (2 * g + wi) % 4
            ready.append((slot, _wload_direct(slot, (w1g, w2g)[wi][g])))
        return ready

    def load_s23(stage_i, g, q):
        slot = g % 4
        wq = stage_i
        if q < wq:
            return [(slot, _wload_direct(slot, (h1g, h2g)[stage_i][g]))]
        return [(slot, _wload(slot, (h1g, h2g)[stage_i][g], q - wq, ws_s23[stage_i, g]))]

    def load_s4(cc, q):
        slot = cc % 4
        return [(slot, _wload_direct(slot, w3g[cc]))]

    def wait_w(eng, ready):
        for slot, v in ready:
            em.wait(eng, f"qW{slot}", v)

    em.dma("sync", ident[:, :], identi, "qC")
    em.dma("sync", scl[:, 0:32], s1i, "qC")
    em.dma("sync", scl[:, 32:64], s2i, "qC")
    em.dma("sync", scl[:, 64:96], h1si, "qC")
    em.dma("sync", scl[:, 96:128], h2si, "qC")
    em.dma("sync", scl[:, 128:136], w3si, "qC")
    em.dma("sync", scl[:, 136:144], gami, "qC")
    em.dma("sync", scl[:, 144:152], beti, "qC")
    em.dma("sync", filt[:, 0:32].rearrange("p (c k) -> p c k", c=8), hfili, "qC")
    em.dma("sync", filt[:, 32:64].rearrange("p (c k) -> p c k", c=8), gfili, "qC")
    em.dma("sync", filt[:, 64:96].rearrange("p (c k) -> p c k", c=8), hsyni, "qC")
    n_const = em.dma("sync", filt[:, 96:128].rearrange("p (c k) -> p c k", c=8),
                     gsyni, "qC")

    em.op("vector", "memset", onesc[:, :], 1.0)
    em.op("vector", "memset", cvals[:, 0:1], EPS_LN)
    em.op("vector", "memset", cvals[:, 1:2], -0.5)
    em.op("vector", "memset", onesr[:, :], 1.0, inc=("tDVE", 1))

    s1_ready = {0: load_s1(0, 0), 1: load_s1(1, 0)}

    abf = abuf.ap()[:, 0:16384].bitcast(F32)
    bbf = bbuf.ap()[:, 0:16384].bitcast(F32)
    xres = ([abf[:, i * 2048:(i + 1) * 2048] for i in range(4)]
            + [bbf[:, i * 2048:(i + 1) * 2048] for i in range(4)])
    sqscr = [azA[:, 0:2048], azB[:, 0:2048]]
    rowA = aof[0:1, 0:2048]
    rowB = azB[0:1, 0:2048]
    m2r = azA[0:1, 0:2048]
    xld = {}
    for cc in range(0, NCH, 2):
        xld[cc] = em.dma("sync", xres[cc], xT[cc * 128:(cc + 1) * 128, :], f"qA{cc}")
    for cc in range(1, NCH, 2):
        xld[cc] = em.dma("scalar", xres[cc], xT[cc * 128:(cc + 1) * 128, :], f"qA{cc}")

    sq_cnt, add_cnt = {}, {}
    for cc in range(NCH):
        if cc >= 2:
            em.wait("scalar", "tDVE", add_cnt[cc - 2])
        em.wait("scalar", f"qA{cc}", xld[cc])
        sq_cnt[cc] = em.op("scalar", "activation", out=sqscr[cc % 2], in_=xres[cc],
                           func=AF.Square, inc=("tACT", 1))
        em.wait("vector", f"qA{cc}", xld[cc])
        if cc == 0:
            em.op("vector", "tensor_copy", out=bcR[:, :], in_=xres[cc])
        else:
            em.op("vector", TT, out=bcR[:, :], in0=bcR[:, :], in1=xres[cc], op=ALU.add)
        em.wait("vector", "tACT", sq_cnt[cc])
        if cc == 0:
            add_cnt[cc] = em.op("vector", "tensor_copy", out=bcB[:, :], in_=sqscr[0],
                                inc=("tDVE", 1))
        else:
            add_cnt[cc] = em.op("vector", TT, out=bcB[:, :], in0=bcB[:, :],
                                in1=sqscr[cc % 2], op=ALU.add, inc=("tDVE", 1))

    em.wait("scalar", "qC", n_const)
    em.op("scalar", "copy", out=identb[:, :], in_=ident[:, :], inc=("tACT", 1))
    identb_done = em.cnt["tACT"]

    em.wait("vector", "qC", n_const)
    for fi, base in ((0, 64), (1, 96)):
        for cc in range(8):
            fsl = filt[:, base + 4 * cc: base + 4 * cc + 4]
            dst = fcmb[:, fi * 40 + 5 * cc: fi * 40 + 5 * cc + 5]
            em.op("vector", "tensor_copy", out=dst[:, 0:1], in_=fsl[:, 0:1])
            em.op("vector", TT, out=dst[:, 1:2], in0=fsl[:, 1:2], in1=fsl[:, 2:3], op=ALU.add)
            em.op("vector", "tensor_copy", out=dst[:, 2:3], in_=fsl[:, 3:4])
            em.op("vector", TT, out=dst[:, 3:4], in0=fsl[:, 0:1], in1=fsl[:, 1:2], op=ALU.add)
            em.op("vector", TT, out=dst[:, 4:5], in0=fsl[:, 2:3], in1=fsl[:, 3:4],
                  op=ALU.add, inc=("tDVE", 1))

    em.wait("tensor", "tDVE", add_cnt[NCH - 1])
    amm, aev, bmm, bev = {}, {}, {}, {}
    for tb in range(4):
        amm[tb] = em.op("tensor", "matmul", p[tb][0:1, 0:512], onesc[:, :],
                        bcR[:, tb * 512:(tb + 1) * 512], start=True, stop=True,
                        inc=("tPE", 1))
    for tb in range(2):
        bmm[tb] = em.op("tensor", "matmul", p[4 + tb][0:1, 0:512], onesc[:, :],
                        bcB[:, tb * 512:(tb + 1) * 512], start=True, stop=True,
                        inc=("tPE", 1))
    for tb in range(4):
        em.wait("scalar", "tPE", amm[tb])
        aev[tb] = em.op("scalar", "copy", out=rowA[:, tb * 512:(tb + 1) * 512],
                        in_=p[tb][0:1, 0:512], inc=("tACT", 1))
    em.wait("tensor", "tACT", aev[1])
    for tb in range(2, 4):
        bmm[tb] = em.op("tensor", "matmul", p[tb - 2][0:1, 0:512], onesc[:, :],
                        bcB[:, tb * 512:(tb + 1) * 512], start=True, stop=True,
                        inc=("tPE", 1))
    for tb in range(4):
        em.wait("scalar", "tPE", bmm[tb])
        bev[tb] = em.op("scalar", "copy", out=rowB[:, tb * 512:(tb + 1) * 512],
                        in_=(p[4 + tb] if tb < 2 else p[tb - 2])[0:1, 0:512],
                        inc=("tACT", 1))

    em.wait("vector", "tACT", bev[3])
    em.op("vector", TS, out=rowA, in0=rowA, scalar1=-1.0 / C, scalar2=None,
          op0=ALU.mult)
    em.op("vector", TT, out=m2r, in0=rowA, in1=rowA, op=ALU.mult)
    em.op("vector", TS, out=rowB, in0=rowB, scalar1=1.0 / C, scalar2=None,
          op0=ALU.mult)
    vm1 = em.op("vector", TT, out=rowB, in0=rowB, in1=m2r, op=ALU.subtract,
                inc=("tDVE", 1))
    em.wait("scalar", "tDVE", vm1)
    em.op("scalar", "activation", out=rowB, in_=rowB, func=AF.Ln, bias=cvals[0:1, 0:1])
    rr = em.op("scalar", "activation", out=rowB, in_=rowB, func=AF.Exp,
               scale=cvals[0:1, 1:2], inc=("tACT", 1))
    em.wait("vector", "tACT", rr)
    arow = em.op("vector", TT, out=rowA, in0=rowA, in1=rowB, op=ALU.mult,
                 inc=("tDVE", 1))

    em.wait("tensor", "tDVE", arow)
    bbm, bbe, bam, bae = {}, {}, {}, {}
    for tb in range(4):
        bbm[tb] = em.op("tensor", "matmul", p[tb][:, :], onesr[:, :],
                        rowB[:, tb * 512:(tb + 1) * 512], start=True, stop=True,
                        inc=("tPE", 1))
    for tb in range(4):
        em.wait("scalar", "tPE", bbm[tb])
        bbe[tb] = em.op("scalar", "copy", out=bcB[:, tb * 512:(tb + 1) * 512],
                        in_=p[tb][:, :], inc=("tACT", 1))
    for tb in range(4):
        em.wait("tensor", "tACT", bbe[tb])
        bam[tb] = em.op("tensor", "matmul", p[tb][:, :], onesr[:, :],
                        rowA[:, tb * 512:(tb + 1) * 512], start=True, stop=True,
                        inc=("tPE", 1))
    for tb in range(4):
        em.wait("scalar", "tPE", bam[tb])
        bae[tb] = em.op("scalar", "copy", out=bcR[:, tb * 512:(tb + 1) * 512],
                        in_=p[tb][:, :], inc=("tACT", 1))
    bcast_done = bae[3]

    def ln_reduce_bcast(tA, tB, sq_scratch, rowA, rowB, guard_dve):
        em.wait("tensor", "tDVE", guard_dve)
        for src, row in ((tA, rowA), (tB, rowB)):
            for half in range(2):
                em.wait("tensor", "tACT", em.cnt["tACT"])
                for i in range(2):
                    tb = 2 * half + i
                    em.op("tensor", "matmul", p[4 + i][0:1, 0:512], onesc[:, :],
                          src[:, tb * 512:(tb + 1) * 512], start=True, stop=True,
                          inc=("tPE", 1))
                mm = em.cnt["tPE"]
                em.wait("scalar", "tPE", mm)
                for i in range(2):
                    tb = 2 * half + i
                    em.op("scalar", "copy", out=row[0:1, tb * 512:(tb + 1) * 512],
                          in_=p[4 + i][0:1, 0:512], inc=("tACT", 1))
        for src, row in ((tA, rowA), (tB, rowB)):
            for half in range(2):
                em.wait("tensor", "tACT", em.cnt["tACT"])
                for i in range(2):
                    tb = 2 * half + i
                    em.op("tensor", "matmul", p[4 + i][:, :], onesr[:, :],
                          row[0:1, tb * 512:(tb + 1) * 512], start=True, stop=True,
                          inc=("tPE", 1))
                mm = em.cnt["tPE"]
                em.wait("scalar", "tPE", mm)
                for i in range(2):
                    tb = 2 * half + i
                    em.op("scalar", "copy", out=src[:, tb * 512:(tb + 1) * 512],
                          in_=p[4 + i][:, :], inc=("tACT", 1))
        e2 = em.cnt["tACT"]
        em.wait("vector", "tACT", e2)
        em.op("vector", TS, out=tA, in0=tA, scalar1=1.0 / C, scalar2=None, op0=ALU.mult)
        em.op("vector", TS, out=tB, in0=tB, scalar1=1.0 / C, scalar2=None, op0=ALU.mult)
        em.op("vector", TT, out=sq_scratch, in0=tA, in1=tA, op=ALU.mult)
        vd = em.op("vector", TT, out=tB, in0=tB, in1=sq_scratch, op=ALU.subtract,
                   inc=("tDVE", 1))
        em.wait("scalar", "tDVE", vd)
        em.op("scalar", "activation", out=tB, in_=tB, func=AF.Ln, bias=cvals[:, 0:1])
        rr = em.op("scalar", "activation", out=tB, in_=tB, func=AF.Exp,
                   scale=cvals[:, 1:2], inc=("tACT", 1))
        em.wait("vector", "tACT", rr)
        em.op("vector", TT, out=tA, in0=tA, in1=tB, op=ALU.mult)
        return em.op("vector", TS, out=tA, in0=tA, scalar1=-1.0, scalar2=None,
                     op0=ALU.mult, inc=("tDVE", 1))

    ckpt = [0]

    def ok():
        ckpt[0] += 1
        return ckpt[0] < stop_after
    go = ok()

    xn_cast, st_cnt = {}, {}
    em.wait("vector", "tACT", bcast_done)
    for cc in range(NCH):
        em.op("vector", TT, out=xres[cc], in0=xres[cc], in1=bcB[:, :], op=ALU.mult)
        zr = em.op("vector", TT, out=xres[cc], in0=xres[cc], in1=bcR[:, :],
                   op=ALU.add, inc=("tDVE", 1))
        em.wait("scalar", "tDVE", zr)
        xn_cast[cc] = em.op("scalar", "copy", out=xn[:, cc * S:(cc + 1) * S],
                            in_=xres[cc], inc=("tACT", 1))
    xn_all = max(xn_cast.values())

    for cc in range(NCH):
        xc = xn[:, cc * S:(cc + 1) * S]
        em.wait("vector", "tACT", xn_cast[cc])
        for kind, fbase, scr in (("l", 0, l0_scr), ("h", 32, h0_scr)):
            tsl = (cc % 2) * 2 + (0 if kind == "l" else 1)
            slot = sb4[tsl]
            if cc >= 2:
                em.wait("vector", f"qT{tsl}", st_cnt[(kind, cc - 2)])
            def fk(k, fb=fbase, c=cc):
                return filt[:, fb + 4 * c + k: fb + 4 * c + k + 1]
            em.op("vector", TS, out=aof[:, 0:1024], in0=xc[:, 0:2048:2], scalar1=fk(1),
                  scalar2=None, op0=ALU.mult)
            em.op("vector", STT, out=aof[:, 0:1024], in0=xc[:, 1:2048:2],
                  scalar=fk(2), in1=aof[:, 0:1024], op0=ALU.mult, op1=ALU.add)
            em.op("vector", STT, out=aof[:, 1:1024], in0=xc[:, 1:2047:2],
                  scalar=fk(0), in1=aof[:, 1:1024], op0=ALU.mult, op1=ALU.add)
            em.op("vector", STT, out=slot[:, 1:1024], in0=xc[:, 2:2048:2],
                  scalar=fk(3), in1=aof[:, 0:1023], op0=ALU.mult, op1=ALU.add)
            em.op("vector", "tensor_copy", out=slot[:, 1024:1025],
                  in_=aof[:, 1023:1024], inc=("tDVE", 1))
            if cc < 2:
                em.op("vector", "memset", slot[:, 0:1], 0.0)
                em.op("vector", "memset", slot[:, 1025:1028], 0.0, inc=("tDVE", 1))
            dve_last = em.cnt["tDVE"]
            em.wait("sync", "tDVE", dve_last)
            st_cnt[(kind, cc)] = em.dma("sync", scr[cc], slot[:, 0:1026], f"qT{tsl}")

    c_l1_done = {}
    for cc in range(NCH):
        tsl = cc % 2
        slot = sb4[tsl]
        if cc >= 2:
            em.wait("sync", "tDVE", c_l1_done[cc - 2])
        em.wait("sync", f"qT{(cc % 2) * 2}", st_cnt[("l", cc)])
        em.wait("sync", f"qT{tsl}", em.cnt[f"qT{tsl}"])
        li = em.dma("sync", slot[:, 0:1026], l0_scr[cc], f"qT{tsl}")
        em.wait("vector", f"qT{tsl}", li)
        for dsl, fbase in ((low2v[:, cc * 512: cc * 512 + 512], 0),
                           (h1v[:, cc * 514 + 1: cc * 514 + 513], 32)):
            def fk(k, fb=fbase, c=cc):
                return filt[:, fb + 4 * c + k: fb + 4 * c + k + 1]
            a5 = aof[:, 0:512]
            em.op("vector", TS, out=a5, in0=slot[:, 0:1024:2], scalar1=fk(0),
                  scalar2=None, op0=ALU.mult)
            for k in (1, 2):
                em.op("vector", STT, out=a5, in0=slot[:, k:k + 1024:2], scalar=fk(k),
                      in1=a5, op0=ALU.mult, op1=ALU.add)
            em.op("vector", STT, out=dsl, in0=slot[:, 3:1026:2], scalar=fk(3),
                  in1=a5, op0=ALU.mult, op1=ALU.add, inc=("tDVE", 1))
        if cc == 0:
            em.op("vector", "memset",
                  h1v.rearrange("p (c n) -> p c n", c=8)[:, :, 0:1], 0.0)
            em.op("vector", "memset",
                  h1v.rearrange("p (c n) -> p c n", c=8)[:, :, 513:514], 0.0,
                  inc=("tDVE", 1))
        c_l1_done[cc] = em.cnt["tDVE"]

    pfree = {i: (0, 0) for i in range(4)}
    s23_ready, s4_ready = {}, {}
    out_st, asm_done = {}, {}
    pending_store = [None]
    ao_gate = [(0, 0)]
    ao_gate_emitted = [False]

    def stage1(q):
        qc0, qc1 = q * TQ, (q + 1) * TQ
        a_cnt = {}
        for oc in range(NOC):
            g = oc // 4
            gi = q * 8 + g
            if oc % 4 == 0:
                if gi not in s1_ready:
                    s1_ready[gi] = load_s1(g, q)
                if g + 1 < 8 and gi + 1 not in s1_ready:
                    s1_ready[gi + 1] = load_s1(g + 1, q)
            pu1, pu2 = p[2 * (oc % 2)], p[2 * (oc % 2) + 1]
            wait_w("tensor", s1_ready[gi])
            fa1, _ = pfree[2 * (oc % 2)]
            em.wait("tensor", "tACT", fa1)
            fa2, _ = pfree[2 * (oc % 2) + 1]
            em.wait("tensor", "tACT", fa2)
            slotW1, slotW2 = (2 * g) % 4, (2 * g + 1) % 4
            ocl = oc % 4
            for ic in range(NCH):
                if oc == 0 and q == 0:
                    em.wait("tensor", "tACT", xn_cast[ic])
                em.op("tensor", "matmul", pu1[:, :],
                      wslot[slotW1][:, ic * 512 + ocl * 128: ic * 512 + ocl * 128 + 128],
                      xn[:, ic * S + qc0: ic * S + qc1],
                      start=(ic == 0), stop=(ic == NCH - 1), inc=("tPE", 1))
            mm1 = em.cnt["tPE"]
            for ic in range(NCH):
                em.op("tensor", "matmul", pu2[:, :],
                      wslot[slotW2][:, ic * 512 + ocl * 128: ic * 512 + ocl * 128 + 128],
                      xn[:, ic * S + qc0: ic * S + qc1],
                      start=(ic == 0), stop=(ic == NCH - 1), inc=("tPE", 1))
            mm2 = em.cnt["tPE"]
            if ocl == 3:
                slot_free[slotW1] = mm1
                slot_free[slotW2] = mm2
            em.wait("scalar", "tPE", mm1)
            if oc >= 2:
                em.wait("scalar", "tPOOL", a_cnt[oc - 2])
            sl = (oc % 2) * TQ
            s1c = em.op("scalar", "activation", out=t1a[:, sl:sl + TQ], in_=pu1[:, :],
                        func=AF.Silu, scale=s1v[:, oc:oc + 1], inc=("tACT", 1))
            em.wait("scalar", "tPE", mm2)
            ev2 = em.op("scalar", "activation", out=t2a[:, sl:sl + TQ], in_=pu2[:, :],
                        func=AF.Identity, scale=s2v[:, oc:oc + 1], inc=("tACT", 1))
            em.wait("gpsimd", "tACT", ev2)
            a_cnt[oc] = em.op("gpsimd", TT, out=abuf.ap()[:, oc * TQ:(oc + 1) * TQ],
                              in0=t1a[:, sl:sl + TQ], in1=t2a[:, sl:sl + TQ],
                              op=ALU.mult, inc=("tPOOL", 1))
            pfree[2 * (oc % 2)] = (s1c, a_cnt[oc])
            pfree[2 * (oc % 2) + 1] = (ev2, a_cnt[oc])
        return em.cnt["tPOOL"]

    def stage23(q, stage_i, scale_v, inv, outv, ready_pool, ready_act, interleave=None):
        kbase = 1000 + q * 100 + stage_i * 50
        b_cnt = {}
        for jc in range(NOC):
            if interleave is not None and jc % 4 == 0:
                interleave(jc // 4)
            gg = kbase + jc
            if gg not in s23_ready:
                s23_ready[gg] = load_s23(stage_i, jc, q)
            slot = jc % 4
            pb = p[2 + jc % 2]
            wait_w("tensor", s23_ready[gg])
            if jc == 0:
                if ready_pool:
                    em.wait("tensor", "tPOOL", ready_pool)
                if ready_act:
                    em.wait("tensor", "tACT", ready_act)
            fa, _ = pfree[2 + jc % 2]
            em.wait("tensor", "tACT", fa)
            for oc in range(NOC):
                em.op("tensor", "matmul", pb[:, :],
                      wslot[slot][:, oc * 128: oc * 128 + 128],
                      inv(oc), start=(oc == 0), stop=(oc == NOC - 1), inc=("tPE", 1))
            mm = em.cnt["tPE"]
            slot_free[slot] = mm
            em.wait("scalar", "tPE", mm)
            b_cnt[jc] = em.op("scalar", "activation", out=outv(jc), in_=pb[:, :],
                              func=AF.Gelu, scale=scale_v[:, jc:jc + 1], inc=("tACT", 1))
            pfree[2 + jc % 2] = (b_cnt[jc], 0)
        return max(b_cnt.values())

    def stage4(q, c_all, with_ao, interleave=None):
        qc0, qc1 = q * TQ, (q + 1) * TQ
        for cc in range(NCH):
            if interleave is not None:
                interleave(cc)
            gi = 10000 + q * 8 + cc
            if gi not in s4_ready:
                s4_ready[gi] = load_s4(cc, q)
            if cc + 1 < NCH and gi + 1 not in s4_ready:
                s4_ready[gi + 1] = load_s4(cc + 1, q)
            ki = (q, cc)
            slot = cc % 2
            wsl = cc % 4
            prev = (q, cc - 2) if cc >= 2 else ((q - 1, cc + 6) if q > 0 else None)
            if prev in asm_done:
                em.wait("sync", "tPOOL", asm_done[prev])
            if with_ao and not ao_gate_emitted[0]:
                em.wait("sync", "qAO0", ao_gate[0][0])
                em.wait("sync", "qAO1", ao_gate[0][1])
                ao_gate_emitted[0] = True
            em.dma("sync", xq[:, slot * TQ: slot * TQ + TQ],
                   xT[cc * 128:(cc + 1) * 128, qc0:qc1], f"qX{slot}")
            if with_ao:
                em.dma("sync", aoq[:, slot * TQ: slot * TQ + TQ],
                       ao_scr[cc][:, qc0:qc1], f"qX{slot}")
            fl = em.cnt[f"qX{slot}"]
            if pending_store[0] is not None:
                pk, pslot, pcc, pq0, pq1 = pending_store[0]
                em.wait("sync", "tPOOL", asm_done[pk])
                out_st[pk] = em.dma("sync", yT[pcc * 128:(pcc + 1) * 128, pq0:pq1],
                                    outb[:, pslot * TQ: pslot * TQ + TQ], f"qO{pslot}")
                pending_store[0] = None
            py = p[cc % 4]
            wait_w("tensor", s4_ready[gi])
            if cc == 0:
                em.wait("tensor", "tACT", c_all)
            fa, fp = pfree[cc % 4]
            em.wait("tensor", "tACT", fa)
            em.wait("tensor", "tPOOL", fp)
            for lc in range(NOC):
                em.op("tensor", "matmul", py[:, :],
                      wslot[wsl][:, lc * 128:(lc + 1) * 128],
                      abuf.ap()[:, lc * TQ:(lc + 1) * TQ],
                      start=(lc == 0), stop=(lc == NOC - 1), inc=("tPE", 1))
            mm = em.cnt["tPE"]
            slot_free[wsl] = mm
            em.wait("scalar", "tPE", mm)
            if prev in out_st:
                em.wait("scalar", f"qO{slot}", out_st[prev])
            ev = em.op("scalar", "activation", out=outb[:, slot * TQ: slot * TQ + TQ],
                       in_=py[:, :], func=AF.Identity, scale=w3sv[:, cc:cc + 1],
                       inc=("tACT", 1))
            em.wait("gpsimd", "tACT", ev)
            em.wait("gpsimd", f"qX{slot}", fl)
            em.op("gpsimd", TT, out=outb[:, slot * TQ: slot * TQ + TQ],
                  in0=outb[:, slot * TQ: slot * TQ + TQ],
                  in1=xq[:, slot * TQ: slot * TQ + TQ], op=ALU.add,
                  inc=None if with_ao else ("tPOOL", 1))
            if with_ao:
                em.op("gpsimd", TT, out=outb[:, slot * TQ: slot * TQ + TQ],
                      in0=outb[:, slot * TQ: slot * TQ + TQ],
                      in1=aoq[:, slot * TQ: slot * TQ + TQ], op=ALU.add,
                      inc=("tPOOL", 1))
            asm_done[ki] = em.cnt["tPOOL"]
            pfree[cc % 4] = (ev, asm_done[ki])
            pending_store[0] = (ki, slot, cc, qc0, qc1)
        pk, pslot, pcc, pq0, pq1 = pending_store[0]
        em.wait("sync", "tPOOL", asm_done[pk])
        out_st[pk] = em.dma("sync", yT[pcc * 128:(pcc + 1) * 128, pq0:pq1],
                            outb[:, pslot * TQ: pslot * TQ + TQ], f"qO{pslot}")
        pending_store[0] = None

    a_all0 = stage1(0)
    b_all0 = stage23(0, 0, h1sv, lambda oc: abuf.ap()[:, oc * TQ:(oc + 1) * TQ],
                     lambda jc: bbuf.ap()[:, jc * TQ:(jc + 1) * TQ], a_all0, None)

    for cc in range(NCH):
        l2c = low2v[:, cc * 512: cc * 512 + 512]
        m0 = em.op("vector", TS, out=aof[:, 0:512], in0=l2c, scalar1=0.0, scalar2=None,
                   op0=ALU.min, inc=("tDVE", 1))
        em.wait("scalar", "tDVE", m0)
        e0 = em.op("scalar", "activation", out=aof[:, 512:1024], in_=aof[:, 0:512],
                   func=AF.Exp, inc=("tACT", 1))
        em.op("vector", TS, out=aof[:, 1024:1536], in0=l2c, scalar1=0.0, scalar2=None,
              op0=ALU.max)
        em.wait("vector", "tACT", e0)
        em.op("vector", TT, out=qfmv[:, cc * 512: cc * 512 + 512], in0=aof[:, 512:1024],
              in1=aof[:, 1024:1536], op=ALU.add, inc=("tDVE", 1))
    for i in range(4):
        em.op("vector", "memset", vtb[i][:, 64:65], 1.0)
        em.op("vector", "memset", vtb[i][:, 129:130], 1.0, inc=("tDVE", 1))

    kv_ev, tr_ev = {}, {}
    amid = {"ti": 0, "prev_kvmm": 0, "first": True, "att_tt": 0, "numev": 0}
    elu_all = em.cnt["tDVE"]

    def attn_kv_block(cc):
        if amid["first"]:
            em.wait("tensor", "tACT", identb_done)
            em.wait("tensor", "tDVE", elu_all)
            amid["first"] = False
        ti = amid["ti"]
        em.wait("vector", "tPE", amid["prev_kvmm"])
        for tn in range(4):
            for which, srcv in ((0, low2v), (1, qfmv)):
                if ti >= 2:
                    em.wait("tensor", "tDVE", tr_ev[ti - 2])
                pt = ptr[ti % 2]
                mm = em.op("tensor", "transpose", pt[:, :],
                           srcv[:, cc * 512 + tn * 128: cc * 512 + tn * 128 + 128],
                           identb[:, :], inc=("tPE", 1))
                em.wait("vector", "tPE", mm)
                if which == 0:
                    em.op("vector", "tensor_copy", out=vtb[tn][:, 0:64], in_=pt[:, 0:64])
                    tr_ev[ti] = em.op("vector", "tensor_copy", out=vtb[tn][:, 65:129],
                                      in_=pt[:, 64:128], inc=("tDVE", 1))
                else:
                    tr_ev[ti] = em.op("vector", "tensor_copy", out=qtb[tn][:, 0:128],
                                      in_=pt[:, :], inc=("tDVE", 1))
                ti += 1
        amid["ti"] = ti
        em.wait("tensor", "tDVE", tr_ev[ti - 1])
        for tn in range(4):
            em.op("tensor", "matmul", p[4][0:64, 0:65], qtb[tn][:, 0:64],
                  vtb[tn][:, 0:65], start=(tn == 0), stop=(tn == 3), inc=("tPE", 1))
        for tn in range(4):
            em.op("tensor", "matmul", p[5][0:64, 0:65], qtb[tn][:, 64:128],
                  vtb[tn][:, 65:130], start=(tn == 0), stop=(tn == 3), inc=("tPE", 1))
        amid["prev_kvmm"] = em.cnt["tPE"]
        em.wait("scalar", "tPE", amid["prev_kvmm"])
        kv_ev[2 * cc] = em.op("scalar", "copy", out=kvs[0:64, 65 * cc: 65 * cc + 65],
                              in_=p[4][0:64, 0:65], inc=("tACT", 1))
        kv_ev[2 * cc + 1] = em.op("scalar", "copy",
                                  out=kvs[64:128, 65 * cc: 65 * cc + 65],
                                  in_=p[5][0:64, 0:65], inc=("tACT", 1))
        em.wait("tensor", "tACT", kv_ev[2 * cc + 1])

    def attn_num_wave(wv):
        hs = [2 * wv, 2 * wv + 1]
        mms = {}
        for i, h in enumerate(hs):
            em.wait("tensor", "tACT", kv_ev[h])
            if wv > 0:
                em.wait("tensor", "tDVE", amid["att_tt"])
                em.wait("tensor", "tACT", amid["numev"])
            cc = h // 2
            rbase = 64 * (h % 2)
            mms[i] = em.op("tensor", "matmul", p[4 + i][0:65, 0:512],
                           kvs[rbase:rbase + 64, 65 * cc: 65 * cc + 65],
                           qfmv[rbase:rbase + 64, cc * 512: cc * 512 + 512],
                           start=True, stop=True, inc=("tPE", 1))
        for i, h in enumerate(hs):
            cc = h // 2
            rbase = 64 * (h % 2)
            em.wait("scalar", "tPE", mms[i])
            amid["numev"] = em.op("scalar", "copy",
                                  out=attv[rbase:rbase + 64, cc * 514 + 1: cc * 514 + 513],
                                  in_=p[4 + i][0:64, 0:512], inc=("tACT", 1))
            em.wait("vector", "tPE", mms[i])
            em.op("vector", TS, out=aof[0:1, i * 512:(i + 1) * 512],
                  in0=p[4 + i][64:65, 0:512], scalar1=EPS_ATTN, scalar2=None,
                  op0=ALU.add, inc=("tDVE", 1))
        dr = em.op("vector", "reciprocal", out=aof[0:1, 1024:2048],
                   in_=aof[0:1, 0:1024], inc=("tDVE", 1))
        em.wait("tensor", "tDVE", dr)
        em.wait("tensor", "tACT", amid["numev"])
        bcs = {}
        for i in range(2):
            bcs[i] = em.op("tensor", "matmul", p[4 + i][0:64, 0:512], onesr[0:1, 0:64],
                           aof[0:1, 1024 + i * 512: 1024 + (i + 1) * 512],
                           start=True, stop=True, inc=("tPE", 1))
        for i, h in enumerate(hs):
            cc = h // 2
            rbase = 64 * (h % 2)
            em.wait("vector", "tPE", bcs[i])
            amid["att_tt"] = em.op(
                "vector", TT,
                out=attv[rbase:rbase + 64, cc * 514 + 1: cc * 514 + 513],
                in0=attv[rbase:rbase + 64, cc * 514 + 1: cc * 514 + 513],
                in1=p[4 + i][0:64, 0:512], op=ALU.mult, inc=("tDVE", 1))

    for cc in range(NCH):
        attn_kv_block(cc)
    for wv in range(NHEAD // 2):
        attn_num_wave(wv)
    c_all0 = stage23(0, 1, h2sv, lambda oc: bbuf.ap()[:, oc * TQ:(oc + 1) * TQ],
                     lambda jc: abuf.ap()[:, jc * TQ:(jc + 1) * TQ], None, b_all0)
    stage4(0, c_all0, with_ao=False)
    em.op("vector", "memset", attv.rearrange("p (c n) -> p c n", c=8)[:, :, 0:1], 0.0)
    em.op("vector", "memset", attv.rearrange("p (c n) -> p c n", c=8)[:, :, 513:514],
          0.0, inc=("tDVE", 1))

    def recon(cc, sa_ap, sb_ap, dst_ap, dst_off, nh):
        def sa(d):
            return sa_ap[:, d: d + nh]
        def sbv(d):
            return sb_ap[:, d: d + nh]
        def hcm(j, c=cc):
            return fcmb[:, 5 * c + j: 5 * c + j + 1]
        def gcm(j, c=cc):
            return fcmb[:, 40 + 5 * c + j: 40 + 5 * c + j + 1]
        a5 = aof[:, 0:nh]
        em.op("vector", TS, out=a5, in0=sa(0), scalar1=hcm(0), scalar2=None, op0=ALU.mult)
        em.op("vector", STT, out=a5, in0=sa(1), scalar=hcm(1), in1=a5, op0=ALU.mult, op1=ALU.add)
        em.op("vector", STT, out=a5, in0=sa(2), scalar=hcm(2), in1=a5, op0=ALU.mult, op1=ALU.add)
        em.op("vector", STT, out=a5, in0=sbv(0), scalar=gcm(0), in1=a5, op0=ALU.mult, op1=ALU.add)
        em.op("vector", STT, out=a5, in0=sbv(1), scalar=gcm(1), in1=a5, op0=ALU.mult, op1=ALU.add)
        em.op("vector", STT, out=dst_ap[:, dst_off: dst_off + 2 * nh: 2], in0=sbv(2),
              scalar=gcm(2), in1=a5, op0=ALU.mult, op1=ALU.add, inc=("tDVE", 1))
        em.op("vector", TS, out=a5, in0=sa(1), scalar1=hcm(3), scalar2=None, op0=ALU.mult)
        em.op("vector", STT, out=a5, in0=sa(2), scalar=hcm(4), in1=a5, op0=ALU.mult, op1=ALU.add)
        em.op("vector", STT, out=a5, in0=sbv(1), scalar=gcm(3), in1=a5, op0=ALU.mult, op1=ALU.add)
        return em.op("vector", STT, out=dst_ap[:, dst_off + 1: dst_off + 2 * nh: 2],
                     in0=sbv(2), scalar=gcm(4), in1=a5, op0=ALU.mult, op1=ALU.add,
                     inc=("tDVE", 1))

    r1_st = {}
    for cc in range(NCH):
        tsl = cc % 2
        slot = sb4[tsl]
        if cc >= 2:
            em.wait("vector", f"qT{tsl}", r1_st[cc - 2])
        d = recon(cc, attv[:, cc * 514: cc * 514 + 514],
                  h1v[:, cc * 514: cc * 514 + 514], slot, 1, 512)
        if cc < 2:
            em.op("vector", "memset", slot[:, 0:1], 0.0)
            em.op("vector", "memset", slot[:, 1025:1028], 0.0, inc=("tDVE", 1))
            d = em.cnt["tDVE"]
        em.wait("sync", "tDVE", d)
        r1_st[cc] = em.dma("sync", r1_scr[cc], slot[:, 0:1026], f"qT{tsl}")

    low_st, f_done = {}, {}
    for cc in range(NCH):
        rtsl = cc % 2
        htsl = 2 + cc % 2
        rsl = sb4[rtsl]
        hsl = sb4[htsl]
        osl = lowsl[cc % 2]
        if cc >= 2:
            em.wait("sync", "tDVE", f_done[cc - 2])
        em.wait("sync", f"qT{rtsl}", r1_st[cc])
        em.wait("sync", f"qT{htsl}", em.cnt[f"qT{htsl}"])
        rl = em.dma("sync", rsl[:, 0:1026], r1_scr[cc], f"qT{rtsl}")
        hl = em.dma("sync", hsl[:, 0:1026], h0_scr[cc], f"qT{htsl}")
        if cc >= 2:
            em.wait("vector", f"qLs{cc % 2}", low_st[cc - 2])
        em.wait("vector", f"qT{rtsl}", rl)
        em.wait("vector", f"qT{htsl}", hl)
        f_done[cc] = recon(cc, rsl[:, 0:1026], hsl[:, 0:1026], osl, 0, 1024)
        em.op("vector", TT, out=azB[:, 0:2048], in0=xn[:, cc * S:(cc + 1) * S],
              in1=osl[:, :], op=ALU.add)
        em.op("vector", TT, out=aof[:, 0:2048], in0=azB[:, 0:2048], in1=azB[:, 0:2048],
              op=ALU.mult)
        if cc == 0:
            em.op("vector", "tensor_copy", out=bcR[:, :], in_=azB[:, 0:2048])
            em.op("vector", "tensor_copy", out=bcB[:, :], in_=aof[:, 0:2048],
                  inc=("tDVE", 1))
        else:
            em.op("vector", TT, out=bcR[:, :], in0=bcR[:, :], in1=azB[:, 0:2048],
                  op=ALU.add)
            em.op("vector", TT, out=bcB[:, :], in0=bcB[:, :], in1=aof[:, 0:2048],
                  op=ALU.add, inc=("tDVE", 1))
        f_done[cc] = em.cnt["tDVE"]
        em.wait("sync", "tDVE", f_done[cc])
        low_st[cc] = em.dma("sync", low_scr[cc], osl[:, :], f"qLs{cc % 2}")

    ln2_acc = em.cnt["tDVE"]

    a_all1 = stage1(1)
    b_all1 = stage23(1, 0, h1sv, lambda oc: abuf.ap()[:, oc * TQ:(oc + 1) * TQ],
                     lambda jc: bbuf.ap()[:, jc * TQ:(jc + 1) * TQ], a_all1, None)

    c_all1 = stage23(1, 1, h2sv, lambda oc: bbuf.ap()[:, oc * TQ:(oc + 1) * TQ],
                     lambda jc: abuf.ap()[:, jc * TQ:(jc + 1) * TQ], None, b_all1)
    stage4(1, c_all1, with_ao=False)
    ln2_done = ln_reduce_bcast(bcR[:, :], bcB[:, :], aof[:, 0:2048],
                               azB[0:1, 0:2048], aof[0:1, 0:2048], ln2_acc)

    ao_st, ao_dve = {}, {}
    for cc in range(NCH):
        isl = lowsl[cc % 2]
        osl = aosl[cc % 2]
        if cc < 2:
            em.wait("sync", "tDVE", ln2_acc)
        else:
            em.wait("sync", "tDVE", ao_dve[cc - 2])
        ll = em.dma("sync", isl[:, :], low_scr[cc], f"qLs{cc % 2}")
        em.wait("vector", f"qLs{cc % 2}", ll)
        if cc >= 2:
            em.wait("vector", f"qAO{cc % 2}", ao_st[cc - 2])
        em.op("vector", TT, out=azB[:, 0:2048], in0=xn[:, cc * S:(cc + 1) * S],
              in1=isl[:, :], op=ALU.add)
        em.op("vector", TT, out=azB[:, 0:2048], in0=azB[:, 0:2048], in1=bcB[:, :],
              op=ALU.mult)
        em.op("vector", TT, out=azB[:, 0:2048], in0=azB[:, 0:2048], in1=bcR[:, :],
              op=ALU.add)
        ao_dve[cc] = em.op("vector", TS, out=osl, in0=azB[:, 0:2048],
                           scalar1=gamv[:, cc:cc + 1], scalar2=betv[:, cc:cc + 1],
                           op0=ALU.mult, op1=ALU.add, inc=("tDVE", 1))
        em.wait("sync", "tDVE", ao_dve[cc])
        ao_st[cc] = em.dma("sync", ao_scr[cc], osl, f"qAO{cc % 2}")
    ao_gate[0] = (em.cnt["qAO0"], em.cnt["qAO1"])

    fx_dve = {}
    fi = 0
    for fq in (0, 1):
        for cc in range(NCH):
            if fi >= 2:
                em.wait("sync", "tDVE", fx_dve[fi - 2])
            em.wait("sync", f"qO{cc % 2}", out_st[(fq, cc)])
            if fi == 0:
                em.wait("sync", "qAO0", ao_gate[0][0])
                em.wait("sync", "qAO1", ao_gate[0][1])
            em.dma("sync", fyb[fi % 2][:, 0:512],
                   yT[cc * 128:(cc + 1) * 128, fq * TQ:(fq + 1) * TQ], f"qF{fi % 2}")
            fl = em.dma("sync", fab[fi % 2], ao_scr[cc][:, fq * TQ:(fq + 1) * TQ],
                        f"qF{fi % 2}")
            em.wait("vector", f"qF{fi % 2}", fl)
            fx_dve[fi] = em.op("vector", TT, out=fyb[fi % 2][:, 0:512],
                               in0=fyb[fi % 2][:, 0:512], in1=fab[fi % 2], op=ALU.add,
                               inc=("tDVE", 1))
            em.wait("sync", "tDVE", fx_dve[fi])
            em.dma("sync", yT[cc * 128:(cc + 1) * 128, fq * TQ:(fq + 1) * TQ],
                   fyb[fi % 2][:, 0:512], f"qF{fi % 2}")
            fi += 1
    for q in range(2, nq):
        a_all = stage1(q)
        b_all = stage23(q, 0, h1sv, lambda oc: abuf.ap()[:, oc * TQ:(oc + 1) * TQ],
                        lambda jc: bbuf.ap()[:, jc * TQ:(jc + 1) * TQ], a_all, None)
        c_all = stage23(q, 1, h2sv, lambda oc: bbuf.ap()[:, oc * TQ:(oc + 1) * TQ],
                        lambda jc: abuf.ap()[:, jc * TQ:(jc + 1) * TQ], None, b_all)
        stage4(q, c_all, with_ao=True)

    em.wait("sync", "qO0", em.cnt["qO0"])
    em.wait("sync", "qO1", em.cnt["qO1"])
    em.wait("sync", "qF0", em.cnt["qF0"])
    em.wait("sync", "qF1", em.cnt["qF1"])
    em.resolve_dwaits()

    if stop_after < 99:
        total = sum(len(em.q[e]) for e in ENGINES)
        budget = int(total * stop_after / 100.0)
        pos = {e: 0 for e in ENGINES}
        cnt = {k: 0 for k in em.cnt}
        done = 0
        progress = True
        while progress and done < budget:
            progress = False
            for e in ENGINES:
                while pos[e] < len(em.q[e]) and done < budget:
                    it = em.q[e][pos[e]]
                    if it[0] == "wait":
                        if cnt[it[1]] < it[2]:
                            break
                    else:
                        if it[4] is not None:
                            cnt[it[4][0]] += it[4][1]
                    pos[e] += 1
                    done += 1
                    progress = True
        for e in ENGINES:
            em.q[e] = em.q[e][:pos[e]]

    with contextlib.ExitStack() as stack:
        sems = {name: stack.enter_context(nc.semaphore(name)) for name in em.cnt}
        blk = stack.enter_context(nc.Block())

        def mk(engname):
            def f(eng):
                em.replay(eng, engname, sems)
            return f
        blk.sync(mk("sync"))
        blk.tensor(mk("tensor"))
        blk.vector(mk("vector"))
        blk.scalar(mk("scalar"))
        blk.gpsimd(mk("gpsimd"))
    for t in reversed(ctxs):
        t.__exit__(None, None, None)
    return nc, em



def _prep_shared(inputs):
    d = {}
    w1t = np.ascontiguousarray(inputs["L0_ffn_w1_w"].T)
    w2t = np.ascontiguousarray(inputs["L0_ffn_w2_w"].T)
    h1t = np.ascontiguousarray(inputs["L0_ffn_h1_w"].T)
    h2t = np.ascontiguousarray(inputs["L0_ffn_h2_w"].T)
    w3t = np.ascontiguousarray(inputs["L0_ffn_w3_w"].T)
    d["w1g"] = np.ascontiguousarray(
        w1t.reshape(8, 128, 8, 512).transpose(2, 1, 0, 3).reshape(8, 128, 4096))
    d["w2g"] = np.ascontiguousarray(
        w2t.reshape(8, 128, 8, 512).transpose(2, 1, 0, 3).reshape(8, 128, 4096))
    d["h1g"] = np.ascontiguousarray(
        h1t.reshape(32, 128, 32, 128).transpose(2, 1, 0, 3).reshape(32, 128, 4096))
    d["h2g"] = np.ascontiguousarray(
        h2t.reshape(32, 128, 32, 128).transpose(2, 1, 0, 3).reshape(32, 128, 4096))
    d["w3g"] = np.ascontiguousarray(
        w3t.reshape(32, 128, 8, 128).transpose(2, 1, 0, 3).reshape(8, 128, 4096))
    d["s1"] = np.ascontiguousarray(inputs["L0_ffn_w1_s"].reshape(32, 128).T)
    d["s2"] = np.ascontiguousarray(inputs["L0_ffn_w2_s"].reshape(32, 128).T)
    d["h1s"] = np.ascontiguousarray(inputs["L0_ffn_h1_s"].reshape(32, 128).T)
    d["h2s"] = np.ascontiguousarray(inputs["L0_ffn_h2_s"].reshape(32, 128).T)
    d["w3s"] = np.ascontiguousarray(inputs["L0_ffn_w3_s"].reshape(8, 128).T)
    d["gam"] = np.ascontiguousarray(inputs["L0_attn_norm_gamma"].reshape(8, 128).T)
    d["bet"] = np.ascontiguousarray(inputs["L0_attn_norm_beta"].reshape(8, 128).T)
    for nm, key in (("hfil", "L0_attn_h_filter"), ("gfil", "L0_attn_g_filter"),
                    ("hsyn", "L0_attn_h_synth"), ("gsyn", "L0_attn_g_synth")):
        f = inputs[key]
        d[nm] = np.ascontiguousarray(f.T.reshape(8, 128, 4).transpose(1, 0, 2))
    d["ident"] = np.eye(128, dtype=np.float32)
    return d


def kernel(**inputs):
    x = np.asarray(inputs["L0_attn_x"], dtype=np.float32)
    shared = _prep_shared({k: np.asarray(v, dtype=np.float32)
                           for k, v in inputs.items() if k != "L0_attn_x"})
    nc, _ = build_program()
    in_maps = []
    for b in range(8):
        m = dict(shared)
        m["xT"] = np.ascontiguousarray(x[b].T)
        in_maps.append(m)
    res = run_bass_kernel_spmd(nc, in_maps, core_ids=list(range(8)))
    out = np.stack([np.asarray(res.results[b]["yT"]).T for b in range(8)], axis=0)
    return np.ascontiguousarray(out.astype(np.float32))


# version 17
# speedup vs baseline: 1.0565x; 1.0565x over previous
import numpy as np
import contextlib
import concourse.bass as bass
import concourse.mybir as mybir
from concourse.bass_utils import run_bass_kernel_spmd

F32 = mybir.dt.float32
BF16 = mybir.dt.bfloat16
AF = mybir.ActivationFunctionType
ALU = mybir.AluOpType

C = 1024
S = 2048
HID = 4096
NCH = C // 128
NOC = HID // 128
NQ = 4
TQ = S // NQ
EPS_LN = 1e-5
EPS_ATTN = 1e-6
NHEAD = 16

ENGINES = ("sync", "tensor", "vector", "scalar", "gpsimd")
TT, TS, STT = "tensor_tensor", "tensor_scalar", "scalar_tensor_tensor"


class Emitter:
    def __init__(self):
        self.q = {e: [] for e in ENGINES}
        self.cnt = {}

    def new_sem(self, name):
        self.cnt[name] = 0

    def op(self, eng, meth, *args, inc=None, **kw):
        if inc is not None:
            self.cnt[inc[0]] += inc[1]
        self.q[eng].append(("op", meth, args, kw, inc))
        return self.cnt[inc[0]] if inc else None

    def dma(self, eng, out, in_, sem):
        return self.op(eng, "dma_start", inc=(sem, 16), out=out, in_=in_)

    def wait(self, eng, sem, val):
        assert val <= self.cnt[sem], f"wait {sem}>={val} > emitted {self.cnt[sem]}"
        if val > 0:
            self.q[eng].append(("wait", sem, val))

    def wait_dma(self, eng, sem, val):
        assert val <= self.cnt[sem], f"wait {sem}>={val} > emitted {self.cnt[sem]}"
        if val > 0:
            self.q[eng].append(("dwait", sem, val))

    def resolve_dwaits(self):
        for eng in ENGINES:
            for i, item in enumerate(self.q[eng]):
                if item[0] == "dwait":
                    _, sem, val = item
                    self.q[eng][i] = ("wait", sem, min(val + 16, self.cnt[sem]))

    def replay(self, eng_obj, eng_name, sems):
        for item in self.q[eng_name]:
            if item[0] == "wait":
                _, sem, val = item
                eng_obj.wait_ge(sems[sem], val)
            else:
                _, meth, args, kw, inc = item
                inst = getattr(eng_obj, meth)(*args, **kw)
                if inc is not None:
                    inst.then_inc(sems[inc[0]], inc[1])


def build_program(nq=NQ, stop_after=99):
    nc = bass.Bass("TRN2", target_bir_lowering=False, debug=False)

    xT = nc.dram_tensor("xT", [C, S], F32, kind="ExternalInput").ap()
    w1g = nc.dram_tensor("w1g", [8, 128, 4096], F32, kind="ExternalInput").ap()
    w2g = nc.dram_tensor("w2g", [8, 128, 4096], F32, kind="ExternalInput").ap()
    h1g = nc.dram_tensor("h1g", [32, 128, 4096], F32, kind="ExternalInput").ap()
    h2g = nc.dram_tensor("h2g", [32, 128, 4096], F32, kind="ExternalInput").ap()
    w3g = nc.dram_tensor("w3g", [8, 128, 4096], F32, kind="ExternalInput").ap()
    s1i = nc.dram_tensor("s1", [128, 32], F32, kind="ExternalInput").ap()
    s2i = nc.dram_tensor("s2", [128, 32], F32, kind="ExternalInput").ap()
    h1si = nc.dram_tensor("h1s", [128, 32], F32, kind="ExternalInput").ap()
    h2si = nc.dram_tensor("h2s", [128, 32], F32, kind="ExternalInput").ap()
    w3si = nc.dram_tensor("w3s", [128, 8], F32, kind="ExternalInput").ap()
    gami = nc.dram_tensor("gam", [128, 8], F32, kind="ExternalInput").ap()
    beti = nc.dram_tensor("bet", [128, 8], F32, kind="ExternalInput").ap()
    hfili = nc.dram_tensor("hfil", [128, 8, 4], F32, kind="ExternalInput").ap()
    gfili = nc.dram_tensor("gfil", [128, 8, 4], F32, kind="ExternalInput").ap()
    hsyni = nc.dram_tensor("hsyn", [128, 8, 4], F32, kind="ExternalInput").ap()
    gsyni = nc.dram_tensor("gsyn", [128, 8, 4], F32, kind="ExternalInput").ap()
    identi = nc.dram_tensor("ident", [128, 128], F32, kind="ExternalInput").ap()
    yT = nc.dram_tensor("yT", [C, S], F32, kind="ExternalOutput").ap()
    ao_scr = nc.dram_tensor("ao_scr", [NCH, 128, S], BF16, kind="Internal").ap()
    l0_scr = nc.dram_tensor("l0_scr", [NCH, 128, 1026], BF16, kind="Internal").ap()
    h0_scr = nc.dram_tensor("h0_scr", [NCH, 128, 1026], BF16, kind="Internal").ap()
    r1_scr = nc.dram_tensor("r1_scr", [NCH, 128, 1026], BF16, kind="Internal").ap()
    low_scr = nc.dram_tensor("low_scr", [NCH, 128, S], BF16, kind="Internal").ap()
    ws_s1 = nc.dram_tensor("ws_s1", [8, 2, 128, 4096], BF16, kind="Internal").ap()
    ws_s23 = nc.dram_tensor("ws_s23", [2, 32, 128, 4096], BF16, kind="Internal").ap()
    ws_s4 = nc.dram_tensor("ws_s4", [8, 128, 4096], BF16, kind="Internal").ap()

    em = Emitter()
    semnames = (["qC", "qA0", "qA1", "qA2", "qA3", "qA4", "qA5", "qA6", "qA7",
                 "qX0", "qX1", "qO0", "qO1",
                 "qF0", "qF1", "qLs0", "qLs1", "qAO0", "qAO1",
                 "tPE", "tACT", "tDVE", "tPOOL"]
                + [f"qW{i}" for i in range(4)] + [f"qS{i}" for i in range(4)]
                + [f"qT{i}" for i in range(4)])
    for sname in semnames:
        em.new_sem(sname)

    ctxs = []

    def sbuf(name, shape, dt):
        t = nc.sbuf_tensor(name, shape, dt)
        ctxs.append(t)
        return t.__enter__()

    def ps_(name, shape, dt=F32):
        t = nc.psum_tensor(name, shape, dt)
        ctxs.append(t)
        return t.__enter__()

    xn = sbuf("xn", [128, NCH * S], BF16)
    abuf = sbuf("abuf", [128, NOC * TQ], BF16)
    bbuf = sbuf("bbuf", [128, NOC * TQ], BF16)
    wb = sbuf("wb", [128, 4 * 4096], BF16)
    bcR = sbuf("bcR", [128, S], F32)
    bcB = sbuf("bcB", [128, S], F32)
    aren = sbuf("aren", [128, 16448], BF16)
    aob = sbuf("aob", [128, 4096], BF16)
    t1a = sbuf("t1a", [128, 2 * TQ], F32)
    t2a = sbuf("t2a", [128, 2 * TQ], F32)
    xq = sbuf("xq", [128, 2 * TQ], F32)
    aoq = sbuf("aoq", [128, 2 * TQ], BF16)
    outb = sbuf("outb", [128, 2 * TQ], F32)
    scl = sbuf("scl", [128, 152], F32)
    filt = sbuf("filt", [128, 128], F32)
    fcmb = sbuf("fcmb", [128, 80], F32)
    ident = sbuf("identsb", [128, 128], F32)
    identb = sbuf("identbb", [128, 128], BF16)
    onesc = sbuf("onesc", [128, 1], F32)
    cvals = sbuf("cvals", [128, 2], F32)
    dtail = sbuf("dtail", [128, 8], F32)
    dtailb = sbuf("dtailb", [128, 8], BF16)
    onesr = sbuf("onesr", [1, 128], F32)
    kvs = sbuf("kvs", [128, NCH * 65], BF16)

    ar = aren.ap()
    azA = ar[:, 0:4112].bitcast(F32)
    azB = ar[:, 4112:8224].bitcast(F32)
    R3 = 8224
    R4 = 12336
    low2v = ar[:, 0:4096]
    h1v = ar[:, 4112:8224]
    qfmv = ar[:, R3:R3 + 4096]
    vtb = [aob.ap()[:, i * 258: i * 258 + 130] for i in range(4)]
    qtb = [aob.ap()[:, i * 258 + 130: i * 258 + 258] for i in range(4)]
    attv = ar[:, R4:R4 + 4112]
    sb4 = [ar[:, R3 + i * 1028: R3 + i * 1028 + 1028] for i in range(4)]
    lowsl = [ar[:, 0:2048], ar[:, R4:R4 + 2048]]
    fyb = [ar[:, R3 + i * 1024: R3 + (i + 1) * 1024].bitcast(F32) for i in range(2)]
    fab = [ar[:, R3 + 2048 + i * 512: R3 + 2048 + (i + 1) * 512] for i in range(2)]
    aosl = [ar[:, R3 + i * 2048: R3 + (i + 1) * 2048] for i in range(2)]
    aof = aob.ap()[:, 0:4096].bitcast(F32)

    s1v, s2v = scl[:, 0:32], scl[:, 32:64]
    h1sv, h2sv = scl[:, 64:96], scl[:, 96:128]
    w3sv, gamv, betv = scl[:, 128:136], scl[:, 136:144], scl[:, 144:152]

    p = [ps_(f"p{i}", [128, 512], F32) for i in range(6)]
    ptr = [ps_(f"ptr{i}", [128, 512], BF16) for i in range(2)]

    wslot = [wb[:, i * 4096:(i + 1) * 4096] for i in range(4)]

    slot_free = {i: 0 for i in range(4)}
    slot_store = {i: 0 for i in range(4)}

    def _store_slot(slot, dst, load_cnt):
        em.wait("scalar", f"qW{slot}", load_cnt)
        em.dma("scalar", dst, wslot[slot][:, :], f"qS{slot}")
        slot_store[slot] = em.cnt[f"qS{slot}"]
        return slot_store[slot]

    def _wload(slot, src, q, dst_scr):
        em.wait("gpsimd", "tPE", slot_free[slot])
        em.wait("gpsimd", f"qS{slot}", slot_store[slot])
        if q == 0:
            v = em.dma("gpsimd", wslot[slot][:, :], src, f"qW{slot}")
            _store_slot(slot, dst_scr, v)
        else:
            v = em.dma("gpsimd", wslot[slot][:, :], dst_scr, f"qW{slot}")
        return v

    def _wload_direct(slot, src):
        em.wait("gpsimd", "tPE", slot_free[slot])
        em.wait("gpsimd", f"qS{slot}", slot_store[slot])
        return em.dma("gpsimd", wslot[slot][:, :], src, f"qW{slot}")

    def load_s1(g, q):
        ready = []
        for wi in (0, 1):
            slot = (2 * g + wi) % 4
            ready.append((slot, _wload_direct(slot, (w1g, w2g)[wi][g])))
        return ready

    def load_s23(stage_i, g, q):
        slot = g % 4
        wq = 0
        if q < wq:
            return [(slot, _wload_direct(slot, (h1g, h2g)[stage_i][g]))]
        return [(slot, _wload(slot, (h1g, h2g)[stage_i][g], q - wq, ws_s23[stage_i, g]))]

    def load_s4(cc, q):
        slot = cc % 4
        return [(slot, _wload_direct(slot, w3g[cc]))]

    def wait_w(eng, ready):
        for slot, v in ready:
            em.wait(eng, f"qW{slot}", v)

    em.dma("sync", ident[:, :], identi, "qC")
    em.dma("sync", scl[:, 0:32], s1i, "qC")
    em.dma("sync", scl[:, 32:64], s2i, "qC")
    em.dma("sync", scl[:, 64:96], h1si, "qC")
    em.dma("sync", scl[:, 96:128], h2si, "qC")
    em.dma("sync", scl[:, 128:136], w3si, "qC")
    em.dma("sync", scl[:, 136:144], gami, "qC")
    em.dma("sync", scl[:, 144:152], beti, "qC")
    em.dma("sync", filt[:, 0:32].rearrange("p (c k) -> p c k", c=8), hfili, "qC")
    em.dma("sync", filt[:, 32:64].rearrange("p (c k) -> p c k", c=8), gfili, "qC")
    em.dma("sync", filt[:, 64:96].rearrange("p (c k) -> p c k", c=8), hsyni, "qC")
    n_const = em.dma("sync", filt[:, 96:128].rearrange("p (c k) -> p c k", c=8),
                     gsyni, "qC")

    em.op("vector", "memset", onesc[:, :], 1.0)
    em.op("vector", "memset", cvals[:, 0:1], EPS_LN)
    em.op("vector", "memset", cvals[:, 1:2], -0.5)
    em.op("vector", "memset", onesr[:, :], 1.0, inc=("tDVE", 1))

    s1_ready = {0: load_s1(0, 0), 1: load_s1(1, 0)}

    abf = abuf.ap()[:, 0:16384].bitcast(F32)
    bbf = bbuf.ap()[:, 0:16384].bitcast(F32)
    xres = ([abf[:, i * 2048:(i + 1) * 2048] for i in range(4)]
            + [bbf[:, i * 2048:(i + 1) * 2048] for i in range(4)])
    sqscr = [azA[:, 0:2048], azB[:, 0:2048]]
    rowA = aof[0:1, 0:2048]
    rowB = azB[0:1, 0:2048]
    m2r = azA[0:1, 0:2048]
    xld = {}
    for cc in range(0, NCH, 2):
        xld[cc] = em.dma("sync", xres[cc], xT[cc * 128:(cc + 1) * 128, :], f"qA{cc}")
    for cc in range(1, NCH, 2):
        xld[cc] = em.dma("scalar", xres[cc], xT[cc * 128:(cc + 1) * 128, :], f"qA{cc}")

    sq_cnt, add_cnt = {}, {}
    for cc in range(NCH):
        if cc >= 2:
            em.wait("scalar", "tDVE", add_cnt[cc - 2])
        em.wait("scalar", f"qA{cc}", xld[cc])
        sq_cnt[cc] = em.op("scalar", "activation", out=sqscr[cc % 2], in_=xres[cc],
                           func=AF.Square, inc=("tACT", 1))
        em.wait("vector", f"qA{cc}", xld[cc])
        if cc == 0:
            em.op("vector", "tensor_copy", out=bcR[:, :], in_=xres[cc])
        else:
            em.op("vector", TT, out=bcR[:, :], in0=bcR[:, :], in1=xres[cc], op=ALU.add)
        em.wait("vector", "tACT", sq_cnt[cc])
        if cc == 0:
            add_cnt[cc] = em.op("vector", "tensor_copy", out=bcB[:, :], in_=sqscr[0],
                                inc=("tDVE", 1))
        else:
            add_cnt[cc] = em.op("vector", TT, out=bcB[:, :], in0=bcB[:, :],
                                in1=sqscr[cc % 2], op=ALU.add, inc=("tDVE", 1))

    em.wait("scalar", "qC", n_const)
    em.op("scalar", "copy", out=identb[:, :], in_=ident[:, :], inc=("tACT", 1))
    identb_done = em.cnt["tACT"]

    em.wait("vector", "qC", n_const)
    for fi, base in ((0, 64), (1, 96)):
        for cc in range(8):
            fsl = filt[:, base + 4 * cc: base + 4 * cc + 4]
            dst = fcmb[:, fi * 40 + 5 * cc: fi * 40 + 5 * cc + 5]
            em.op("vector", "tensor_copy", out=dst[:, 0:1], in_=fsl[:, 0:1])
            em.op("vector", TT, out=dst[:, 1:2], in0=fsl[:, 1:2], in1=fsl[:, 2:3], op=ALU.add)
            em.op("vector", "tensor_copy", out=dst[:, 2:3], in_=fsl[:, 3:4])
            em.op("vector", TT, out=dst[:, 3:4], in0=fsl[:, 0:1], in1=fsl[:, 1:2], op=ALU.add)
            em.op("vector", TT, out=dst[:, 4:5], in0=fsl[:, 2:3], in1=fsl[:, 3:4],
                  op=ALU.add, inc=("tDVE", 1))

    em.wait("tensor", "tDVE", add_cnt[NCH - 1])
    amm, aev, bmm, bev = {}, {}, {}, {}
    for tb in range(4):
        amm[tb] = em.op("tensor", "matmul", p[tb][0:1, 0:512], onesc[:, :],
                        bcR[:, tb * 512:(tb + 1) * 512], start=True, stop=True,
                        inc=("tPE", 1))
    for tb in range(2):
        bmm[tb] = em.op("tensor", "matmul", p[4 + tb][0:1, 0:512], onesc[:, :],
                        bcB[:, tb * 512:(tb + 1) * 512], start=True, stop=True,
                        inc=("tPE", 1))
    for tb in range(4):
        em.wait("scalar", "tPE", amm[tb])
        aev[tb] = em.op("scalar", "copy", out=rowA[:, tb * 512:(tb + 1) * 512],
                        in_=p[tb][0:1, 0:512], inc=("tACT", 1))
    em.wait("tensor", "tACT", aev[1])
    for tb in range(2, 4):
        bmm[tb] = em.op("tensor", "matmul", p[tb - 2][0:1, 0:512], onesc[:, :],
                        bcB[:, tb * 512:(tb + 1) * 512], start=True, stop=True,
                        inc=("tPE", 1))
    for tb in range(4):
        em.wait("scalar", "tPE", bmm[tb])
        bev[tb] = em.op("scalar", "copy", out=rowB[:, tb * 512:(tb + 1) * 512],
                        in_=(p[4 + tb] if tb < 2 else p[tb - 2])[0:1, 0:512],
                        inc=("tACT", 1))

    em.wait("vector", "tACT", bev[3])
    em.op("vector", TS, out=rowA, in0=rowA, scalar1=-1.0 / C, scalar2=None,
          op0=ALU.mult)
    em.op("vector", TT, out=m2r, in0=rowA, in1=rowA, op=ALU.mult)
    em.op("vector", TS, out=rowB, in0=rowB, scalar1=1.0 / C, scalar2=None,
          op0=ALU.mult)
    vm1 = em.op("vector", TT, out=rowB, in0=rowB, in1=m2r, op=ALU.subtract,
                inc=("tDVE", 1))
    em.wait("scalar", "tDVE", vm1)
    em.op("scalar", "activation", out=rowB, in_=rowB, func=AF.Ln, bias=cvals[0:1, 0:1])
    rr = em.op("scalar", "activation", out=rowB, in_=rowB, func=AF.Exp,
               scale=cvals[0:1, 1:2], inc=("tACT", 1))
    em.wait("vector", "tACT", rr)
    arow = em.op("vector", TT, out=rowA, in0=rowA, in1=rowB, op=ALU.mult,
                 inc=("tDVE", 1))

    em.wait("tensor", "tDVE", arow)
    bbm, bbe, bam, bae = {}, {}, {}, {}
    for tb in range(4):
        bbm[tb] = em.op("tensor", "matmul", p[tb][:, :], onesr[:, :],
                        rowB[:, tb * 512:(tb + 1) * 512], start=True, stop=True,
                        inc=("tPE", 1))
    for tb in range(4):
        em.wait("scalar", "tPE", bbm[tb])
        bbe[tb] = em.op("scalar", "copy", out=bcB[:, tb * 512:(tb + 1) * 512],
                        in_=p[tb][:, :], inc=("tACT", 1))
    for tb in range(4):
        em.wait("tensor", "tACT", bbe[tb])
        bam[tb] = em.op("tensor", "matmul", p[tb][:, :], onesr[:, :],
                        rowA[:, tb * 512:(tb + 1) * 512], start=True, stop=True,
                        inc=("tPE", 1))
    for tb in range(4):
        em.wait("scalar", "tPE", bam[tb])
        bae[tb] = em.op("scalar", "copy", out=bcR[:, tb * 512:(tb + 1) * 512],
                        in_=p[tb][:, :], inc=("tACT", 1))
    bcast_done = bae[3]

    def ln_reduce_bcast(tA, tB, sq_scratch, rowA, rowB, guard_dve):
        em.wait("tensor", "tDVE", guard_dve)
        for src, row in ((tA, rowA), (tB, rowB)):
            for half in range(2):
                em.wait("tensor", "tACT", em.cnt["tACT"])
                for i in range(2):
                    tb = 2 * half + i
                    em.op("tensor", "matmul", p[4 + i][0:1, 0:512], onesc[:, :],
                          src[:, tb * 512:(tb + 1) * 512], start=True, stop=True,
                          inc=("tPE", 1))
                mm = em.cnt["tPE"]
                em.wait("scalar", "tPE", mm)
                for i in range(2):
                    tb = 2 * half + i
                    em.op("scalar", "copy", out=row[0:1, tb * 512:(tb + 1) * 512],
                          in_=p[4 + i][0:1, 0:512], inc=("tACT", 1))
        for src, row in ((tA, rowA), (tB, rowB)):
            for half in range(2):
                em.wait("tensor", "tACT", em.cnt["tACT"])
                for i in range(2):
                    tb = 2 * half + i
                    em.op("tensor", "matmul", p[4 + i][:, :], onesr[:, :],
                          row[0:1, tb * 512:(tb + 1) * 512], start=True, stop=True,
                          inc=("tPE", 1))
                mm = em.cnt["tPE"]
                em.wait("scalar", "tPE", mm)
                for i in range(2):
                    tb = 2 * half + i
                    em.op("scalar", "copy", out=src[:, tb * 512:(tb + 1) * 512],
                          in_=p[4 + i][:, :], inc=("tACT", 1))
        e2 = em.cnt["tACT"]
        em.wait("vector", "tACT", e2)
        em.op("vector", TS, out=tA, in0=tA, scalar1=1.0 / C, scalar2=None, op0=ALU.mult)
        em.op("vector", TS, out=tB, in0=tB, scalar1=1.0 / C, scalar2=None, op0=ALU.mult)
        em.op("vector", TT, out=sq_scratch, in0=tA, in1=tA, op=ALU.mult)
        vd = em.op("vector", TT, out=tB, in0=tB, in1=sq_scratch, op=ALU.subtract,
                   inc=("tDVE", 1))
        em.wait("scalar", "tDVE", vd)
        em.op("scalar", "activation", out=tB, in_=tB, func=AF.Ln, bias=cvals[:, 0:1])
        rr = em.op("scalar", "activation", out=tB, in_=tB, func=AF.Exp,
                   scale=cvals[:, 1:2], inc=("tACT", 1))
        em.wait("vector", "tACT", rr)
        em.op("vector", TT, out=tA, in0=tA, in1=tB, op=ALU.mult)
        return em.op("vector", TS, out=tA, in0=tA, scalar1=-1.0, scalar2=None,
                     op0=ALU.mult, inc=("tDVE", 1))

    ckpt = [0]

    def ok():
        ckpt[0] += 1
        return ckpt[0] < stop_after
    go = ok()

    xn_cast, st_cnt = {}, {}
    em.wait("vector", "tACT", bcast_done)
    for cc in range(NCH):
        em.op("vector", TT, out=xres[cc], in0=xres[cc], in1=bcB[:, :], op=ALU.mult)
        zr = em.op("vector", TT, out=xres[cc], in0=xres[cc], in1=bcR[:, :],
                   op=ALU.add, inc=("tDVE", 1))
        em.wait("scalar", "tDVE", zr)
        xn_cast[cc] = em.op("scalar", "copy", out=xn[:, cc * S:(cc + 1) * S],
                            in_=xres[cc], inc=("tACT", 1))
    xn_all = max(xn_cast.values())

    for cc in range(NCH):
        xc = xn[:, cc * S:(cc + 1) * S]
        em.wait("vector", "tACT", xn_cast[cc])
        for kind, fbase, scr in (("l", 0, l0_scr), ("h", 32, h0_scr)):
            tsl = (cc % 2) * 2 + (0 if kind == "l" else 1)
            slot = sb4[tsl]
            if cc >= 2:
                em.wait("vector", f"qT{tsl}", st_cnt[(kind, cc - 2)])
            def fk(k, fb=fbase, c=cc):
                return filt[:, fb + 4 * c + k: fb + 4 * c + k + 1]
            em.op("vector", TS, out=aof[:, 0:1024], in0=xc[:, 0:2048:2], scalar1=fk(1),
                  scalar2=None, op0=ALU.mult)
            em.op("vector", STT, out=aof[:, 0:1024], in0=xc[:, 1:2048:2],
                  scalar=fk(2), in1=aof[:, 0:1024], op0=ALU.mult, op1=ALU.add)
            em.op("vector", STT, out=aof[:, 1:1024], in0=xc[:, 1:2047:2],
                  scalar=fk(0), in1=aof[:, 1:1024], op0=ALU.mult, op1=ALU.add)
            em.op("vector", STT, out=slot[:, 1:1024], in0=xc[:, 2:2048:2],
                  scalar=fk(3), in1=aof[:, 0:1023], op0=ALU.mult, op1=ALU.add)
            em.op("vector", "tensor_copy", out=slot[:, 1024:1025],
                  in_=aof[:, 1023:1024], inc=("tDVE", 1))
            if cc < 2:
                em.op("vector", "memset", slot[:, 0:1], 0.0)
                em.op("vector", "memset", slot[:, 1025:1028], 0.0, inc=("tDVE", 1))
            dve_last = em.cnt["tDVE"]
            em.wait("sync", "tDVE", dve_last)
            st_cnt[(kind, cc)] = em.dma("sync", scr[cc], slot[:, 0:1026], f"qT{tsl}")

    c_l1_done = {}
    for cc in range(NCH):
        tsl = cc % 2
        slot = sb4[tsl]
        if cc >= 2:
            em.wait("sync", "tDVE", c_l1_done[cc - 2])
        em.wait("sync", f"qT{(cc % 2) * 2}", st_cnt[("l", cc)])
        em.wait("sync", f"qT{tsl}", em.cnt[f"qT{tsl}"])
        li = em.dma("sync", slot[:, 0:1026], l0_scr[cc], f"qT{tsl}")
        em.wait("vector", f"qT{tsl}", li)
        for dsl, fbase in ((low2v[:, cc * 512: cc * 512 + 512], 0),
                           (h1v[:, cc * 514 + 1: cc * 514 + 513], 32)):
            def fk(k, fb=fbase, c=cc):
                return filt[:, fb + 4 * c + k: fb + 4 * c + k + 1]
            a5 = aof[:, 0:512]
            em.op("vector", TS, out=a5, in0=slot[:, 0:1024:2], scalar1=fk(0),
                  scalar2=None, op0=ALU.mult)
            for k in (1, 2):
                em.op("vector", STT, out=a5, in0=slot[:, k:k + 1024:2], scalar=fk(k),
                      in1=a5, op0=ALU.mult, op1=ALU.add)
            em.op("vector", STT, out=dsl, in0=slot[:, 3:1026:2], scalar=fk(3),
                  in1=a5, op0=ALU.mult, op1=ALU.add, inc=("tDVE", 1))
        if cc == 0:
            em.op("vector", "memset",
                  h1v.rearrange("p (c n) -> p c n", c=8)[:, :, 0:1], 0.0)
            em.op("vector", "memset",
                  h1v.rearrange("p (c n) -> p c n", c=8)[:, :, 513:514], 0.0,
                  inc=("tDVE", 1))
        c_l1_done[cc] = em.cnt["tDVE"]

    pfree = {i: (0, 0) for i in range(4)}
    s23_ready, s4_ready = {}, {}
    out_st, asm_done = {}, {}
    pending_store = [None]
    ao_gate = [(0, 0)]
    ao_gate_emitted = [False]

    def stage1(q):
        qc0, qc1 = q * TQ, (q + 1) * TQ
        a_cnt = {}
        for oc in range(NOC):
            g = oc // 4
            gi = q * 8 + g
            if oc % 4 == 0:
                if gi not in s1_ready:
                    s1_ready[gi] = load_s1(g, q)
                if g + 1 < 8 and gi + 1 not in s1_ready:
                    s1_ready[gi + 1] = load_s1(g + 1, q)
            pu1, pu2 = p[2 * (oc % 2)], p[2 * (oc % 2) + 1]
            wait_w("tensor", s1_ready[gi])
            fa1, _ = pfree[2 * (oc % 2)]
            em.wait("tensor", "tACT", fa1)
            fa2, _ = pfree[2 * (oc % 2) + 1]
            em.wait("tensor", "tACT", fa2)
            slotW1, slotW2 = (2 * g) % 4, (2 * g + 1) % 4
            ocl = oc % 4
            for ic in range(NCH):
                if oc == 0 and q == 0:
                    em.wait("tensor", "tACT", xn_cast[ic])
                em.op("tensor", "matmul", pu1[:, :],
                      wslot[slotW1][:, ic * 512 + ocl * 128: ic * 512 + ocl * 128 + 128],
                      xn[:, ic * S + qc0: ic * S + qc1],
                      start=(ic == 0), stop=(ic == NCH - 1), inc=("tPE", 1))
            mm1 = em.cnt["tPE"]
            for ic in range(NCH):
                em.op("tensor", "matmul", pu2[:, :],
                      wslot[slotW2][:, ic * 512 + ocl * 128: ic * 512 + ocl * 128 + 128],
                      xn[:, ic * S + qc0: ic * S + qc1],
                      start=(ic == 0), stop=(ic == NCH - 1), inc=("tPE", 1))
            mm2 = em.cnt["tPE"]
            if ocl == 3:
                slot_free[slotW1] = mm1
                slot_free[slotW2] = mm2
            em.wait("scalar", "tPE", mm1)
            if oc >= 2:
                em.wait("scalar", "tPOOL", a_cnt[oc - 2])
            sl = (oc % 2) * TQ
            s1c = em.op("scalar", "activation", out=t1a[:, sl:sl + TQ], in_=pu1[:, :],
                        func=AF.Silu, scale=s1v[:, oc:oc + 1], inc=("tACT", 1))
            em.wait("scalar", "tPE", mm2)
            ev2 = em.op("scalar", "activation", out=t2a[:, sl:sl + TQ], in_=pu2[:, :],
                        func=AF.Identity, scale=s2v[:, oc:oc + 1], inc=("tACT", 1))
            em.wait("gpsimd", "tACT", ev2)
            a_cnt[oc] = em.op("gpsimd", TT, out=abuf.ap()[:, oc * TQ:(oc + 1) * TQ],
                              in0=t1a[:, sl:sl + TQ], in1=t2a[:, sl:sl + TQ],
                              op=ALU.mult, inc=("tPOOL", 1))
            pfree[2 * (oc % 2)] = (s1c, a_cnt[oc])
            pfree[2 * (oc % 2) + 1] = (ev2, a_cnt[oc])
        return em.cnt["tPOOL"]

    def stage23(q, stage_i, scale_v, inv, outv, ready_pool, ready_act, interleave=None):
        kbase = 1000 + q * 100 + stage_i * 50
        b_cnt = {}
        for jc in range(NOC):
            if interleave is not None:
                interleave(jc)
            gg = kbase + jc
            if gg not in s23_ready:
                s23_ready[gg] = load_s23(stage_i, jc, q)
            slot = jc % 4
            pb = p[2 + jc % 2]
            wait_w("tensor", s23_ready[gg])
            if jc == 0:
                if ready_pool:
                    em.wait("tensor", "tPOOL", ready_pool)
                if ready_act:
                    em.wait("tensor", "tACT", ready_act)
            fa, _ = pfree[2 + jc % 2]
            em.wait("tensor", "tACT", fa)
            for oc in range(NOC):
                em.op("tensor", "matmul", pb[:, :],
                      wslot[slot][:, oc * 128: oc * 128 + 128],
                      inv(oc), start=(oc == 0), stop=(oc == NOC - 1), inc=("tPE", 1))
            mm = em.cnt["tPE"]
            slot_free[slot] = mm
            em.wait("scalar", "tPE", mm)
            b_cnt[jc] = em.op("scalar", "activation", out=outv(jc), in_=pb[:, :],
                              func=AF.Gelu, scale=scale_v[:, jc:jc + 1], inc=("tACT", 1))
            pfree[2 + jc % 2] = (b_cnt[jc], 0)
        return max(b_cnt.values())

    def stage4(q, c_all, with_ao, interleave=None):
        qc0, qc1 = q * TQ, (q + 1) * TQ
        for cc in range(NCH):
            if interleave is not None:
                interleave(cc)
            gi = 10000 + q * 8 + cc
            if gi not in s4_ready:
                s4_ready[gi] = load_s4(cc, q)
            if cc + 1 < NCH and gi + 1 not in s4_ready:
                s4_ready[gi + 1] = load_s4(cc + 1, q)
            ki = (q, cc)
            slot = cc % 2
            wsl = cc % 4
            prev = (q, cc - 2) if cc >= 2 else ((q - 1, cc + 6) if q > 0 else None)
            if prev in asm_done:
                em.wait("sync", "tPOOL", asm_done[prev])
            if with_ao and not ao_gate_emitted[0]:
                em.wait("sync", "qAO0", ao_gate[0][0])
                em.wait("sync", "qAO1", ao_gate[0][1])
                ao_gate_emitted[0] = True
            em.dma("sync", xq[:, slot * TQ: slot * TQ + TQ],
                   xT[cc * 128:(cc + 1) * 128, qc0:qc1], f"qX{slot}")
            if with_ao:
                em.dma("sync", aoq[:, slot * TQ: slot * TQ + TQ],
                       ao_scr[cc][:, qc0:qc1], f"qX{slot}")
            fl = em.cnt[f"qX{slot}"]
            if pending_store[0] is not None:
                pk, pslot, pcc, pq0, pq1 = pending_store[0]
                em.wait("sync", "tPOOL", asm_done[pk])
                out_st[pk] = em.dma("sync", yT[pcc * 128:(pcc + 1) * 128, pq0:pq1],
                                    outb[:, pslot * TQ: pslot * TQ + TQ], f"qO{pslot}")
                pending_store[0] = None
            py = p[cc % 4]
            wait_w("tensor", s4_ready[gi])
            if cc == 0:
                em.wait("tensor", "tACT", c_all)
            fa, fp = pfree[cc % 4]
            em.wait("tensor", "tACT", fa)
            em.wait("tensor", "tPOOL", fp)
            for lc in range(NOC):
                em.op("tensor", "matmul", py[:, :],
                      wslot[wsl][:, lc * 128:(lc + 1) * 128],
                      abuf.ap()[:, lc * TQ:(lc + 1) * TQ],
                      start=(lc == 0), stop=(lc == NOC - 1), inc=("tPE", 1))
            mm = em.cnt["tPE"]
            slot_free[wsl] = mm
            em.wait("scalar", "tPE", mm)
            if prev in out_st:
                em.wait("scalar", f"qO{slot}", out_st[prev])
            ev = em.op("scalar", "activation", out=outb[:, slot * TQ: slot * TQ + TQ],
                       in_=py[:, :], func=AF.Identity, scale=w3sv[:, cc:cc + 1],
                       inc=("tACT", 1))
            em.wait("gpsimd", "tACT", ev)
            em.wait("gpsimd", f"qX{slot}", fl)
            em.op("gpsimd", TT, out=outb[:, slot * TQ: slot * TQ + TQ],
                  in0=outb[:, slot * TQ: slot * TQ + TQ],
                  in1=xq[:, slot * TQ: slot * TQ + TQ], op=ALU.add,
                  inc=None if with_ao else ("tPOOL", 1))
            if with_ao:
                em.op("gpsimd", TT, out=outb[:, slot * TQ: slot * TQ + TQ],
                      in0=outb[:, slot * TQ: slot * TQ + TQ],
                      in1=aoq[:, slot * TQ: slot * TQ + TQ], op=ALU.add,
                      inc=("tPOOL", 1))
            asm_done[ki] = em.cnt["tPOOL"]
            pfree[cc % 4] = (ev, asm_done[ki])
            pending_store[0] = (ki, slot, cc, qc0, qc1)
        pk, pslot, pcc, pq0, pq1 = pending_store[0]
        em.wait("sync", "tPOOL", asm_done[pk])
        out_st[pk] = em.dma("sync", yT[pcc * 128:(pcc + 1) * 128, pq0:pq1],
                            outb[:, pslot * TQ: pslot * TQ + TQ], f"qO{pslot}")
        pending_store[0] = None

    a_all0 = stage1(0)
    b_all0 = stage23(0, 0, h1sv, lambda oc: abuf.ap()[:, oc * TQ:(oc + 1) * TQ],
                     lambda jc: bbuf.ap()[:, jc * TQ:(jc + 1) * TQ], a_all0, None)

    for cc in range(NCH):
        l2c = low2v[:, cc * 512: cc * 512 + 512]
        m0 = em.op("vector", TS, out=aof[:, 0:512], in0=l2c, scalar1=0.0, scalar2=None,
                   op0=ALU.min, inc=("tDVE", 1))
        em.wait("scalar", "tDVE", m0)
        e0 = em.op("scalar", "activation", out=aof[:, 512:1024], in_=aof[:, 0:512],
                   func=AF.Exp, inc=("tACT", 1))
        em.op("vector", TS, out=aof[:, 1024:1536], in0=l2c, scalar1=0.0, scalar2=None,
              op0=ALU.max)
        em.wait("vector", "tACT", e0)
        em.op("vector", TT, out=qfmv[:, cc * 512: cc * 512 + 512], in0=aof[:, 512:1024],
              in1=aof[:, 1024:1536], op=ALU.add, inc=("tDVE", 1))
    for i in range(4):
        em.op("vector", "memset", vtb[i][:, 64:65], 1.0)
        em.op("vector", "memset", vtb[i][:, 129:130], 1.0, inc=("tDVE", 1))

    kv_ev, tr_done = {}, {}
    kvst = {"prev_kvmm": 0, "prev_trev": 0, "first": True}
    nst = {"att_tt": 0, "numev": 0, "dens": 0}
    dr_cnt = {}
    elu_all = em.cnt["tDVE"]

    def kv_T(cc):
        if kvst["first"]:
            em.wait("tensor", "tACT", identb_done)
            em.wait("tensor", "tDVE", elu_all)
            kvst["first"] = False
        em.wait("tensor", "tDVE", kvst["prev_trev"])
        em.wait("vector", "tPE", kvst["prev_kvmm"])
        mmt = {}
        for j in range(8):
            tn, which = j // 2, j % 2
            srcv = (low2v, qfmv)[which]
            psl = ptr[j // 4][:, (j % 4) * 128:(j % 4) * 128 + 128]
            mmt[j] = em.op("tensor", "transpose", psl,
                           srcv[:, cc * 512 + tn * 128: cc * 512 + tn * 128 + 128],
                           identb[:, :], inc=("tPE", 1))
        for j in range(8):
            tn, which = j // 2, j % 2
            psl = ptr[j // 4][:, (j % 4) * 128:(j % 4) * 128 + 128]
            if j % 4 == 0:
                em.wait("vector", "tPE", mmt[j + 3])
            if which == 0:
                em.op("vector", "tensor_copy", out=vtb[tn][:, 0:64], in_=psl[:, 0:64])
                em.op("vector", "tensor_copy", out=vtb[tn][:, 65:129],
                      in_=psl[:, 64:128], inc=("tDVE", 1))
            else:
                em.op("vector", "tensor_copy", out=qtb[tn][:, 0:128],
                      in_=psl[:, :], inc=("tDVE", 1))
        tr_done[cc] = em.cnt["tDVE"]
        kvst["prev_trev"] = tr_done[cc]

    def kv_M(cc):
        em.wait("tensor", "tDVE", tr_done[cc])
        if cc > 0:
            em.wait("tensor", "tACT", kv_ev[2 * cc - 1])
        for tn in range(4):
            em.op("tensor", "matmul", p[0][0:64, 0:65], qtb[tn][:, 0:64],
                  vtb[tn][:, 0:65], start=(tn == 0), stop=(tn == 3), inc=("tPE", 1))
        for tn in range(4):
            em.op("tensor", "matmul", p[1][0:64, 0:65], qtb[tn][:, 64:128],
                  vtb[tn][:, 65:130], start=(tn == 0), stop=(tn == 3), inc=("tPE", 1))
        kvst["prev_kvmm"] = em.cnt["tPE"]
        em.wait("scalar", "tPE", kvst["prev_kvmm"])
        kv_ev[2 * cc] = em.op("scalar", "copy", out=kvs[0:64, 65 * cc: 65 * cc + 65],
                              in_=p[0][0:64, 0:65], inc=("tACT", 1))
        kv_ev[2 * cc + 1] = em.op("scalar", "copy",
                                  out=kvs[64:128, 65 * cc: 65 * cc + 65],
                                  in_=p[1][0:64, 0:65], inc=("tACT", 1))

    def num_half(wv, half):
        for i in (2 * half, 2 * half + 1):
            h = 4 * wv + i
            pi = 4 + i % 2
            em.wait("tensor", "tACT", kv_ev[h])
            if wv > 0 or half > 0:
                em.wait("tensor", "tDVE",
                        nst["att_tt"] if half == 0 else nst["dens"])
                if half > 0:
                    em.wait("tensor", "tACT", nst["numev"])
            cc = h // 2
            rbase = 64 * (h % 2)
            mm = em.op("tensor", "matmul", p[pi][0:65, 0:512],
                       kvs[rbase:rbase + 64, 65 * cc: 65 * cc + 65],
                       qfmv[rbase:rbase + 64, cc * 512: cc * 512 + 512],
                       start=True, stop=True, inc=("tPE", 1))
            em.wait("scalar", "tPE", mm)
            nst["numev"] = em.op("scalar", "copy",
                                 out=attv[rbase:rbase + 64, cc * 514 + 1: cc * 514 + 513],
                                 in_=p[pi][0:64, 0:512], inc=("tACT", 1))
            em.wait("vector", "tPE", mm)
            nst["dens"] = em.op("vector", TS, out=bcB[0:1, i * 512:(i + 1) * 512],
                                in0=p[pi][64:65, 0:512], scalar1=EPS_ATTN,
                                scalar2=None, op0=ALU.add, inc=("tDVE", 1))
        if half == 1:
            dr_cnt[wv] = em.op("vector", "reciprocal", out=bcR[0:1, 0:2048],
                               in_=bcB[0:1, 0:2048], inc=("tDVE", 1))

    def bc_half(wv, half):
        if half == 0:
            em.wait("tensor", "tDVE", dr_cnt[wv])
            em.wait("tensor", "tACT", nst["numev"])
        else:
            em.wait("tensor", "tDVE", nst["att_tt"])
        bcs = {}
        for i in (2 * half, 2 * half + 1):
            bcs[i] = em.op("tensor", "matmul", p[4 + i % 2][0:64, 0:512],
                           onesr[0:1, 0:64], bcR[0:1, i * 512:(i + 1) * 512],
                           start=True, stop=True, inc=("tPE", 1))
        for i in (2 * half, 2 * half + 1):
            h = 4 * wv + i
            cc = h // 2
            rbase = 64 * (h % 2)
            em.wait("vector", "tPE", bcs[i])
            nst["att_tt"] = em.op(
                "vector", TT,
                out=attv[rbase:rbase + 64, cc * 514 + 1: cc * 514 + 513],
                in0=attv[rbase:rbase + 64, cc * 514 + 1: cc * 514 + 513],
                in1=p[4 + i % 2][0:64, 0:512], op=ALU.mult, inc=("tDVE", 1))

    s3_tasks = {}
    for k in range(8):
        s3_tasks[2 + 4 * k] = (lambda c=k: kv_T(c))
        if k < 7:
            s3_tasks[4 + 4 * k] = (lambda c=k: kv_M(c))
    for sl, w, hf in ((5, 0, 0), (9, 0, 1), (11, 0, 2), (13, 0, 3),
                      (15, 1, 0), (17, 1, 1), (19, 1, 2), (21, 1, 3),
                      (23, 2, 0), (25, 2, 1), (27, 2, 2), (29, 2, 3),
                      (31, 3, 0)):
        fn = (num_half if hf < 2 else bc_half)
        s3_tasks[sl] = (lambda f=fn, a=w, b=hf: f(a, b % 2))

    def s3_inter(jc):
        if jc in s3_tasks:
            s3_tasks[jc]()

    def s4_inter(cc):
        if cc == 1:
            num_half(3, 1)
        elif cc == 2:
            bc_half(3, 0)
        elif cc == 3:
            bc_half(3, 1)

    c_all0 = stage23(0, 1, h2sv, lambda oc: bbuf.ap()[:, oc * TQ:(oc + 1) * TQ],
                     lambda jc: abuf.ap()[:, jc * TQ:(jc + 1) * TQ], None, b_all0,
                     interleave=s3_inter)
    kv_M(7)
    pfree[0] = (kv_ev[14], pfree[0][1])
    pfree[1] = (kv_ev[15], pfree[1][1])
    stage4(0, c_all0, with_ao=False, interleave=s4_inter)
    em.op("vector", "memset", attv.rearrange("p (c n) -> p c n", c=8)[:, :, 0:1], 0.0)
    em.op("vector", "memset", attv.rearrange("p (c n) -> p c n", c=8)[:, :, 513:514],
          0.0, inc=("tDVE", 1))

    def recon(cc, sa_ap, sb_ap, dst_ap, dst_off, nh):
        def sa(d):
            return sa_ap[:, d: d + nh]
        def sbv(d):
            return sb_ap[:, d: d + nh]
        def hcm(j, c=cc):
            return fcmb[:, 5 * c + j: 5 * c + j + 1]
        def gcm(j, c=cc):
            return fcmb[:, 40 + 5 * c + j: 40 + 5 * c + j + 1]
        a5 = aof[:, 0:nh]
        em.op("vector", TS, out=a5, in0=sa(0), scalar1=hcm(0), scalar2=None, op0=ALU.mult)
        em.op("vector", STT, out=a5, in0=sa(1), scalar=hcm(1), in1=a5, op0=ALU.mult, op1=ALU.add)
        em.op("vector", STT, out=a5, in0=sa(2), scalar=hcm(2), in1=a5, op0=ALU.mult, op1=ALU.add)
        em.op("vector", STT, out=a5, in0=sbv(0), scalar=gcm(0), in1=a5, op0=ALU.mult, op1=ALU.add)
        em.op("vector", STT, out=a5, in0=sbv(1), scalar=gcm(1), in1=a5, op0=ALU.mult, op1=ALU.add)
        em.op("vector", STT, out=dst_ap[:, dst_off: dst_off + 2 * nh: 2], in0=sbv(2),
              scalar=gcm(2), in1=a5, op0=ALU.mult, op1=ALU.add, inc=("tDVE", 1))
        em.op("vector", TS, out=a5, in0=sa(1), scalar1=hcm(3), scalar2=None, op0=ALU.mult)
        em.op("vector", STT, out=a5, in0=sa(2), scalar=hcm(4), in1=a5, op0=ALU.mult, op1=ALU.add)
        em.op("vector", STT, out=a5, in0=sbv(1), scalar=gcm(3), in1=a5, op0=ALU.mult, op1=ALU.add)
        return em.op("vector", STT, out=dst_ap[:, dst_off + 1: dst_off + 2 * nh: 2],
                     in0=sbv(2), scalar=gcm(4), in1=a5, op0=ALU.mult, op1=ALU.add,
                     inc=("tDVE", 1))

    r1_st = {}
    for cc in range(NCH):
        tsl = cc % 2
        slot = sb4[tsl]
        if cc >= 2:
            em.wait("vector", f"qT{tsl}", r1_st[cc - 2])
        d = recon(cc, attv[:, cc * 514: cc * 514 + 514],
                  h1v[:, cc * 514: cc * 514 + 514], slot, 1, 512)
        if cc < 2:
            em.op("vector", "memset", slot[:, 0:1], 0.0)
            em.op("vector", "memset", slot[:, 1025:1028], 0.0, inc=("tDVE", 1))
            d = em.cnt["tDVE"]
        em.wait("sync", "tDVE", d)
        r1_st[cc] = em.dma("sync", r1_scr[cc], slot[:, 0:1026], f"qT{tsl}")

    low_st, f_done = {}, {}
    for cc in range(NCH):
        rtsl = cc % 2
        htsl = 2 + cc % 2
        rsl = sb4[rtsl]
        hsl = sb4[htsl]
        osl = lowsl[cc % 2]
        if cc >= 2:
            em.wait("sync", "tDVE", f_done[cc - 2])
        em.wait("sync", f"qT{rtsl}", r1_st[cc])
        em.wait("sync", f"qT{htsl}", em.cnt[f"qT{htsl}"])
        rl = em.dma("sync", rsl[:, 0:1026], r1_scr[cc], f"qT{rtsl}")
        hl = em.dma("sync", hsl[:, 0:1026], h0_scr[cc], f"qT{htsl}")
        if cc >= 2:
            em.wait("vector", f"qLs{cc % 2}", low_st[cc - 2])
        em.wait("vector", f"qT{rtsl}", rl)
        em.wait("vector", f"qT{htsl}", hl)
        f_done[cc] = recon(cc, rsl[:, 0:1026], hsl[:, 0:1026], osl, 0, 1024)
        em.op("vector", TT, out=azB[:, 0:2048], in0=xn[:, cc * S:(cc + 1) * S],
              in1=osl[:, :], op=ALU.add)
        em.op("vector", TT, out=aof[:, 0:2048], in0=azB[:, 0:2048], in1=azB[:, 0:2048],
              op=ALU.mult)
        if cc == 0:
            em.op("vector", "tensor_copy", out=bcR[:, :], in_=azB[:, 0:2048])
            em.op("vector", "tensor_copy", out=bcB[:, :], in_=aof[:, 0:2048],
                  inc=("tDVE", 1))
        else:
            em.op("vector", TT, out=bcR[:, :], in0=bcR[:, :], in1=azB[:, 0:2048],
                  op=ALU.add)
            em.op("vector", TT, out=bcB[:, :], in0=bcB[:, :], in1=aof[:, 0:2048],
                  op=ALU.add, inc=("tDVE", 1))
        f_done[cc] = em.cnt["tDVE"]
        em.wait("sync", "tDVE", f_done[cc])
        low_st[cc] = em.dma("sync", low_scr[cc], osl[:, :], f"qLs{cc % 2}")

    ln2_acc = em.cnt["tDVE"]

    a_all1 = stage1(1)
    b_all1 = stage23(1, 0, h1sv, lambda oc: abuf.ap()[:, oc * TQ:(oc + 1) * TQ],
                     lambda jc: bbuf.ap()[:, jc * TQ:(jc + 1) * TQ], a_all1, None)

    c_all1 = stage23(1, 1, h2sv, lambda oc: bbuf.ap()[:, oc * TQ:(oc + 1) * TQ],
                     lambda jc: abuf.ap()[:, jc * TQ:(jc + 1) * TQ], None, b_all1)
    stage4(1, c_all1, with_ao=False)
    ln2_done = ln_reduce_bcast(bcR[:, :], bcB[:, :], aof[:, 0:2048],
                               azB[0:1, 0:2048], aof[0:1, 0:2048], ln2_acc)

    ao_st, ao_dve = {}, {}
    for cc in range(NCH):
        isl = lowsl[cc % 2]
        osl = aosl[cc % 2]
        if cc < 2:
            em.wait("sync", "tDVE", ln2_acc)
        else:
            em.wait("sync", "tDVE", ao_dve[cc - 2])
        ll = em.dma("sync", isl[:, :], low_scr[cc], f"qLs{cc % 2}")
        em.wait("vector", f"qLs{cc % 2}", ll)
        if cc >= 2:
            em.wait("vector", f"qAO{cc % 2}", ao_st[cc - 2])
        em.op("vector", TT, out=azB[:, 0:2048], in0=xn[:, cc * S:(cc + 1) * S],
              in1=isl[:, :], op=ALU.add)
        em.op("vector", TT, out=azB[:, 0:2048], in0=azB[:, 0:2048], in1=bcB[:, :],
              op=ALU.mult)
        em.op("vector", TT, out=azB[:, 0:2048], in0=azB[:, 0:2048], in1=bcR[:, :],
              op=ALU.add)
        ao_dve[cc] = em.op("vector", TS, out=osl, in0=azB[:, 0:2048],
                           scalar1=gamv[:, cc:cc + 1], scalar2=betv[:, cc:cc + 1],
                           op0=ALU.mult, op1=ALU.add, inc=("tDVE", 1))
        em.wait("sync", "tDVE", ao_dve[cc])
        ao_st[cc] = em.dma("sync", ao_scr[cc], osl, f"qAO{cc % 2}")
    ao_gate[0] = (em.cnt["qAO0"], em.cnt["qAO1"])

    fx_dve = {}
    fi = 0
    for fq in (0, 1):
        for cc in range(NCH):
            if fi >= 2:
                em.wait("sync", "tDVE", fx_dve[fi - 2])
            em.wait("sync", f"qO{cc % 2}", out_st[(fq, cc)])
            if fi == 0:
                em.wait("sync", "qAO0", ao_gate[0][0])
                em.wait("sync", "qAO1", ao_gate[0][1])
            em.dma("sync", fyb[fi % 2][:, 0:512],
                   yT[cc * 128:(cc + 1) * 128, fq * TQ:(fq + 1) * TQ], f"qF{fi % 2}")
            fl = em.dma("sync", fab[fi % 2], ao_scr[cc][:, fq * TQ:(fq + 1) * TQ],
                        f"qF{fi % 2}")
            em.wait("vector", f"qF{fi % 2}", fl)
            fx_dve[fi] = em.op("vector", TT, out=fyb[fi % 2][:, 0:512],
                               in0=fyb[fi % 2][:, 0:512], in1=fab[fi % 2], op=ALU.add,
                               inc=("tDVE", 1))
            em.wait("sync", "tDVE", fx_dve[fi])
            em.dma("sync", yT[cc * 128:(cc + 1) * 128, fq * TQ:(fq + 1) * TQ],
                   fyb[fi % 2][:, 0:512], f"qF{fi % 2}")
            fi += 1
    for q in range(2, nq):
        a_all = stage1(q)
        b_all = stage23(q, 0, h1sv, lambda oc: abuf.ap()[:, oc * TQ:(oc + 1) * TQ],
                        lambda jc: bbuf.ap()[:, jc * TQ:(jc + 1) * TQ], a_all, None)
        c_all = stage23(q, 1, h2sv, lambda oc: bbuf.ap()[:, oc * TQ:(oc + 1) * TQ],
                        lambda jc: abuf.ap()[:, jc * TQ:(jc + 1) * TQ], None, b_all)
        stage4(q, c_all, with_ao=True)

    em.wait("sync", "qO0", em.cnt["qO0"])
    em.wait("sync", "qO1", em.cnt["qO1"])
    em.wait("sync", "qF0", em.cnt["qF0"])
    em.wait("sync", "qF1", em.cnt["qF1"])
    em.resolve_dwaits()

    if stop_after < 99:
        total = sum(len(em.q[e]) for e in ENGINES)
        budget = int(total * stop_after / 100.0)
        pos = {e: 0 for e in ENGINES}
        cnt = {k: 0 for k in em.cnt}
        done = 0
        progress = True
        while progress and done < budget:
            progress = False
            for e in ENGINES:
                while pos[e] < len(em.q[e]) and done < budget:
                    it = em.q[e][pos[e]]
                    if it[0] == "wait":
                        if cnt[it[1]] < it[2]:
                            break
                    else:
                        if it[4] is not None:
                            cnt[it[4][0]] += it[4][1]
                    pos[e] += 1
                    done += 1
                    progress = True
        for e in ENGINES:
            em.q[e] = em.q[e][:pos[e]]

    with contextlib.ExitStack() as stack:
        sems = {name: stack.enter_context(nc.semaphore(name)) for name in em.cnt}
        blk = stack.enter_context(nc.Block())

        def mk(engname):
            def f(eng):
                em.replay(eng, engname, sems)
            return f
        blk.sync(mk("sync"))
        blk.tensor(mk("tensor"))
        blk.vector(mk("vector"))
        blk.scalar(mk("scalar"))
        blk.gpsimd(mk("gpsimd"))
    for t in reversed(ctxs):
        t.__exit__(None, None, None)
    return nc, em



def _prep_shared(inputs):
    d = {}
    w1t = np.ascontiguousarray(inputs["L0_ffn_w1_w"].T)
    w2t = np.ascontiguousarray(inputs["L0_ffn_w2_w"].T)
    h1t = np.ascontiguousarray(inputs["L0_ffn_h1_w"].T)
    h2t = np.ascontiguousarray(inputs["L0_ffn_h2_w"].T)
    w3t = np.ascontiguousarray(inputs["L0_ffn_w3_w"].T)
    d["w1g"] = np.ascontiguousarray(
        w1t.reshape(8, 128, 8, 512).transpose(2, 1, 0, 3).reshape(8, 128, 4096))
    d["w2g"] = np.ascontiguousarray(
        w2t.reshape(8, 128, 8, 512).transpose(2, 1, 0, 3).reshape(8, 128, 4096))
    d["h1g"] = np.ascontiguousarray(
        h1t.reshape(32, 128, 32, 128).transpose(2, 1, 0, 3).reshape(32, 128, 4096))
    d["h2g"] = np.ascontiguousarray(
        h2t.reshape(32, 128, 32, 128).transpose(2, 1, 0, 3).reshape(32, 128, 4096))
    d["w3g"] = np.ascontiguousarray(
        w3t.reshape(32, 128, 8, 128).transpose(2, 1, 0, 3).reshape(8, 128, 4096))
    d["s1"] = np.ascontiguousarray(inputs["L0_ffn_w1_s"].reshape(32, 128).T)
    d["s2"] = np.ascontiguousarray(inputs["L0_ffn_w2_s"].reshape(32, 128).T)
    d["h1s"] = np.ascontiguousarray(inputs["L0_ffn_h1_s"].reshape(32, 128).T)
    d["h2s"] = np.ascontiguousarray(inputs["L0_ffn_h2_s"].reshape(32, 128).T)
    d["w3s"] = np.ascontiguousarray(inputs["L0_ffn_w3_s"].reshape(8, 128).T)
    d["gam"] = np.ascontiguousarray(inputs["L0_attn_norm_gamma"].reshape(8, 128).T)
    d["bet"] = np.ascontiguousarray(inputs["L0_attn_norm_beta"].reshape(8, 128).T)
    for nm, key in (("hfil", "L0_attn_h_filter"), ("gfil", "L0_attn_g_filter"),
                    ("hsyn", "L0_attn_h_synth"), ("gsyn", "L0_attn_g_synth")):
        f = inputs[key]
        d[nm] = np.ascontiguousarray(f.T.reshape(8, 128, 4).transpose(1, 0, 2))
    d["ident"] = np.eye(128, dtype=np.float32)
    return d


def kernel(**inputs):
    x = np.asarray(inputs["L0_attn_x"], dtype=np.float32)
    shared = _prep_shared({k: np.asarray(v, dtype=np.float32)
                           for k, v in inputs.items() if k != "L0_attn_x"})
    nc, _ = build_program()
    in_maps = []
    for b in range(8):
        m = dict(shared)
        m["xT"] = np.ascontiguousarray(x[b].T)
        in_maps.append(m)
    res = run_bass_kernel_spmd(nc, in_maps, core_ids=list(range(8)))
    out = np.stack([np.asarray(res.results[b]["yT"]).T for b in range(8)], axis=0)
    return np.ascontiguousarray(out.astype(np.float32))


# version 18 (final)
# speedup vs baseline: 1.0582x; 1.0016x over previous
import numpy as np
import contextlib
import concourse.bass as bass
import concourse.mybir as mybir
from concourse.bass_utils import run_bass_kernel_spmd

F32 = mybir.dt.float32
BF16 = mybir.dt.bfloat16
AF = mybir.ActivationFunctionType
ALU = mybir.AluOpType

C = 1024
S = 2048
HID = 4096
NCH = C // 128
NOC = HID // 128
NQ = 4
TQ = S // NQ
EPS_LN = 1e-5
EPS_ATTN = 1e-6
NHEAD = 16

ENGINES = ("sync", "tensor", "vector", "scalar", "gpsimd")
TT, TS, STT = "tensor_tensor", "tensor_scalar", "scalar_tensor_tensor"


class Emitter:
    def __init__(self):
        self.q = {e: [] for e in ENGINES}
        self.cnt = {}

    def new_sem(self, name):
        self.cnt[name] = 0

    def op(self, eng, meth, *args, inc=None, **kw):
        if inc is not None:
            self.cnt[inc[0]] += inc[1]
        self.q[eng].append(("op", meth, args, kw, inc))
        return self.cnt[inc[0]] if inc else None

    def dma(self, eng, out, in_, sem):
        return self.op(eng, "dma_start", inc=(sem, 16), out=out, in_=in_)

    def wait(self, eng, sem, val):
        assert val <= self.cnt[sem], f"wait {sem}>={val} > emitted {self.cnt[sem]}"
        if val > 0:
            self.q[eng].append(("wait", sem, val))

    def wait_dma(self, eng, sem, val):
        assert val <= self.cnt[sem], f"wait {sem}>={val} > emitted {self.cnt[sem]}"
        if val > 0:
            self.q[eng].append(("dwait", sem, val))

    def resolve_dwaits(self):
        for eng in ENGINES:
            for i, item in enumerate(self.q[eng]):
                if item[0] == "dwait":
                    _, sem, val = item
                    self.q[eng][i] = ("wait", sem, min(val + 16, self.cnt[sem]))

    def replay(self, eng_obj, eng_name, sems):
        for item in self.q[eng_name]:
            if item[0] == "wait":
                _, sem, val = item
                eng_obj.wait_ge(sems[sem], val)
            else:
                _, meth, args, kw, inc = item
                inst = getattr(eng_obj, meth)(*args, **kw)
                if inc is not None:
                    inst.then_inc(sems[inc[0]], inc[1])


def build_program(nq=NQ, stop_after=99):
    nc = bass.Bass("TRN2", target_bir_lowering=False, debug=False)

    xT = nc.dram_tensor("xT", [C, S], F32, kind="ExternalInput").ap()
    w1g = nc.dram_tensor("w1g", [8, 128, 4096], F32, kind="ExternalInput").ap()
    w2g = nc.dram_tensor("w2g", [8, 128, 4096], F32, kind="ExternalInput").ap()
    h1g = nc.dram_tensor("h1g", [32, 128, 4096], F32, kind="ExternalInput").ap()
    h2g = nc.dram_tensor("h2g", [32, 128, 4096], F32, kind="ExternalInput").ap()
    w3g = nc.dram_tensor("w3g", [8, 128, 4096], F32, kind="ExternalInput").ap()
    s1i = nc.dram_tensor("s1", [128, 32], F32, kind="ExternalInput").ap()
    s2i = nc.dram_tensor("s2", [128, 32], F32, kind="ExternalInput").ap()
    h1si = nc.dram_tensor("h1s", [128, 32], F32, kind="ExternalInput").ap()
    h2si = nc.dram_tensor("h2s", [128, 32], F32, kind="ExternalInput").ap()
    w3si = nc.dram_tensor("w3s", [128, 8], F32, kind="ExternalInput").ap()
    gami = nc.dram_tensor("gam", [128, 8], F32, kind="ExternalInput").ap()
    beti = nc.dram_tensor("bet", [128, 8], F32, kind="ExternalInput").ap()
    hfili = nc.dram_tensor("hfil", [128, 8, 4], F32, kind="ExternalInput").ap()
    gfili = nc.dram_tensor("gfil", [128, 8, 4], F32, kind="ExternalInput").ap()
    hsyni = nc.dram_tensor("hsyn", [128, 8, 4], F32, kind="ExternalInput").ap()
    gsyni = nc.dram_tensor("gsyn", [128, 8, 4], F32, kind="ExternalInput").ap()
    identi = nc.dram_tensor("ident", [128, 128], F32, kind="ExternalInput").ap()
    yT = nc.dram_tensor("yT", [C, S], F32, kind="ExternalOutput").ap()
    ao_scr = nc.dram_tensor("ao_scr", [NCH, 128, S], BF16, kind="Internal").ap()
    l0_scr = nc.dram_tensor("l0_scr", [NCH, 128, 1026], BF16, kind="Internal").ap()
    h0_scr = nc.dram_tensor("h0_scr", [NCH, 128, 1026], BF16, kind="Internal").ap()
    r1_scr = nc.dram_tensor("r1_scr", [NCH, 128, 1026], BF16, kind="Internal").ap()
    low_scr = nc.dram_tensor("low_scr", [NCH, 128, S], BF16, kind="Internal").ap()
    ws_s1 = nc.dram_tensor("ws_s1", [8, 2, 128, 4096], BF16, kind="Internal").ap()
    ws_s23 = nc.dram_tensor("ws_s23", [2, 32, 128, 4096], BF16, kind="Internal").ap()
    ws_s4 = nc.dram_tensor("ws_s4", [8, 128, 4096], BF16, kind="Internal").ap()

    em = Emitter()
    semnames = (["qC", "qA0", "qA1", "qA2", "qA3", "qA4", "qA5", "qA6", "qA7",
                 "qX0", "qX1", "qO0", "qO1",
                 "qF0", "qF1", "qLs0", "qLs1", "qAO0", "qAO1",
                 "tPE", "tACT", "tDVE", "tPOOL"]
                + [f"qW{i}" for i in range(4)] + [f"qS{i}" for i in range(4)]
                + [f"qT{i}" for i in range(4)])
    for sname in semnames:
        em.new_sem(sname)

    ctxs = []

    def sbuf(name, shape, dt):
        t = nc.sbuf_tensor(name, shape, dt)
        ctxs.append(t)
        return t.__enter__()

    def ps_(name, shape, dt=F32):
        t = nc.psum_tensor(name, shape, dt)
        ctxs.append(t)
        return t.__enter__()

    xn = sbuf("xn", [128, NCH * S], BF16)
    abuf = sbuf("abuf", [128, NOC * TQ], BF16)
    bbuf = sbuf("bbuf", [128, NOC * TQ], BF16)
    wb = sbuf("wb", [128, 4 * 4096], BF16)
    bcR = sbuf("bcR", [128, S], F32)
    bcB = sbuf("bcB", [128, S], F32)
    aren = sbuf("aren", [128, 16448], BF16)
    aob = sbuf("aob", [128, 4096], BF16)
    t1a = sbuf("t1a", [128, 2 * TQ], F32)
    t2a = sbuf("t2a", [128, 2 * TQ], F32)
    xq = sbuf("xq", [128, 2 * TQ], F32)
    aoq = sbuf("aoq", [128, 2 * TQ], BF16)
    outb = sbuf("outb", [128, 2 * TQ], F32)
    scl = sbuf("scl", [128, 152], F32)
    filt = sbuf("filt", [128, 128], F32)
    fcmb = sbuf("fcmb", [128, 80], F32)
    ident = sbuf("identsb", [128, 128], F32)
    identb = sbuf("identbb", [128, 128], BF16)
    onesc = sbuf("onesc", [128, 1], F32)
    cvals = sbuf("cvals", [128, 2], F32)
    dtail = sbuf("dtail", [128, 8], F32)
    dtailb = sbuf("dtailb", [128, 8], BF16)
    onesr = sbuf("onesr", [1, 128], F32)
    kvs = sbuf("kvs", [128, NCH * 65], BF16)

    ar = aren.ap()
    azA = ar[:, 0:4112].bitcast(F32)
    azB = ar[:, 4112:8224].bitcast(F32)
    R3 = 8224
    R4 = 12336
    low2v = ar[:, 0:4096]
    h1v = ar[:, 4112:8224]
    qfmv = ar[:, R3:R3 + 4096]
    vtb = [aob.ap()[:, i * 258: i * 258 + 130] for i in range(4)]
    qtb = [aob.ap()[:, i * 258 + 130: i * 258 + 258] for i in range(4)]
    attv = ar[:, R4:R4 + 4112]
    sb4 = [ar[:, R3 + i * 1028: R3 + i * 1028 + 1028] for i in range(4)]
    lowsl = [ar[:, 0:2048], ar[:, R4:R4 + 2048]]
    fyb = [ar[:, R3 + i * 1024: R3 + (i + 1) * 1024].bitcast(F32) for i in range(2)]
    fab = [ar[:, R3 + 2048 + i * 512: R3 + 2048 + (i + 1) * 512] for i in range(2)]
    aosl = [ar[:, R3 + i * 2048: R3 + (i + 1) * 2048] for i in range(2)]
    aof = aob.ap()[:, 0:4096].bitcast(F32)

    s1v, s2v = scl[:, 0:32], scl[:, 32:64]
    h1sv, h2sv = scl[:, 64:96], scl[:, 96:128]
    w3sv, gamv, betv = scl[:, 128:136], scl[:, 136:144], scl[:, 144:152]

    p = [ps_(f"p{i}", [128, 512], F32) for i in range(6)]
    ptr = [ps_(f"ptr{i}", [128, 512], BF16) for i in range(2)]

    wslot = [wb[:, i * 4096:(i + 1) * 4096] for i in range(4)]

    slot_free = {i: 0 for i in range(4)}
    slot_store = {i: 0 for i in range(4)}

    def _store_slot(slot, dst, load_cnt):
        em.wait("scalar", f"qW{slot}", load_cnt)
        em.dma("scalar", dst, wslot[slot][:, :], f"qS{slot}")
        slot_store[slot] = em.cnt[f"qS{slot}"]
        return slot_store[slot]

    def _wload(slot, src, q, dst_scr):
        em.wait("gpsimd", "tPE", slot_free[slot])
        em.wait("gpsimd", f"qS{slot}", slot_store[slot])
        if q == 0:
            v = em.dma("gpsimd", wslot[slot][:, :], src, f"qW{slot}")
            _store_slot(slot, dst_scr, v)
        else:
            v = em.dma("gpsimd", wslot[slot][:, :], dst_scr, f"qW{slot}")
        return v

    def _wload_direct(slot, src):
        em.wait("gpsimd", "tPE", slot_free[slot])
        em.wait("gpsimd", f"qS{slot}", slot_store[slot])
        return em.dma("gpsimd", wslot[slot][:, :], src, f"qW{slot}")

    def load_s1(g, q):
        ready = []
        for wi in (0, 1):
            slot = (2 * g + wi) % 4
            ready.append((slot, _wload(slot, (w1g, w2g)[wi][g], q, ws_s1[g, wi])))
        return ready

    def load_s23(stage_i, g, q):
        slot = g % 4
        wq = 0
        if q < wq:
            return [(slot, _wload_direct(slot, (h1g, h2g)[stage_i][g]))]
        return [(slot, _wload(slot, (h1g, h2g)[stage_i][g], q - wq, ws_s23[stage_i, g]))]

    def load_s4(cc, q):
        slot = cc % 4
        return [(slot, _wload(slot, w3g[cc], q, ws_s4[cc]))]

    def wait_w(eng, ready):
        for slot, v in ready:
            em.wait(eng, f"qW{slot}", v)

    em.dma("sync", ident[:, :], identi, "qC")
    em.dma("sync", scl[:, 0:32], s1i, "qC")
    em.dma("sync", scl[:, 32:64], s2i, "qC")
    em.dma("sync", scl[:, 64:96], h1si, "qC")
    em.dma("sync", scl[:, 96:128], h2si, "qC")
    em.dma("sync", scl[:, 128:136], w3si, "qC")
    em.dma("sync", scl[:, 136:144], gami, "qC")
    em.dma("sync", scl[:, 144:152], beti, "qC")
    em.dma("sync", filt[:, 0:32].rearrange("p (c k) -> p c k", c=8), hfili, "qC")
    em.dma("sync", filt[:, 32:64].rearrange("p (c k) -> p c k", c=8), gfili, "qC")
    em.dma("sync", filt[:, 64:96].rearrange("p (c k) -> p c k", c=8), hsyni, "qC")
    n_const = em.dma("sync", filt[:, 96:128].rearrange("p (c k) -> p c k", c=8),
                     gsyni, "qC")

    em.op("vector", "memset", onesc[:, :], 1.0)
    em.op("vector", "memset", cvals[:, 0:1], EPS_LN)
    em.op("vector", "memset", cvals[:, 1:2], -0.5)
    em.op("vector", "memset", onesr[:, :], 1.0, inc=("tDVE", 1))

    s1_ready = {0: load_s1(0, 0), 1: load_s1(1, 0)}

    abf = abuf.ap()[:, 0:16384].bitcast(F32)
    bbf = bbuf.ap()[:, 0:16384].bitcast(F32)
    xres = ([abf[:, i * 2048:(i + 1) * 2048] for i in range(4)]
            + [bbf[:, i * 2048:(i + 1) * 2048] for i in range(4)])
    sqscr = [azA[:, 0:2048], azB[:, 0:2048]]
    rowA = aof[0:1, 0:2048]
    rowB = azB[0:1, 0:2048]
    m2r = azA[0:1, 0:2048]
    xld = {}
    for cc in range(0, NCH, 2):
        xld[cc] = em.dma("sync", xres[cc], xT[cc * 128:(cc + 1) * 128, :], f"qA{cc}")
    for cc in range(1, NCH, 2):
        xld[cc] = em.dma("scalar", xres[cc], xT[cc * 128:(cc + 1) * 128, :], f"qA{cc}")

    sq_cnt, add_cnt = {}, {}
    for cc in range(NCH):
        if cc >= 2:
            em.wait("scalar", "tDVE", add_cnt[cc - 2])
        em.wait("scalar", f"qA{cc}", xld[cc])
        sq_cnt[cc] = em.op("scalar", "activation", out=sqscr[cc % 2], in_=xres[cc],
                           func=AF.Square, inc=("tACT", 1))
        em.wait("vector", f"qA{cc}", xld[cc])
        if cc == 0:
            em.op("vector", "tensor_copy", out=bcR[:, :], in_=xres[cc])
        else:
            em.op("vector", TT, out=bcR[:, :], in0=bcR[:, :], in1=xres[cc], op=ALU.add)
        em.wait("vector", "tACT", sq_cnt[cc])
        if cc == 0:
            add_cnt[cc] = em.op("vector", "tensor_copy", out=bcB[:, :], in_=sqscr[0],
                                inc=("tDVE", 1))
        else:
            add_cnt[cc] = em.op("vector", TT, out=bcB[:, :], in0=bcB[:, :],
                                in1=sqscr[cc % 2], op=ALU.add, inc=("tDVE", 1))

    em.wait("scalar", "qC", n_const)
    em.op("scalar", "copy", out=identb[:, :], in_=ident[:, :], inc=("tACT", 1))
    identb_done = em.cnt["tACT"]

    em.wait("vector", "qC", n_const)
    for fi, base in ((0, 64), (1, 96)):
        for cc in range(8):
            fsl = filt[:, base + 4 * cc: base + 4 * cc + 4]
            dst = fcmb[:, fi * 40 + 5 * cc: fi * 40 + 5 * cc + 5]
            em.op("vector", "tensor_copy", out=dst[:, 0:1], in_=fsl[:, 0:1])
            em.op("vector", TT, out=dst[:, 1:2], in0=fsl[:, 1:2], in1=fsl[:, 2:3], op=ALU.add)
            em.op("vector", "tensor_copy", out=dst[:, 2:3], in_=fsl[:, 3:4])
            em.op("vector", TT, out=dst[:, 3:4], in0=fsl[:, 0:1], in1=fsl[:, 1:2], op=ALU.add)
            em.op("vector", TT, out=dst[:, 4:5], in0=fsl[:, 2:3], in1=fsl[:, 3:4],
                  op=ALU.add, inc=("tDVE", 1))

    em.wait("tensor", "tDVE", add_cnt[NCH - 1])
    amm, aev, bmm, bev = {}, {}, {}, {}
    for tb in range(4):
        amm[tb] = em.op("tensor", "matmul", p[tb][0:1, 0:512], onesc[:, :],
                        bcR[:, tb * 512:(tb + 1) * 512], start=True, stop=True,
                        inc=("tPE", 1))
    for tb in range(2):
        bmm[tb] = em.op("tensor", "matmul", p[4 + tb][0:1, 0:512], onesc[:, :],
                        bcB[:, tb * 512:(tb + 1) * 512], start=True, stop=True,
                        inc=("tPE", 1))
    for tb in range(4):
        em.wait("scalar", "tPE", amm[tb])
        aev[tb] = em.op("scalar", "copy", out=rowA[:, tb * 512:(tb + 1) * 512],
                        in_=p[tb][0:1, 0:512], inc=("tACT", 1))
    em.wait("tensor", "tACT", aev[1])
    for tb in range(2, 4):
        bmm[tb] = em.op("tensor", "matmul", p[tb - 2][0:1, 0:512], onesc[:, :],
                        bcB[:, tb * 512:(tb + 1) * 512], start=True, stop=True,
                        inc=("tPE", 1))
    for tb in range(4):
        em.wait("scalar", "tPE", bmm[tb])
        bev[tb] = em.op("scalar", "copy", out=rowB[:, tb * 512:(tb + 1) * 512],
                        in_=(p[4 + tb] if tb < 2 else p[tb - 2])[0:1, 0:512],
                        inc=("tACT", 1))

    em.wait("vector", "tACT", bev[3])
    em.op("vector", TS, out=rowA, in0=rowA, scalar1=-1.0 / C, scalar2=None,
          op0=ALU.mult)
    em.op("vector", TT, out=m2r, in0=rowA, in1=rowA, op=ALU.mult)
    em.op("vector", TS, out=rowB, in0=rowB, scalar1=1.0 / C, scalar2=None,
          op0=ALU.mult)
    vm1 = em.op("vector", TT, out=rowB, in0=rowB, in1=m2r, op=ALU.subtract,
                inc=("tDVE", 1))
    em.wait("scalar", "tDVE", vm1)
    em.op("scalar", "activation", out=rowB, in_=rowB, func=AF.Ln, bias=cvals[0:1, 0:1])
    rr = em.op("scalar", "activation", out=rowB, in_=rowB, func=AF.Exp,
               scale=cvals[0:1, 1:2], inc=("tACT", 1))
    em.wait("vector", "tACT", rr)
    arow = em.op("vector", TT, out=rowA, in0=rowA, in1=rowB, op=ALU.mult,
                 inc=("tDVE", 1))

    em.wait("tensor", "tDVE", arow)
    bbm, bbe, bam, bae = {}, {}, {}, {}
    for tb in range(4):
        bbm[tb] = em.op("tensor", "matmul", p[tb][:, :], onesr[:, :],
                        rowB[:, tb * 512:(tb + 1) * 512], start=True, stop=True,
                        inc=("tPE", 1))
    for tb in range(4):
        em.wait("scalar", "tPE", bbm[tb])
        bbe[tb] = em.op("scalar", "copy", out=bcB[:, tb * 512:(tb + 1) * 512],
                        in_=p[tb][:, :], inc=("tACT", 1))
    for tb in range(4):
        em.wait("tensor", "tACT", bbe[tb])
        bam[tb] = em.op("tensor", "matmul", p[tb][:, :], onesr[:, :],
                        rowA[:, tb * 512:(tb + 1) * 512], start=True, stop=True,
                        inc=("tPE", 1))
    for tb in range(4):
        em.wait("scalar", "tPE", bam[tb])
        bae[tb] = em.op("scalar", "copy", out=bcR[:, tb * 512:(tb + 1) * 512],
                        in_=p[tb][:, :], inc=("tACT", 1))
    bcast_done = bae[3]

    def ln_reduce_bcast(tA, tB, sq_scratch, rowA, rowB, guard_dve):
        em.wait("tensor", "tDVE", guard_dve)
        for src, row in ((tA, rowA), (tB, rowB)):
            for half in range(2):
                em.wait("tensor", "tACT", em.cnt["tACT"])
                for i in range(2):
                    tb = 2 * half + i
                    em.op("tensor", "matmul", p[4 + i][0:1, 0:512], onesc[:, :],
                          src[:, tb * 512:(tb + 1) * 512], start=True, stop=True,
                          inc=("tPE", 1))
                mm = em.cnt["tPE"]
                em.wait("scalar", "tPE", mm)
                for i in range(2):
                    tb = 2 * half + i
                    em.op("scalar", "copy", out=row[0:1, tb * 512:(tb + 1) * 512],
                          in_=p[4 + i][0:1, 0:512], inc=("tACT", 1))
        for src, row in ((tA, rowA), (tB, rowB)):
            for half in range(2):
                em.wait("tensor", "tACT", em.cnt["tACT"])
                for i in range(2):
                    tb = 2 * half + i
                    em.op("tensor", "matmul", p[4 + i][:, :], onesr[:, :],
                          row[0:1, tb * 512:(tb + 1) * 512], start=True, stop=True,
                          inc=("tPE", 1))
                mm = em.cnt["tPE"]
                em.wait("scalar", "tPE", mm)
                for i in range(2):
                    tb = 2 * half + i
                    em.op("scalar", "copy", out=src[:, tb * 512:(tb + 1) * 512],
                          in_=p[4 + i][:, :], inc=("tACT", 1))
        e2 = em.cnt["tACT"]
        em.wait("vector", "tACT", e2)
        em.op("vector", TS, out=tA, in0=tA, scalar1=1.0 / C, scalar2=None, op0=ALU.mult)
        em.op("vector", TS, out=tB, in0=tB, scalar1=1.0 / C, scalar2=None, op0=ALU.mult)
        em.op("vector", TT, out=sq_scratch, in0=tA, in1=tA, op=ALU.mult)
        vd = em.op("vector", TT, out=tB, in0=tB, in1=sq_scratch, op=ALU.subtract,
                   inc=("tDVE", 1))
        em.wait("scalar", "tDVE", vd)
        em.op("scalar", "activation", out=tB, in_=tB, func=AF.Ln, bias=cvals[:, 0:1])
        rr = em.op("scalar", "activation", out=tB, in_=tB, func=AF.Exp,
                   scale=cvals[:, 1:2], inc=("tACT", 1))
        em.wait("vector", "tACT", rr)
        em.op("vector", TT, out=tA, in0=tA, in1=tB, op=ALU.mult)
        return em.op("vector", TS, out=tA, in0=tA, scalar1=-1.0, scalar2=None,
                     op0=ALU.mult, inc=("tDVE", 1))

    ckpt = [0]

    def ok():
        ckpt[0] += 1
        return ckpt[0] < stop_after
    go = ok()

    xn_cast, st_cnt = {}, {}
    em.wait("vector", "tACT", bcast_done)
    for cc in range(NCH):
        em.op("vector", TT, out=xres[cc], in0=xres[cc], in1=bcB[:, :], op=ALU.mult)
        zr = em.op("vector", TT, out=xres[cc], in0=xres[cc], in1=bcR[:, :],
                   op=ALU.add, inc=("tDVE", 1))
        em.wait("scalar", "tDVE", zr)
        xn_cast[cc] = em.op("scalar", "copy", out=xn[:, cc * S:(cc + 1) * S],
                            in_=xres[cc], inc=("tACT", 1))
    xn_all = max(xn_cast.values())

    for cc in range(NCH):
        xc = xn[:, cc * S:(cc + 1) * S]
        em.wait("vector", "tACT", xn_cast[cc])
        for kind, fbase, scr in (("l", 0, l0_scr), ("h", 32, h0_scr)):
            tsl = (cc % 2) * 2 + (0 if kind == "l" else 1)
            slot = sb4[tsl]
            if cc >= 2:
                em.wait("vector", f"qT{tsl}", st_cnt[(kind, cc - 2)])
            def fk(k, fb=fbase, c=cc):
                return filt[:, fb + 4 * c + k: fb + 4 * c + k + 1]
            em.op("vector", TS, out=aof[:, 0:1024], in0=xc[:, 0:2048:2], scalar1=fk(1),
                  scalar2=None, op0=ALU.mult)
            em.op("vector", STT, out=aof[:, 0:1024], in0=xc[:, 1:2048:2],
                  scalar=fk(2), in1=aof[:, 0:1024], op0=ALU.mult, op1=ALU.add)
            em.op("vector", STT, out=aof[:, 1:1024], in0=xc[:, 1:2047:2],
                  scalar=fk(0), in1=aof[:, 1:1024], op0=ALU.mult, op1=ALU.add)
            em.op("vector", STT, out=slot[:, 1:1024], in0=xc[:, 2:2048:2],
                  scalar=fk(3), in1=aof[:, 0:1023], op0=ALU.mult, op1=ALU.add)
            em.op("vector", "tensor_copy", out=slot[:, 1024:1025],
                  in_=aof[:, 1023:1024], inc=("tDVE", 1))
            if cc < 2:
                em.op("vector", "memset", slot[:, 0:1], 0.0)
                em.op("vector", "memset", slot[:, 1025:1028], 0.0, inc=("tDVE", 1))
            dve_last = em.cnt["tDVE"]
            em.wait("sync", "tDVE", dve_last)
            st_cnt[(kind, cc)] = em.dma("sync", scr[cc], slot[:, 0:1026], f"qT{tsl}")

    c_l1_done = {}
    for cc in range(NCH):
        tsl = cc % 2
        slot = sb4[tsl]
        if cc >= 2:
            em.wait("sync", "tDVE", c_l1_done[cc - 2])
        em.wait("sync", f"qT{(cc % 2) * 2}", st_cnt[("l", cc)])
        em.wait("sync", f"qT{tsl}", em.cnt[f"qT{tsl}"])
        li = em.dma("sync", slot[:, 0:1026], l0_scr[cc], f"qT{tsl}")
        em.wait("vector", f"qT{tsl}", li)
        for dsl, fbase in ((low2v[:, cc * 512: cc * 512 + 512], 0),
                           (h1v[:, cc * 514 + 1: cc * 514 + 513], 32)):
            def fk(k, fb=fbase, c=cc):
                return filt[:, fb + 4 * c + k: fb + 4 * c + k + 1]
            a5 = aof[:, 0:512]
            em.op("vector", TS, out=a5, in0=slot[:, 0:1024:2], scalar1=fk(0),
                  scalar2=None, op0=ALU.mult)
            for k in (1, 2):
                em.op("vector", STT, out=a5, in0=slot[:, k:k + 1024:2], scalar=fk(k),
                      in1=a5, op0=ALU.mult, op1=ALU.add)
            em.op("vector", STT, out=dsl, in0=slot[:, 3:1026:2], scalar=fk(3),
                  in1=a5, op0=ALU.mult, op1=ALU.add, inc=("tDVE", 1))
        if cc == 0:
            em.op("vector", "memset",
                  h1v.rearrange("p (c n) -> p c n", c=8)[:, :, 0:1], 0.0)
            em.op("vector", "memset",
                  h1v.rearrange("p (c n) -> p c n", c=8)[:, :, 513:514], 0.0,
                  inc=("tDVE", 1))
        c_l1_done[cc] = em.cnt["tDVE"]

    pfree = {i: (0, 0) for i in range(4)}
    s23_ready, s4_ready = {}, {}
    out_st, asm_done = {}, {}
    pending_store = [None]
    ao_gate = [(0, 0)]
    ao_gate_emitted = [False]

    def stage1(q):
        qc0, qc1 = q * TQ, (q + 1) * TQ
        a_cnt = {}
        for oc in range(NOC):
            g = oc // 4
            gi = q * 8 + g
            if oc % 4 == 0:
                if gi not in s1_ready:
                    s1_ready[gi] = load_s1(g, q)
                if g + 1 < 8 and gi + 1 not in s1_ready:
                    s1_ready[gi + 1] = load_s1(g + 1, q)
            pu1, pu2 = p[2 * (oc % 2)], p[2 * (oc % 2) + 1]
            wait_w("tensor", s1_ready[gi])
            fa1, _ = pfree[2 * (oc % 2)]
            em.wait("tensor", "tACT", fa1)
            fa2, _ = pfree[2 * (oc % 2) + 1]
            em.wait("tensor", "tACT", fa2)
            slotW1, slotW2 = (2 * g) % 4, (2 * g + 1) % 4
            ocl = oc % 4
            for ic in range(NCH):
                if oc == 0 and q == 0:
                    em.wait("tensor", "tACT", xn_cast[ic])
                em.op("tensor", "matmul", pu1[:, :],
                      wslot[slotW1][:, ic * 512 + ocl * 128: ic * 512 + ocl * 128 + 128],
                      xn[:, ic * S + qc0: ic * S + qc1],
                      start=(ic == 0), stop=(ic == NCH - 1), inc=("tPE", 1))
            mm1 = em.cnt["tPE"]
            for ic in range(NCH):
                em.op("tensor", "matmul", pu2[:, :],
                      wslot[slotW2][:, ic * 512 + ocl * 128: ic * 512 + ocl * 128 + 128],
                      xn[:, ic * S + qc0: ic * S + qc1],
                      start=(ic == 0), stop=(ic == NCH - 1), inc=("tPE", 1))
            mm2 = em.cnt["tPE"]
            if ocl == 3:
                slot_free[slotW1] = mm1
                slot_free[slotW2] = mm2
            em.wait("scalar", "tPE", mm1)
            if oc >= 2:
                em.wait("scalar", "tPOOL", a_cnt[oc - 2])
            sl = (oc % 2) * TQ
            s1c = em.op("scalar", "activation", out=t1a[:, sl:sl + TQ], in_=pu1[:, :],
                        func=AF.Silu, scale=s1v[:, oc:oc + 1], inc=("tACT", 1))
            em.wait("scalar", "tPE", mm2)
            ev2 = em.op("scalar", "activation", out=t2a[:, sl:sl + TQ], in_=pu2[:, :],
                        func=AF.Identity, scale=s2v[:, oc:oc + 1], inc=("tACT", 1))
            em.wait("gpsimd", "tACT", ev2)
            a_cnt[oc] = em.op("gpsimd", TT, out=abuf.ap()[:, oc * TQ:(oc + 1) * TQ],
                              in0=t1a[:, sl:sl + TQ], in1=t2a[:, sl:sl + TQ],
                              op=ALU.mult, inc=("tPOOL", 1))
            pfree[2 * (oc % 2)] = (s1c, a_cnt[oc])
            pfree[2 * (oc % 2) + 1] = (ev2, a_cnt[oc])
        return em.cnt["tPOOL"]

    def stage23(q, stage_i, scale_v, inv, outv, ready_pool, ready_act, interleave=None):
        kbase = 1000 + q * 100 + stage_i * 50
        b_cnt = {}
        for jc in range(NOC):
            if interleave is not None:
                interleave(jc)
            gg = kbase + jc
            if gg not in s23_ready:
                s23_ready[gg] = load_s23(stage_i, jc, q)
            slot = jc % 4
            pb = p[2 + jc % 2]
            wait_w("tensor", s23_ready[gg])
            if jc == 0:
                if ready_pool:
                    em.wait("tensor", "tPOOL", ready_pool)
                if ready_act:
                    em.wait("tensor", "tACT", ready_act)
            fa, _ = pfree[2 + jc % 2]
            em.wait("tensor", "tACT", fa)
            for oc in range(NOC):
                em.op("tensor", "matmul", pb[:, :],
                      wslot[slot][:, oc * 128: oc * 128 + 128],
                      inv(oc), start=(oc == 0), stop=(oc == NOC - 1), inc=("tPE", 1))
            mm = em.cnt["tPE"]
            slot_free[slot] = mm
            em.wait("scalar", "tPE", mm)
            b_cnt[jc] = em.op("scalar", "activation", out=outv(jc), in_=pb[:, :],
                              func=AF.Gelu, scale=scale_v[:, jc:jc + 1], inc=("tACT", 1))
            pfree[2 + jc % 2] = (b_cnt[jc], 0)
        return max(b_cnt.values())

    def stage4(q, c_all, with_ao, interleave=None):
        qc0, qc1 = q * TQ, (q + 1) * TQ
        for cc in range(NCH):
            if interleave is not None:
                interleave(cc)
            gi = 10000 + q * 8 + cc
            if gi not in s4_ready:
                s4_ready[gi] = load_s4(cc, q)
            if cc + 1 < NCH and gi + 1 not in s4_ready:
                s4_ready[gi + 1] = load_s4(cc + 1, q)
            ki = (q, cc)
            slot = cc % 2
            wsl = cc % 4
            prev = (q, cc - 2) if cc >= 2 else ((q - 1, cc + 6) if q > 0 else None)
            if prev in asm_done:
                em.wait("sync", "tPOOL", asm_done[prev])
            if with_ao and not ao_gate_emitted[0]:
                em.wait("sync", "qAO0", ao_gate[0][0])
                em.wait("sync", "qAO1", ao_gate[0][1])
                ao_gate_emitted[0] = True
            em.dma("sync", xq[:, slot * TQ: slot * TQ + TQ],
                   xT[cc * 128:(cc + 1) * 128, qc0:qc1], f"qX{slot}")
            if with_ao:
                em.dma("sync", aoq[:, slot * TQ: slot * TQ + TQ],
                       ao_scr[cc][:, qc0:qc1], f"qX{slot}")
            fl = em.cnt[f"qX{slot}"]
            if pending_store[0] is not None:
                pk, pslot, pcc, pq0, pq1 = pending_store[0]
                em.wait("sync", "tPOOL", asm_done[pk])
                out_st[pk] = em.dma("sync", yT[pcc * 128:(pcc + 1) * 128, pq0:pq1],
                                    outb[:, pslot * TQ: pslot * TQ + TQ], f"qO{pslot}")
                pending_store[0] = None
            py = p[cc % 4]
            wait_w("tensor", s4_ready[gi])
            if cc == 0:
                em.wait("tensor", "tACT", c_all)
            fa, fp = pfree[cc % 4]
            em.wait("tensor", "tACT", fa)
            em.wait("tensor", "tPOOL", fp)
            for lc in range(NOC):
                em.op("tensor", "matmul", py[:, :],
                      wslot[wsl][:, lc * 128:(lc + 1) * 128],
                      abuf.ap()[:, lc * TQ:(lc + 1) * TQ],
                      start=(lc == 0), stop=(lc == NOC - 1), inc=("tPE", 1))
            mm = em.cnt["tPE"]
            slot_free[wsl] = mm
            em.wait("scalar", "tPE", mm)
            if prev in out_st:
                em.wait("scalar", f"qO{slot}", out_st[prev])
            ev = em.op("scalar", "activation", out=outb[:, slot * TQ: slot * TQ + TQ],
                       in_=py[:, :], func=AF.Identity, scale=w3sv[:, cc:cc + 1],
                       inc=("tACT", 1))
            em.wait("gpsimd", "tACT", ev)
            em.wait("gpsimd", f"qX{slot}", fl)
            em.op("gpsimd", TT, out=outb[:, slot * TQ: slot * TQ + TQ],
                  in0=outb[:, slot * TQ: slot * TQ + TQ],
                  in1=xq[:, slot * TQ: slot * TQ + TQ], op=ALU.add,
                  inc=None if with_ao else ("tPOOL", 1))
            if with_ao:
                em.op("gpsimd", TT, out=outb[:, slot * TQ: slot * TQ + TQ],
                      in0=outb[:, slot * TQ: slot * TQ + TQ],
                      in1=aoq[:, slot * TQ: slot * TQ + TQ], op=ALU.add,
                      inc=("tPOOL", 1))
            asm_done[ki] = em.cnt["tPOOL"]
            pfree[cc % 4] = (ev, asm_done[ki])
            pending_store[0] = (ki, slot, cc, qc0, qc1)
        pk, pslot, pcc, pq0, pq1 = pending_store[0]
        em.wait("sync", "tPOOL", asm_done[pk])
        out_st[pk] = em.dma("sync", yT[pcc * 128:(pcc + 1) * 128, pq0:pq1],
                            outb[:, pslot * TQ: pslot * TQ + TQ], f"qO{pslot}")
        pending_store[0] = None

    a_all0 = stage1(0)
    b_all0 = stage23(0, 0, h1sv, lambda oc: abuf.ap()[:, oc * TQ:(oc + 1) * TQ],
                     lambda jc: bbuf.ap()[:, jc * TQ:(jc + 1) * TQ], a_all0, None)

    for cc in range(NCH):
        l2c = low2v[:, cc * 512: cc * 512 + 512]
        m0 = em.op("vector", TS, out=aof[:, 0:512], in0=l2c, scalar1=0.0, scalar2=None,
                   op0=ALU.min, inc=("tDVE", 1))
        em.wait("scalar", "tDVE", m0)
        e0 = em.op("scalar", "activation", out=aof[:, 512:1024], in_=aof[:, 0:512],
                   func=AF.Exp, inc=("tACT", 1))
        em.op("vector", TS, out=aof[:, 1024:1536], in0=l2c, scalar1=0.0, scalar2=None,
              op0=ALU.max)
        em.wait("vector", "tACT", e0)
        em.op("vector", TT, out=qfmv[:, cc * 512: cc * 512 + 512], in0=aof[:, 512:1024],
              in1=aof[:, 1024:1536], op=ALU.add, inc=("tDVE", 1))
    for i in range(4):
        em.op("vector", "memset", vtb[i][:, 64:65], 1.0)
        em.op("vector", "memset", vtb[i][:, 129:130], 1.0, inc=("tDVE", 1))

    kv_ev, tr_done = {}, {}
    kvst = {"prev_kvmm": 0, "prev_trev": 0, "first": True}
    nst = {"att_tt": 0, "numev": 0, "dens": 0}
    dr_cnt = {}
    elu_all = em.cnt["tDVE"]

    def kv_T(cc):
        if kvst["first"]:
            em.wait("tensor", "tACT", identb_done)
            em.wait("tensor", "tDVE", elu_all)
            kvst["first"] = False
        em.wait("tensor", "tDVE", kvst["prev_trev"])
        em.wait("vector", "tPE", kvst["prev_kvmm"])
        mmt = {}
        for j in range(8):
            tn, which = j // 2, j % 2
            srcv = (low2v, qfmv)[which]
            psl = ptr[j // 4][:, (j % 4) * 128:(j % 4) * 128 + 128]
            mmt[j] = em.op("tensor", "transpose", psl,
                           srcv[:, cc * 512 + tn * 128: cc * 512 + tn * 128 + 128],
                           identb[:, :], inc=("tPE", 1))
        for j in range(8):
            tn, which = j // 2, j % 2
            psl = ptr[j // 4][:, (j % 4) * 128:(j % 4) * 128 + 128]
            if j % 4 == 0:
                em.wait("vector", "tPE", mmt[j + 3])
            if which == 0:
                em.op("vector", "tensor_copy", out=vtb[tn][:, 0:64], in_=psl[:, 0:64])
                em.op("vector", "tensor_copy", out=vtb[tn][:, 65:129],
                      in_=psl[:, 64:128], inc=("tDVE", 1))
            else:
                em.op("vector", "tensor_copy", out=qtb[tn][:, 0:128],
                      in_=psl[:, :], inc=("tDVE", 1))
        tr_done[cc] = em.cnt["tDVE"]
        kvst["prev_trev"] = tr_done[cc]

    def kv_M(cc):
        em.wait("tensor", "tDVE", tr_done[cc])
        if cc > 0:
            em.wait("tensor", "tACT", kv_ev[2 * cc - 1])
        for tn in range(4):
            em.op("tensor", "matmul", p[0][0:64, 0:65], qtb[tn][:, 0:64],
                  vtb[tn][:, 0:65], start=(tn == 0), stop=(tn == 3), inc=("tPE", 1))
        for tn in range(4):
            em.op("tensor", "matmul", p[1][0:64, 0:65], qtb[tn][:, 64:128],
                  vtb[tn][:, 65:130], start=(tn == 0), stop=(tn == 3), inc=("tPE", 1))
        kvst["prev_kvmm"] = em.cnt["tPE"]
        em.wait("scalar", "tPE", kvst["prev_kvmm"])
        kv_ev[2 * cc] = em.op("scalar", "copy", out=kvs[0:64, 65 * cc: 65 * cc + 65],
                              in_=p[0][0:64, 0:65], inc=("tACT", 1))
        kv_ev[2 * cc + 1] = em.op("scalar", "copy",
                                  out=kvs[64:128, 65 * cc: 65 * cc + 65],
                                  in_=p[1][0:64, 0:65], inc=("tACT", 1))

    def num_half(wv, half):
        for i in (2 * half, 2 * half + 1):
            h = 4 * wv + i
            pi = 4 + i % 2
            em.wait("tensor", "tACT", kv_ev[h])
            if wv > 0 or half > 0:
                em.wait("tensor", "tDVE",
                        nst["att_tt"] if half == 0 else nst["dens"])
                if half > 0:
                    em.wait("tensor", "tACT", nst["numev"])
            cc = h // 2
            rbase = 64 * (h % 2)
            mm = em.op("tensor", "matmul", p[pi][0:65, 0:512],
                       kvs[rbase:rbase + 64, 65 * cc: 65 * cc + 65],
                       qfmv[rbase:rbase + 64, cc * 512: cc * 512 + 512],
                       start=True, stop=True, inc=("tPE", 1))
            em.wait("scalar", "tPE", mm)
            nst["numev"] = em.op("scalar", "copy",
                                 out=attv[rbase:rbase + 64, cc * 514 + 1: cc * 514 + 513],
                                 in_=p[pi][0:64, 0:512], inc=("tACT", 1))
            em.wait("vector", "tPE", mm)
            nst["dens"] = em.op("vector", TS, out=bcB[0:1, i * 512:(i + 1) * 512],
                                in0=p[pi][64:65, 0:512], scalar1=EPS_ATTN,
                                scalar2=None, op0=ALU.add, inc=("tDVE", 1))
        if half == 1:
            dr_cnt[wv] = em.op("vector", "reciprocal", out=bcR[0:1, 0:2048],
                               in_=bcB[0:1, 0:2048], inc=("tDVE", 1))

    def bc_half(wv, half):
        if half == 0:
            em.wait("tensor", "tDVE", dr_cnt[wv])
            em.wait("tensor", "tACT", nst["numev"])
        else:
            em.wait("tensor", "tDVE", nst["att_tt"])
        bcs = {}
        for i in (2 * half, 2 * half + 1):
            bcs[i] = em.op("tensor", "matmul", p[4 + i % 2][0:64, 0:512],
                           onesr[0:1, 0:64], bcR[0:1, i * 512:(i + 1) * 512],
                           start=True, stop=True, inc=("tPE", 1))
        for i in (2 * half, 2 * half + 1):
            h = 4 * wv + i
            cc = h // 2
            rbase = 64 * (h % 2)
            em.wait("vector", "tPE", bcs[i])
            nst["att_tt"] = em.op(
                "vector", TT,
                out=attv[rbase:rbase + 64, cc * 514 + 1: cc * 514 + 513],
                in0=attv[rbase:rbase + 64, cc * 514 + 1: cc * 514 + 513],
                in1=p[4 + i % 2][0:64, 0:512], op=ALU.mult, inc=("tDVE", 1))

    s3_tasks = {}
    for k in range(8):
        s3_tasks[2 + 4 * k] = (lambda c=k: kv_T(c))
        if k < 7:
            s3_tasks[4 + 4 * k] = (lambda c=k: kv_M(c))
    for sl, w, hf in ((5, 0, 0), (9, 0, 1), (11, 0, 2), (13, 0, 3),
                      (15, 1, 0), (17, 1, 1), (19, 1, 2), (21, 1, 3),
                      (23, 2, 0), (25, 2, 1), (27, 2, 2), (29, 2, 3),
                      (31, 3, 0)):
        fn = (num_half if hf < 2 else bc_half)
        s3_tasks[sl] = (lambda f=fn, a=w, b=hf: f(a, b % 2))

    def s3_inter(jc):
        if jc in s3_tasks:
            s3_tasks[jc]()

    def s4_inter(cc):
        if cc == 1:
            num_half(3, 1)
        elif cc == 2:
            bc_half(3, 0)
        elif cc == 3:
            bc_half(3, 1)

    c_all0 = stage23(0, 1, h2sv, lambda oc: bbuf.ap()[:, oc * TQ:(oc + 1) * TQ],
                     lambda jc: abuf.ap()[:, jc * TQ:(jc + 1) * TQ], None, b_all0,
                     interleave=s3_inter)
    kv_M(7)
    pfree[0] = (kv_ev[14], pfree[0][1])
    pfree[1] = (kv_ev[15], pfree[1][1])
    stage4(0, c_all0, with_ao=False, interleave=s4_inter)
    em.op("vector", "memset", attv.rearrange("p (c n) -> p c n", c=8)[:, :, 0:1], 0.0)
    em.op("vector", "memset", attv.rearrange("p (c n) -> p c n", c=8)[:, :, 513:514],
          0.0, inc=("tDVE", 1))

    def recon(cc, sa_ap, sb_ap, dst_ap, dst_off, nh):
        def sa(d):
            return sa_ap[:, d: d + nh]
        def sbv(d):
            return sb_ap[:, d: d + nh]
        def hcm(j, c=cc):
            return fcmb[:, 5 * c + j: 5 * c + j + 1]
        def gcm(j, c=cc):
            return fcmb[:, 40 + 5 * c + j: 40 + 5 * c + j + 1]
        a5 = aof[:, 0:nh]
        em.op("vector", TS, out=a5, in0=sa(0), scalar1=hcm(0), scalar2=None, op0=ALU.mult)
        em.op("vector", STT, out=a5, in0=sa(1), scalar=hcm(1), in1=a5, op0=ALU.mult, op1=ALU.add)
        em.op("vector", STT, out=a5, in0=sa(2), scalar=hcm(2), in1=a5, op0=ALU.mult, op1=ALU.add)
        em.op("vector", STT, out=a5, in0=sbv(0), scalar=gcm(0), in1=a5, op0=ALU.mult, op1=ALU.add)
        em.op("vector", STT, out=a5, in0=sbv(1), scalar=gcm(1), in1=a5, op0=ALU.mult, op1=ALU.add)
        em.op("vector", STT, out=dst_ap[:, dst_off: dst_off + 2 * nh: 2], in0=sbv(2),
              scalar=gcm(2), in1=a5, op0=ALU.mult, op1=ALU.add, inc=("tDVE", 1))
        em.op("vector", TS, out=a5, in0=sa(1), scalar1=hcm(3), scalar2=None, op0=ALU.mult)
        em.op("vector", STT, out=a5, in0=sa(2), scalar=hcm(4), in1=a5, op0=ALU.mult, op1=ALU.add)
        em.op("vector", STT, out=a5, in0=sbv(1), scalar=gcm(3), in1=a5, op0=ALU.mult, op1=ALU.add)
        return em.op("vector", STT, out=dst_ap[:, dst_off + 1: dst_off + 2 * nh: 2],
                     in0=sbv(2), scalar=gcm(4), in1=a5, op0=ALU.mult, op1=ALU.add,
                     inc=("tDVE", 1))

    r1_st = {}
    for cc in range(NCH):
        tsl = cc % 2
        slot = sb4[tsl]
        if cc >= 2:
            em.wait("vector", f"qT{tsl}", r1_st[cc - 2])
        d = recon(cc, attv[:, cc * 514: cc * 514 + 514],
                  h1v[:, cc * 514: cc * 514 + 514], slot, 1, 512)
        if cc < 2:
            em.op("vector", "memset", slot[:, 0:1], 0.0)
            em.op("vector", "memset", slot[:, 1025:1028], 0.0, inc=("tDVE", 1))
            d = em.cnt["tDVE"]
        em.wait("sync", "tDVE", d)
        r1_st[cc] = em.dma("sync", r1_scr[cc], slot[:, 0:1026], f"qT{tsl}")

    low_st, f_done = {}, {}
    for cc in range(NCH):
        rtsl = cc % 2
        htsl = 2 + cc % 2
        rsl = sb4[rtsl]
        hsl = sb4[htsl]
        osl = lowsl[cc % 2]
        if cc >= 2:
            em.wait("sync", "tDVE", f_done[cc - 2])
        em.wait("sync", f"qT{rtsl}", r1_st[cc])
        em.wait("sync", f"qT{htsl}", em.cnt[f"qT{htsl}"])
        rl = em.dma("sync", rsl[:, 0:1026], r1_scr[cc], f"qT{rtsl}")
        hl = em.dma("sync", hsl[:, 0:1026], h0_scr[cc], f"qT{htsl}")
        if cc >= 2:
            em.wait("vector", f"qLs{cc % 2}", low_st[cc - 2])
        em.wait("vector", f"qT{rtsl}", rl)
        em.wait("vector", f"qT{htsl}", hl)
        f_done[cc] = recon(cc, rsl[:, 0:1026], hsl[:, 0:1026], osl, 0, 1024)
        em.op("vector", TT, out=azB[:, 0:2048], in0=xn[:, cc * S:(cc + 1) * S],
              in1=osl[:, :], op=ALU.add)
        em.op("vector", TT, out=aof[:, 0:2048], in0=azB[:, 0:2048], in1=azB[:, 0:2048],
              op=ALU.mult)
        if cc == 0:
            em.op("vector", "tensor_copy", out=bcR[:, :], in_=azB[:, 0:2048])
            em.op("vector", "tensor_copy", out=bcB[:, :], in_=aof[:, 0:2048],
                  inc=("tDVE", 1))
        else:
            em.op("vector", TT, out=bcR[:, :], in0=bcR[:, :], in1=azB[:, 0:2048],
                  op=ALU.add)
            em.op("vector", TT, out=bcB[:, :], in0=bcB[:, :], in1=aof[:, 0:2048],
                  op=ALU.add, inc=("tDVE", 1))
        f_done[cc] = em.cnt["tDVE"]
        em.wait("sync", "tDVE", f_done[cc])
        low_st[cc] = em.dma("sync", low_scr[cc], osl[:, :], f"qLs{cc % 2}")

    ln2_acc = em.cnt["tDVE"]

    a_all1 = stage1(1)
    b_all1 = stage23(1, 0, h1sv, lambda oc: abuf.ap()[:, oc * TQ:(oc + 1) * TQ],
                     lambda jc: bbuf.ap()[:, jc * TQ:(jc + 1) * TQ], a_all1, None)

    c_all1 = stage23(1, 1, h2sv, lambda oc: bbuf.ap()[:, oc * TQ:(oc + 1) * TQ],
                     lambda jc: abuf.ap()[:, jc * TQ:(jc + 1) * TQ], None, b_all1)
    stage4(1, c_all1, with_ao=False)
    ln2_done = ln_reduce_bcast(bcR[:, :], bcB[:, :], aof[:, 0:2048],
                               azB[0:1, 0:2048], aof[0:1, 0:2048], ln2_acc)

    ao_st, ao_dve = {}, {}
    for cc in range(NCH):
        isl = lowsl[cc % 2]
        osl = aosl[cc % 2]
        if cc < 2:
            em.wait("sync", "tDVE", ln2_acc)
        else:
            em.wait("sync", "tDVE", ao_dve[cc - 2])
        ll = em.dma("sync", isl[:, :], low_scr[cc], f"qLs{cc % 2}")
        em.wait("vector", f"qLs{cc % 2}", ll)
        if cc >= 2:
            em.wait("vector", f"qAO{cc % 2}", ao_st[cc - 2])
        em.op("vector", TT, out=azB[:, 0:2048], in0=xn[:, cc * S:(cc + 1) * S],
              in1=isl[:, :], op=ALU.add)
        em.op("vector", TT, out=azB[:, 0:2048], in0=azB[:, 0:2048], in1=bcB[:, :],
              op=ALU.mult)
        em.op("vector", TT, out=azB[:, 0:2048], in0=azB[:, 0:2048], in1=bcR[:, :],
              op=ALU.add)
        ao_dve[cc] = em.op("vector", TS, out=osl, in0=azB[:, 0:2048],
                           scalar1=gamv[:, cc:cc + 1], scalar2=betv[:, cc:cc + 1],
                           op0=ALU.mult, op1=ALU.add, inc=("tDVE", 1))
        em.wait("sync", "tDVE", ao_dve[cc])
        ao_st[cc] = em.dma("sync", ao_scr[cc], osl, f"qAO{cc % 2}")
    ao_gate[0] = (em.cnt["qAO0"], em.cnt["qAO1"])

    fx_dve = {}
    fi = 0
    for fq in (0, 1):
        for cc in range(NCH):
            if fi >= 2:
                em.wait("sync", "tDVE", fx_dve[fi - 2])
            em.wait("sync", f"qO{cc % 2}", out_st[(fq, cc)])
            if fi == 0:
                em.wait("sync", "qAO0", ao_gate[0][0])
                em.wait("sync", "qAO1", ao_gate[0][1])
            em.dma("sync", fyb[fi % 2][:, 0:512],
                   yT[cc * 128:(cc + 1) * 128, fq * TQ:(fq + 1) * TQ], f"qF{fi % 2}")
            fl = em.dma("sync", fab[fi % 2], ao_scr[cc][:, fq * TQ:(fq + 1) * TQ],
                        f"qF{fi % 2}")
            em.wait("vector", f"qF{fi % 2}", fl)
            fx_dve[fi] = em.op("vector", TT, out=fyb[fi % 2][:, 0:512],
                               in0=fyb[fi % 2][:, 0:512], in1=fab[fi % 2], op=ALU.add,
                               inc=("tDVE", 1))
            em.wait("sync", "tDVE", fx_dve[fi])
            em.dma("sync", yT[cc * 128:(cc + 1) * 128, fq * TQ:(fq + 1) * TQ],
                   fyb[fi % 2][:, 0:512], f"qF{fi % 2}")
            fi += 1
    for q in range(2, nq):
        a_all = stage1(q)
        b_all = stage23(q, 0, h1sv, lambda oc: abuf.ap()[:, oc * TQ:(oc + 1) * TQ],
                        lambda jc: bbuf.ap()[:, jc * TQ:(jc + 1) * TQ], a_all, None)
        c_all = stage23(q, 1, h2sv, lambda oc: bbuf.ap()[:, oc * TQ:(oc + 1) * TQ],
                        lambda jc: abuf.ap()[:, jc * TQ:(jc + 1) * TQ], None, b_all)
        stage4(q, c_all, with_ao=True)

    em.wait("sync", "qO0", em.cnt["qO0"])
    em.wait("sync", "qO1", em.cnt["qO1"])
    em.wait("sync", "qF0", em.cnt["qF0"])
    em.wait("sync", "qF1", em.cnt["qF1"])
    em.resolve_dwaits()

    if stop_after < 99:
        total = sum(len(em.q[e]) for e in ENGINES)
        budget = int(total * stop_after / 100.0)
        pos = {e: 0 for e in ENGINES}
        cnt = {k: 0 for k in em.cnt}
        done = 0
        progress = True
        while progress and done < budget:
            progress = False
            for e in ENGINES:
                while pos[e] < len(em.q[e]) and done < budget:
                    it = em.q[e][pos[e]]
                    if it[0] == "wait":
                        if cnt[it[1]] < it[2]:
                            break
                    else:
                        if it[4] is not None:
                            cnt[it[4][0]] += it[4][1]
                    pos[e] += 1
                    done += 1
                    progress = True
        for e in ENGINES:
            em.q[e] = em.q[e][:pos[e]]

    with contextlib.ExitStack() as stack:
        sems = {name: stack.enter_context(nc.semaphore(name)) for name in em.cnt}
        blk = stack.enter_context(nc.Block())

        def mk(engname):
            def f(eng):
                em.replay(eng, engname, sems)
            return f
        blk.sync(mk("sync"))
        blk.tensor(mk("tensor"))
        blk.vector(mk("vector"))
        blk.scalar(mk("scalar"))
        blk.gpsimd(mk("gpsimd"))
    for t in reversed(ctxs):
        t.__exit__(None, None, None)
    return nc, em



def _prep_shared(inputs):
    d = {}
    w1t = np.ascontiguousarray(inputs["L0_ffn_w1_w"].T)
    w2t = np.ascontiguousarray(inputs["L0_ffn_w2_w"].T)
    h1t = np.ascontiguousarray(inputs["L0_ffn_h1_w"].T)
    h2t = np.ascontiguousarray(inputs["L0_ffn_h2_w"].T)
    w3t = np.ascontiguousarray(inputs["L0_ffn_w3_w"].T)
    d["w1g"] = np.ascontiguousarray(
        w1t.reshape(8, 128, 8, 512).transpose(2, 1, 0, 3).reshape(8, 128, 4096))
    d["w2g"] = np.ascontiguousarray(
        w2t.reshape(8, 128, 8, 512).transpose(2, 1, 0, 3).reshape(8, 128, 4096))
    d["h1g"] = np.ascontiguousarray(
        h1t.reshape(32, 128, 32, 128).transpose(2, 1, 0, 3).reshape(32, 128, 4096))
    d["h2g"] = np.ascontiguousarray(
        h2t.reshape(32, 128, 32, 128).transpose(2, 1, 0, 3).reshape(32, 128, 4096))
    d["w3g"] = np.ascontiguousarray(
        w3t.reshape(32, 128, 8, 128).transpose(2, 1, 0, 3).reshape(8, 128, 4096))
    d["s1"] = np.ascontiguousarray(inputs["L0_ffn_w1_s"].reshape(32, 128).T)
    d["s2"] = np.ascontiguousarray(inputs["L0_ffn_w2_s"].reshape(32, 128).T)
    d["h1s"] = np.ascontiguousarray(inputs["L0_ffn_h1_s"].reshape(32, 128).T)
    d["h2s"] = np.ascontiguousarray(inputs["L0_ffn_h2_s"].reshape(32, 128).T)
    d["w3s"] = np.ascontiguousarray(inputs["L0_ffn_w3_s"].reshape(8, 128).T)
    d["gam"] = np.ascontiguousarray(inputs["L0_attn_norm_gamma"].reshape(8, 128).T)
    d["bet"] = np.ascontiguousarray(inputs["L0_attn_norm_beta"].reshape(8, 128).T)
    for nm, key in (("hfil", "L0_attn_h_filter"), ("gfil", "L0_attn_g_filter"),
                    ("hsyn", "L0_attn_h_synth"), ("gsyn", "L0_attn_g_synth")):
        f = inputs[key]
        d[nm] = np.ascontiguousarray(f.T.reshape(8, 128, 4).transpose(1, 0, 2))
    d["ident"] = np.eye(128, dtype=np.float32)
    return d


def kernel(**inputs):
    x = np.asarray(inputs["L0_attn_x"], dtype=np.float32)
    shared = _prep_shared({k: np.asarray(v, dtype=np.float32)
                           for k, v in inputs.items() if k != "L0_attn_x"})
    nc, _ = build_program()
    in_maps = []
    for b in range(8):
        m = dict(shared)
        m["xT"] = np.ascontiguousarray(x[b].T)
        in_maps.append(m)
    res = run_bass_kernel_spmd(nc, in_maps, core_ids=list(range(8)))
    out = np.stack([np.asarray(res.results[b]["yT"]).T for b in range(8)], axis=0)
    return np.ascontiguousarray(out.astype(np.float32))


# version 19
# speedup vs baseline: 1.0634x; 1.0049x over previous
import numpy as np
import contextlib
import concourse.bass as bass
import concourse.mybir as mybir
from concourse.bass_utils import run_bass_kernel_spmd

F32 = mybir.dt.float32
BF16 = mybir.dt.bfloat16
AF = mybir.ActivationFunctionType
ALU = mybir.AluOpType

C = 1024
S = 2048
HID = 4096
NCH = C // 128
NOC = HID // 128
NQ = 4
TQ = S // NQ
EPS_LN = 1e-5
EPS_ATTN = 1e-6
NHEAD = 16

ENGINES = ("sync", "tensor", "vector", "scalar", "gpsimd")
TT, TS, STT = "tensor_tensor", "tensor_scalar", "scalar_tensor_tensor"


class Emitter:
    def __init__(self):
        self.q = {e: [] for e in ENGINES}
        self.cnt = {}

    def new_sem(self, name):
        self.cnt[name] = 0

    def op(self, eng, meth, *args, inc=None, **kw):
        if inc is not None:
            self.cnt[inc[0]] += inc[1]
        self.q[eng].append(("op", meth, args, kw, inc))
        return self.cnt[inc[0]] if inc else None

    def dma(self, eng, out, in_, sem):
        return self.op(eng, "dma_start", inc=(sem, 16), out=out, in_=in_)

    def wait(self, eng, sem, val):
        assert val <= self.cnt[sem], f"wait {sem}>={val} > emitted {self.cnt[sem]}"
        if val > 0:
            self.q[eng].append(("wait", sem, val))

    def wait_dma(self, eng, sem, val):
        assert val <= self.cnt[sem], f"wait {sem}>={val} > emitted {self.cnt[sem]}"
        if val > 0:
            self.q[eng].append(("dwait", sem, val))

    def resolve_dwaits(self):
        for eng in ENGINES:
            for i, item in enumerate(self.q[eng]):
                if item[0] == "dwait":
                    _, sem, val = item
                    self.q[eng][i] = ("wait", sem, min(val + 16, self.cnt[sem]))

    def replay(self, eng_obj, eng_name, sems):
        for item in self.q[eng_name]:
            if item[0] == "wait":
                _, sem, val = item
                eng_obj.wait_ge(sems[sem], val)
            else:
                _, meth, args, kw, inc = item
                inst = getattr(eng_obj, meth)(*args, **kw)
                if inc is not None:
                    inst.then_inc(sems[inc[0]], inc[1])


def build_program(nq=NQ, stop_after=99):
    nc = bass.Bass("TRN2", target_bir_lowering=False, debug=False)

    xT = nc.dram_tensor("xT", [C, S], F32, kind="ExternalInput").ap()
    w1g = nc.dram_tensor("w1g", [8, 128, 4096], F32, kind="ExternalInput").ap()
    w2g = nc.dram_tensor("w2g", [8, 128, 4096], F32, kind="ExternalInput").ap()
    h1g = nc.dram_tensor("h1g", [32, 128, 4096], F32, kind="ExternalInput").ap()
    h2g = nc.dram_tensor("h2g", [32, 128, 4096], F32, kind="ExternalInput").ap()
    w3g = nc.dram_tensor("w3g", [8, 128, 4096], F32, kind="ExternalInput").ap()
    s1i = nc.dram_tensor("s1", [128, 32], F32, kind="ExternalInput").ap()
    s2i = nc.dram_tensor("s2", [128, 32], F32, kind="ExternalInput").ap()
    h1si = nc.dram_tensor("h1s", [128, 32], F32, kind="ExternalInput").ap()
    h2si = nc.dram_tensor("h2s", [128, 32], F32, kind="ExternalInput").ap()
    w3si = nc.dram_tensor("w3s", [128, 8], F32, kind="ExternalInput").ap()
    gami = nc.dram_tensor("gam", [128, 8], F32, kind="ExternalInput").ap()
    beti = nc.dram_tensor("bet", [128, 8], F32, kind="ExternalInput").ap()
    hfili = nc.dram_tensor("hfil", [128, 8, 4], F32, kind="ExternalInput").ap()
    gfili = nc.dram_tensor("gfil", [128, 8, 4], F32, kind="ExternalInput").ap()
    hsyni = nc.dram_tensor("hsyn", [128, 8, 4], F32, kind="ExternalInput").ap()
    gsyni = nc.dram_tensor("gsyn", [128, 8, 4], F32, kind="ExternalInput").ap()
    identi = nc.dram_tensor("ident", [128, 128], F32, kind="ExternalInput").ap()
    yT = nc.dram_tensor("yT", [C, S], F32, kind="ExternalOutput").ap()
    ao_scr = nc.dram_tensor("ao_scr", [NCH, 128, S], BF16, kind="Internal").ap()
    l0_scr = nc.dram_tensor("l0_scr", [NCH, 128, 1026], BF16, kind="Internal").ap()
    h0_scr = nc.dram_tensor("h0_scr", [NCH, 128, 1026], BF16, kind="Internal").ap()
    r1_scr = nc.dram_tensor("r1_scr", [NCH, 128, 1026], BF16, kind="Internal").ap()
    low_scr = nc.dram_tensor("low_scr", [NCH, 128, S], BF16, kind="Internal").ap()
    ws_s1 = nc.dram_tensor("ws_s1", [8, 2, 128, 4096], BF16, kind="Internal").ap()
    ws_s23 = nc.dram_tensor("ws_s23", [2, 32, 128, 4096], BF16, kind="Internal").ap()
    ws_s4 = nc.dram_tensor("ws_s4", [8, 128, 4096], BF16, kind="Internal").ap()

    em = Emitter()
    semnames = (["qC", "qA0", "qA1", "qA2", "qA3", "qA4", "qA5", "qA6", "qA7",
                 "qX0", "qX1", "qO0", "qO1",
                 "qF0", "qF1", "qLs0", "qLs1", "qAO0", "qAO1",
                 "tPE", "tACT", "tDVE", "tPOOL"]
                + [f"qW{i}" for i in range(4)] + [f"qS{i}" for i in range(4)]
                + [f"qT{i}" for i in range(4)])
    for sname in semnames:
        em.new_sem(sname)

    ctxs = []

    def sbuf(name, shape, dt):
        t = nc.sbuf_tensor(name, shape, dt)
        ctxs.append(t)
        return t.__enter__()

    def ps_(name, shape, dt=F32):
        t = nc.psum_tensor(name, shape, dt)
        ctxs.append(t)
        return t.__enter__()

    xn = sbuf("xn", [128, NCH * S], BF16)
    abuf = sbuf("abuf", [128, NOC * TQ], BF16)
    bbuf = sbuf("bbuf", [128, NOC * TQ], BF16)
    wb = sbuf("wb", [128, 4 * 4096], BF16)
    bcR = sbuf("bcR", [128, S], F32)
    bcB = sbuf("bcB", [128, S], F32)
    aren = sbuf("aren", [128, 16448], BF16)
    aob = sbuf("aob", [128, 4096], BF16)
    t1a = sbuf("t1a", [128, 2 * TQ], F32)
    t2a = sbuf("t2a", [128, 2 * TQ], F32)
    xq = sbuf("xq", [128, 2 * TQ], F32)
    aoq = sbuf("aoq", [128, 2 * TQ], BF16)
    outb = sbuf("outb", [128, 2 * TQ], F32)
    scl = sbuf("scl", [128, 152], F32)
    filt = sbuf("filt", [128, 128], F32)
    fcmb = sbuf("fcmb", [128, 80], F32)
    ident = sbuf("identsb", [128, 128], F32)
    identb = sbuf("identbb", [128, 128], BF16)
    onesc = sbuf("onesc", [128, 1], F32)
    cvals = sbuf("cvals", [128, 2], F32)
    dtail = sbuf("dtail", [128, 8], F32)
    dtailb = sbuf("dtailb", [128, 8], BF16)
    onesr = sbuf("onesr", [1, 128], F32)
    kvs = sbuf("kvs", [128, NCH * 65], BF16)

    ar = aren.ap()
    azA = ar[:, 0:4112].bitcast(F32)
    azB = ar[:, 4112:8224].bitcast(F32)
    R3 = 8224
    R4 = 12336
    low2v = ar[:, 0:4096]
    h1v = ar[:, 4112:8224]
    qfmv = ar[:, R3:R3 + 4096]
    vtb = [aob.ap()[:, i * 258: i * 258 + 130] for i in range(4)]
    qtb = [aob.ap()[:, i * 258 + 130: i * 258 + 258] for i in range(4)]
    attv = ar[:, R4:R4 + 4112]
    sb4 = [ar[:, R3 + i * 1028: R3 + i * 1028 + 1028] for i in range(4)]
    lowsl = [ar[:, 0:2048], ar[:, R4:R4 + 2048]]
    fyb = [ar[:, R3 + i * 1024: R3 + (i + 1) * 1024].bitcast(F32) for i in range(2)]
    fab = [ar[:, R3 + 2048 + i * 512: R3 + 2048 + (i + 1) * 512] for i in range(2)]
    aosl = [ar[:, R3 + i * 2048: R3 + (i + 1) * 2048] for i in range(2)]
    aof = aob.ap()[:, 0:4096].bitcast(F32)

    s1v, s2v = scl[:, 0:32], scl[:, 32:64]
    h1sv, h2sv = scl[:, 64:96], scl[:, 96:128]
    w3sv, gamv, betv = scl[:, 128:136], scl[:, 136:144], scl[:, 144:152]

    p = [ps_(f"p{i}", [128, 512], F32) for i in range(6)]
    ptr = [ps_(f"ptr{i}", [128, 512], BF16) for i in range(2)]

    wslot = [wb[:, i * 4096:(i + 1) * 4096] for i in range(4)]

    slot_free = {i: 0 for i in range(4)}
    slot_store = {i: 0 for i in range(4)}

    def _store_slot(slot, dst, load_cnt):
        em.wait("scalar", f"qW{slot}", load_cnt)
        em.dma("scalar", dst, wslot[slot][:, :], f"qS{slot}")
        slot_store[slot] = em.cnt[f"qS{slot}"]
        return slot_store[slot]

    def _wload(slot, src, q, dst_scr):
        em.wait("gpsimd", "tPE", slot_free[slot])
        em.wait("gpsimd", f"qS{slot}", slot_store[slot])
        if q == 0:
            v = em.dma("gpsimd", wslot[slot][:, :], src, f"qW{slot}")
            _store_slot(slot, dst_scr, v)
        else:
            v = em.dma("gpsimd", wslot[slot][:, :], dst_scr, f"qW{slot}")
        return v

    def _wload_direct(slot, src):
        em.wait("gpsimd", "tPE", slot_free[slot])
        em.wait("gpsimd", f"qS{slot}", slot_store[slot])
        return em.dma("gpsimd", wslot[slot][:, :], src, f"qW{slot}")

    def load_s1(g, q):
        ready = []
        for wi in (0, 1):
            slot = (2 * g + wi) % 4
            ready.append((slot, _wload(slot, (w1g, w2g)[wi][g], q, ws_s1[g, wi])))
        return ready

    def load_s23(stage_i, g, q):
        slot = g % 4
        wq = 0
        if q < wq:
            return [(slot, _wload_direct(slot, (h1g, h2g)[stage_i][g]))]
        return [(slot, _wload(slot, (h1g, h2g)[stage_i][g], q - wq, ws_s23[stage_i, g]))]

    def load_s4(cc, q):
        slot = cc % 4
        return [(slot, _wload(slot, w3g[cc], q, ws_s4[cc]))]

    def wait_w(eng, ready):
        for slot, v in ready:
            em.wait(eng, f"qW{slot}", v)

    em.dma("sync", ident[:, :], identi, "qC")
    em.dma("sync", scl[:, 0:32], s1i, "qC")
    em.dma("sync", scl[:, 32:64], s2i, "qC")
    em.dma("sync", scl[:, 64:96], h1si, "qC")
    em.dma("sync", scl[:, 96:128], h2si, "qC")
    em.dma("sync", scl[:, 128:136], w3si, "qC")
    em.dma("sync", scl[:, 136:144], gami, "qC")
    em.dma("sync", scl[:, 144:152], beti, "qC")
    em.dma("sync", filt[:, 0:32].rearrange("p (c k) -> p c k", c=8), hfili, "qC")
    em.dma("sync", filt[:, 32:64].rearrange("p (c k) -> p c k", c=8), gfili, "qC")
    em.dma("sync", filt[:, 64:96].rearrange("p (c k) -> p c k", c=8), hsyni, "qC")
    n_const = em.dma("sync", filt[:, 96:128].rearrange("p (c k) -> p c k", c=8),
                     gsyni, "qC")

    em.op("vector", "memset", onesc[:, :], 1.0)
    em.op("vector", "memset", cvals[:, 0:1], EPS_LN)
    em.op("vector", "memset", cvals[:, 1:2], -0.5)
    em.op("vector", "memset", onesr[:, :], 1.0, inc=("tDVE", 1))

    s1_ready = {0: load_s1(0, 0), 1: load_s1(1, 0)}

    abf = abuf.ap()[:, 0:16384].bitcast(F32)
    bbf = bbuf.ap()[:, 0:16384].bitcast(F32)
    xres = ([abf[:, i * 2048:(i + 1) * 2048] for i in range(4)]
            + [bbf[:, i * 2048:(i + 1) * 2048] for i in range(4)])
    sqscr = [azA[:, 0:2048], azB[:, 0:2048]]
    rowA = aof[0:1, 0:2048]
    rowB = azB[0:1, 0:2048]
    m2r = azA[0:1, 0:2048]
    xld = {}
    for cc in range(0, NCH, 2):
        xld[cc] = em.dma("sync", xres[cc], xT[cc * 128:(cc + 1) * 128, :], f"qA{cc}")
    for cc in range(1, NCH, 2):
        xld[cc] = em.dma("scalar", xres[cc], xT[cc * 128:(cc + 1) * 128, :], f"qA{cc}")

    sq_cnt, add_cnt = {}, {}
    for cc in range(NCH):
        if cc >= 2:
            em.wait("scalar", "tDVE", add_cnt[cc - 2])
        em.wait("scalar", f"qA{cc}", xld[cc])
        sq_cnt[cc] = em.op("scalar", "activation", out=sqscr[cc % 2], in_=xres[cc],
                           func=AF.Square, inc=("tACT", 1))
        em.wait("vector", f"qA{cc}", xld[cc])
        if cc == 0:
            em.op("vector", "tensor_copy", out=bcR[:, :], in_=xres[cc])
        else:
            em.op("vector", TT, out=bcR[:, :], in0=bcR[:, :], in1=xres[cc], op=ALU.add)
        em.wait("vector", "tACT", sq_cnt[cc])
        if cc == 0:
            add_cnt[cc] = em.op("vector", "tensor_copy", out=bcB[:, :], in_=sqscr[0],
                                inc=("tDVE", 1))
        else:
            add_cnt[cc] = em.op("vector", TT, out=bcB[:, :], in0=bcB[:, :],
                                in1=sqscr[cc % 2], op=ALU.add, inc=("tDVE", 1))

    em.wait("scalar", "qC", n_const)
    em.op("scalar", "copy", out=identb[:, :], in_=ident[:, :], inc=("tACT", 1))
    identb_done = em.cnt["tACT"]

    em.wait("vector", "qC", n_const)
    for fi, base in ((0, 64), (1, 96)):
        for cc in range(8):
            fsl = filt[:, base + 4 * cc: base + 4 * cc + 4]
            dst = fcmb[:, fi * 40 + 5 * cc: fi * 40 + 5 * cc + 5]
            em.op("vector", "tensor_copy", out=dst[:, 0:1], in_=fsl[:, 0:1])
            em.op("vector", TT, out=dst[:, 1:2], in0=fsl[:, 1:2], in1=fsl[:, 2:3], op=ALU.add)
            em.op("vector", "tensor_copy", out=dst[:, 2:3], in_=fsl[:, 3:4])
            em.op("vector", TT, out=dst[:, 3:4], in0=fsl[:, 0:1], in1=fsl[:, 1:2], op=ALU.add)
            em.op("vector", TT, out=dst[:, 4:5], in0=fsl[:, 2:3], in1=fsl[:, 3:4],
                  op=ALU.add, inc=("tDVE", 1))

    em.wait("tensor", "tDVE", add_cnt[NCH - 1])
    amm, aev, bmm, bev = {}, {}, {}, {}
    for tb in range(4):
        amm[tb] = em.op("tensor", "matmul", p[tb][0:1, 0:512], onesc[:, :],
                        bcR[:, tb * 512:(tb + 1) * 512], start=True, stop=True,
                        inc=("tPE", 1))
    for tb in range(2):
        bmm[tb] = em.op("tensor", "matmul", p[4 + tb][0:1, 0:512], onesc[:, :],
                        bcB[:, tb * 512:(tb + 1) * 512], start=True, stop=True,
                        inc=("tPE", 1))
    for tb in range(4):
        em.wait("scalar", "tPE", amm[tb])
        aev[tb] = em.op("scalar", "copy", out=rowA[:, tb * 512:(tb + 1) * 512],
                        in_=p[tb][0:1, 0:512], inc=("tACT", 1))
    em.wait("tensor", "tACT", aev[1])
    for tb in range(2, 4):
        bmm[tb] = em.op("tensor", "matmul", p[tb - 2][0:1, 0:512], onesc[:, :],
                        bcB[:, tb * 512:(tb + 1) * 512], start=True, stop=True,
                        inc=("tPE", 1))
    for tb in range(4):
        em.wait("scalar", "tPE", bmm[tb])
        bev[tb] = em.op("scalar", "copy", out=rowB[:, tb * 512:(tb + 1) * 512],
                        in_=(p[4 + tb] if tb < 2 else p[tb - 2])[0:1, 0:512],
                        inc=("tACT", 1))

    em.wait("vector", "tACT", bev[3])
    em.op("vector", TS, out=rowA, in0=rowA, scalar1=-1.0 / C, scalar2=None,
          op0=ALU.mult)
    em.op("vector", TT, out=m2r, in0=rowA, in1=rowA, op=ALU.mult)
    em.op("vector", TS, out=rowB, in0=rowB, scalar1=1.0 / C, scalar2=None,
          op0=ALU.mult)
    vm1 = em.op("vector", TT, out=rowB, in0=rowB, in1=m2r, op=ALU.subtract,
                inc=("tDVE", 1))
    em.wait("scalar", "tDVE", vm1)
    em.op("scalar", "activation", out=rowB, in_=rowB, func=AF.Ln, bias=cvals[0:1, 0:1])
    rr = em.op("scalar", "activation", out=rowB, in_=rowB, func=AF.Exp,
               scale=cvals[0:1, 1:2], inc=("tACT", 1))
    em.wait("vector", "tACT", rr)
    arow = em.op("vector", TT, out=rowA, in0=rowA, in1=rowB, op=ALU.mult,
                 inc=("tDVE", 1))

    em.wait("tensor", "tDVE", arow)
    bbm, bbe, bam, bae = {}, {}, {}, {}
    for tb in range(4):
        bbm[tb] = em.op("tensor", "matmul", p[tb][:, :], onesr[:, :],
                        rowB[:, tb * 512:(tb + 1) * 512], start=True, stop=True,
                        inc=("tPE", 1))
    for tb in range(4):
        em.wait("scalar", "tPE", bbm[tb])
        bbe[tb] = em.op("scalar", "copy", out=bcB[:, tb * 512:(tb + 1) * 512],
                        in_=p[tb][:, :], inc=("tACT", 1))
    for tb in range(4):
        em.wait("tensor", "tACT", bbe[tb])
        bam[tb] = em.op("tensor", "matmul", p[tb][:, :], onesr[:, :],
                        rowA[:, tb * 512:(tb + 1) * 512], start=True, stop=True,
                        inc=("tPE", 1))
    for tb in range(4):
        em.wait("scalar", "tPE", bam[tb])
        bae[tb] = em.op("scalar", "copy", out=bcR[:, tb * 512:(tb + 1) * 512],
                        in_=p[tb][:, :], inc=("tACT", 1))
    bcast_done = bae[3]

    def ln_reduce_bcast(tA, tB, sq_scratch, rowA, rowB, guard_dve):
        em.wait("tensor", "tDVE", guard_dve)
        for src, row in ((tA, rowA), (tB, rowB)):
            for half in range(2):
                em.wait("tensor", "tACT", em.cnt["tACT"])
                for i in range(2):
                    tb = 2 * half + i
                    em.op("tensor", "matmul", p[4 + i][0:1, 0:512], onesc[:, :],
                          src[:, tb * 512:(tb + 1) * 512], start=True, stop=True,
                          inc=("tPE", 1))
                mm = em.cnt["tPE"]
                em.wait("scalar", "tPE", mm)
                for i in range(2):
                    tb = 2 * half + i
                    em.op("scalar", "copy", out=row[0:1, tb * 512:(tb + 1) * 512],
                          in_=p[4 + i][0:1, 0:512], inc=("tACT", 1))
        for src, row in ((tA, rowA), (tB, rowB)):
            for half in range(2):
                em.wait("tensor", "tACT", em.cnt["tACT"])
                for i in range(2):
                    tb = 2 * half + i
                    em.op("tensor", "matmul", p[4 + i][:, :], onesr[:, :],
                          row[0:1, tb * 512:(tb + 1) * 512], start=True, stop=True,
                          inc=("tPE", 1))
                mm = em.cnt["tPE"]
                em.wait("scalar", "tPE", mm)
                for i in range(2):
                    tb = 2 * half + i
                    em.op("scalar", "copy", out=src[:, tb * 512:(tb + 1) * 512],
                          in_=p[4 + i][:, :], inc=("tACT", 1))
        e2 = em.cnt["tACT"]
        em.wait("vector", "tACT", e2)
        em.op("vector", TS, out=tA, in0=tA, scalar1=1.0 / C, scalar2=None, op0=ALU.mult)
        em.op("vector", TS, out=tB, in0=tB, scalar1=1.0 / C, scalar2=None, op0=ALU.mult)
        em.op("vector", TT, out=sq_scratch, in0=tA, in1=tA, op=ALU.mult)
        vd = em.op("vector", TT, out=tB, in0=tB, in1=sq_scratch, op=ALU.subtract,
                   inc=("tDVE", 1))
        em.wait("scalar", "tDVE", vd)
        em.op("scalar", "activation", out=tB, in_=tB, func=AF.Ln, bias=cvals[:, 0:1])
        rr = em.op("scalar", "activation", out=tB, in_=tB, func=AF.Exp,
                   scale=cvals[:, 1:2], inc=("tACT", 1))
        em.wait("vector", "tACT", rr)
        em.op("vector", TT, out=tA, in0=tA, in1=tB, op=ALU.mult)
        return em.op("vector", TS, out=tA, in0=tA, scalar1=-1.0, scalar2=None,
                     op0=ALU.mult, inc=("tDVE", 1))

    ckpt = [0]

    def ok():
        ckpt[0] += 1
        return ckpt[0] < stop_after
    go = ok()

    xn_cast1, xn_cast, st_cnt = {}, {}, {}
    em.wait("vector", "tACT", bcast_done)
    for cc in range(NCH):
        em.op("vector", TT, out=xres[cc][:, 0:TQ], in0=xres[cc][:, 0:TQ],
              in1=bcB[:, 0:TQ], op=ALU.mult)
        zr = em.op("vector", TT, out=xres[cc][:, 0:TQ], in0=xres[cc][:, 0:TQ],
                   in1=bcR[:, 0:TQ], op=ALU.add, inc=("tDVE", 1))
        em.wait("scalar", "tDVE", zr)
        xn_cast1[cc] = em.op("scalar", "copy", out=xn[:, cc * S: cc * S + TQ],
                             in_=xres[cc][:, 0:TQ], inc=("tACT", 1))
    for cc in range(NCH):
        em.op("vector", TT, out=xres[cc][:, TQ:S], in0=xres[cc][:, TQ:S],
              in1=bcB[:, TQ:S], op=ALU.mult)
        zr = em.op("vector", TT, out=xres[cc][:, TQ:S], in0=xres[cc][:, TQ:S],
                   in1=bcR[:, TQ:S], op=ALU.add, inc=("tDVE", 1))
        em.wait("scalar", "tDVE", zr)
        xn_cast[cc] = em.op("scalar", "copy", out=xn[:, cc * S + TQ:(cc + 1) * S],
                            in_=xres[cc][:, TQ:S], inc=("tACT", 1))
    xn_all = max(xn_cast.values())

    for cc in range(NCH):
        xc = xn[:, cc * S:(cc + 1) * S]
        em.wait("vector", "tACT", xn_cast[cc])
        for kind, fbase, scr in (("l", 0, l0_scr), ("h", 32, h0_scr)):
            tsl = (cc % 2) * 2 + (0 if kind == "l" else 1)
            slot = sb4[tsl]
            if cc >= 2:
                em.wait("vector", f"qT{tsl}", st_cnt[(kind, cc - 2)])
            def fk(k, fb=fbase, c=cc):
                return filt[:, fb + 4 * c + k: fb + 4 * c + k + 1]
            em.op("vector", TS, out=aof[:, 0:1024], in0=xc[:, 0:2048:2], scalar1=fk(1),
                  scalar2=None, op0=ALU.mult)
            em.op("vector", STT, out=aof[:, 0:1024], in0=xc[:, 1:2048:2],
                  scalar=fk(2), in1=aof[:, 0:1024], op0=ALU.mult, op1=ALU.add)
            em.op("vector", STT, out=aof[:, 1:1024], in0=xc[:, 1:2047:2],
                  scalar=fk(0), in1=aof[:, 1:1024], op0=ALU.mult, op1=ALU.add)
            em.op("vector", STT, out=slot[:, 1:1024], in0=xc[:, 2:2048:2],
                  scalar=fk(3), in1=aof[:, 0:1023], op0=ALU.mult, op1=ALU.add)
            em.op("vector", "tensor_copy", out=slot[:, 1024:1025],
                  in_=aof[:, 1023:1024], inc=("tDVE", 1))
            if cc < 2:
                em.op("vector", "memset", slot[:, 0:1], 0.0)
                em.op("vector", "memset", slot[:, 1025:1028], 0.0, inc=("tDVE", 1))
            dve_last = em.cnt["tDVE"]
            em.wait("sync", "tDVE", dve_last)
            st_cnt[(kind, cc)] = em.dma("sync", scr[cc], slot[:, 0:1026], f"qT{tsl}")

    c_l1_done = {}
    for cc in range(NCH):
        tsl = cc % 2
        slot = sb4[tsl]
        if cc >= 2:
            em.wait("sync", "tDVE", c_l1_done[cc - 2])
        em.wait("sync", f"qT{(cc % 2) * 2}", st_cnt[("l", cc)])
        em.wait("sync", f"qT{tsl}", em.cnt[f"qT{tsl}"])
        li = em.dma("sync", slot[:, 0:1026], l0_scr[cc], f"qT{tsl}")
        em.wait("vector", f"qT{tsl}", li)
        for dsl, fbase in ((low2v[:, cc * 512: cc * 512 + 512], 0),
                           (h1v[:, cc * 514 + 1: cc * 514 + 513], 32)):
            def fk(k, fb=fbase, c=cc):
                return filt[:, fb + 4 * c + k: fb + 4 * c + k + 1]
            a5 = aof[:, 0:512]
            em.op("vector", TS, out=a5, in0=slot[:, 0:1024:2], scalar1=fk(0),
                  scalar2=None, op0=ALU.mult)
            for k in (1, 2):
                em.op("vector", STT, out=a5, in0=slot[:, k:k + 1024:2], scalar=fk(k),
                      in1=a5, op0=ALU.mult, op1=ALU.add)
            em.op("vector", STT, out=dsl, in0=slot[:, 3:1026:2], scalar=fk(3),
                  in1=a5, op0=ALU.mult, op1=ALU.add, inc=("tDVE", 1))
        if cc == 0:
            em.op("vector", "memset",
                  h1v.rearrange("p (c n) -> p c n", c=8)[:, :, 0:1], 0.0)
            em.op("vector", "memset",
                  h1v.rearrange("p (c n) -> p c n", c=8)[:, :, 513:514], 0.0,
                  inc=("tDVE", 1))
        c_l1_done[cc] = em.cnt["tDVE"]

    pfree = {i: (0, 0) for i in range(4)}
    s23_ready, s4_ready = {}, {}
    out_st, asm_done = {}, {}
    pending_store = [None]
    ao_gate = [(0, 0)]
    ao_gate_emitted = [False]

    def stage1(q):
        qc0, qc1 = q * TQ, (q + 1) * TQ
        a_cnt = {}
        for oc in range(NOC):
            g = oc // 4
            gi = q * 8 + g
            if oc % 4 == 0:
                if gi not in s1_ready:
                    s1_ready[gi] = load_s1(g, q)
                if g + 1 < 8 and gi + 1 not in s1_ready:
                    s1_ready[gi + 1] = load_s1(g + 1, q)
            pu1, pu2 = p[2 * (oc % 2)], p[2 * (oc % 2) + 1]
            wait_w("tensor", s1_ready[gi])
            fa1, _ = pfree[2 * (oc % 2)]
            em.wait("tensor", "tACT", fa1)
            fa2, _ = pfree[2 * (oc % 2) + 1]
            em.wait("tensor", "tACT", fa2)
            slotW1, slotW2 = (2 * g) % 4, (2 * g + 1) % 4
            ocl = oc % 4
            for ic in range(NCH):
                if oc == 0 and q == 0:
                    em.wait("tensor", "tACT", xn_cast1[ic])
                em.op("tensor", "matmul", pu1[:, :],
                      wslot[slotW1][:, ic * 512 + ocl * 128: ic * 512 + ocl * 128 + 128],
                      xn[:, ic * S + qc0: ic * S + qc1],
                      start=(ic == 0), stop=(ic == NCH - 1), inc=("tPE", 1))
            mm1 = em.cnt["tPE"]
            for ic in range(NCH):
                em.op("tensor", "matmul", pu2[:, :],
                      wslot[slotW2][:, ic * 512 + ocl * 128: ic * 512 + ocl * 128 + 128],
                      xn[:, ic * S + qc0: ic * S + qc1],
                      start=(ic == 0), stop=(ic == NCH - 1), inc=("tPE", 1))
            mm2 = em.cnt["tPE"]
            if ocl == 3:
                slot_free[slotW1] = mm1
                slot_free[slotW2] = mm2
            em.wait("scalar", "tPE", mm1)
            if oc >= 2:
                em.wait("scalar", "tPOOL", a_cnt[oc - 2])
            sl = (oc % 2) * TQ
            s1c = em.op("scalar", "activation", out=t1a[:, sl:sl + TQ], in_=pu1[:, :],
                        func=AF.Silu, scale=s1v[:, oc:oc + 1], inc=("tACT", 1))
            em.wait("scalar", "tPE", mm2)
            ev2 = em.op("scalar", "activation", out=t2a[:, sl:sl + TQ], in_=pu2[:, :],
                        func=AF.Identity, scale=s2v[:, oc:oc + 1], inc=("tACT", 1))
            em.wait("gpsimd", "tACT", ev2)
            a_cnt[oc] = em.op("gpsimd", TT, out=abuf.ap()[:, oc * TQ:(oc + 1) * TQ],
                              in0=t1a[:, sl:sl + TQ], in1=t2a[:, sl:sl + TQ],
                              op=ALU.mult, inc=("tPOOL", 1))
            pfree[2 * (oc % 2)] = (s1c, a_cnt[oc])
            pfree[2 * (oc % 2) + 1] = (ev2, a_cnt[oc])
        return em.cnt["tPOOL"]

    def stage23(q, stage_i, scale_v, inv, outv, ready_pool, ready_act, interleave=None):
        kbase = 1000 + q * 100 + stage_i * 50
        b_cnt = {}
        for jc in range(NOC):
            if interleave is not None:
                interleave(jc)
            gg = kbase + jc
            if gg not in s23_ready:
                s23_ready[gg] = load_s23(stage_i, jc, q)
            slot = jc % 4
            pb = p[2 + jc % 2]
            wait_w("tensor", s23_ready[gg])
            if jc == 0:
                if ready_pool:
                    em.wait("tensor", "tPOOL", ready_pool)
                if ready_act:
                    em.wait("tensor", "tACT", ready_act)
            fa, _ = pfree[2 + jc % 2]
            em.wait("tensor", "tACT", fa)
            for oc in range(NOC):
                em.op("tensor", "matmul", pb[:, :],
                      wslot[slot][:, oc * 128: oc * 128 + 128],
                      inv(oc), start=(oc == 0), stop=(oc == NOC - 1), inc=("tPE", 1))
            mm = em.cnt["tPE"]
            slot_free[slot] = mm
            em.wait("scalar", "tPE", mm)
            b_cnt[jc] = em.op("scalar", "activation", out=outv(jc), in_=pb[:, :],
                              func=AF.Gelu, scale=scale_v[:, jc:jc + 1], inc=("tACT", 1))
            pfree[2 + jc % 2] = (b_cnt[jc], 0)
        return max(b_cnt.values())

    def stage4(q, c_all, with_ao, interleave=None):
        qc0, qc1 = q * TQ, (q + 1) * TQ
        for cc in range(NCH):
            if interleave is not None:
                interleave(cc)
            gi = 10000 + q * 8 + cc
            if gi not in s4_ready:
                s4_ready[gi] = load_s4(cc, q)
            if cc + 1 < NCH and gi + 1 not in s4_ready:
                s4_ready[gi + 1] = load_s4(cc + 1, q)
            ki = (q, cc)
            slot = cc % 2
            wsl = cc % 4
            prev = (q, cc - 2) if cc >= 2 else ((q - 1, cc + 6) if q > 0 else None)
            if prev in asm_done:
                em.wait("sync", "tPOOL", asm_done[prev])
            if with_ao and not ao_gate_emitted[0]:
                em.wait("sync", "qAO0", ao_gate[0][0])
                em.wait("sync", "qAO1", ao_gate[0][1])
                ao_gate_emitted[0] = True
            em.dma("sync", xq[:, slot * TQ: slot * TQ + TQ],
                   xT[cc * 128:(cc + 1) * 128, qc0:qc1], f"qX{slot}")
            if with_ao:
                em.dma("sync", aoq[:, slot * TQ: slot * TQ + TQ],
                       ao_scr[cc][:, qc0:qc1], f"qX{slot}")
            fl = em.cnt[f"qX{slot}"]
            if pending_store[0] is not None:
                pk, pslot, pcc, pq0, pq1 = pending_store[0]
                em.wait("sync", "tPOOL", asm_done[pk])
                out_st[pk] = em.dma("sync", yT[pcc * 128:(pcc + 1) * 128, pq0:pq1],
                                    outb[:, pslot * TQ: pslot * TQ + TQ], f"qO{pslot}")
                pending_store[0] = None
            py = p[cc % 4]
            wait_w("tensor", s4_ready[gi])
            if cc == 0:
                em.wait("tensor", "tACT", c_all)
            fa, fp = pfree[cc % 4]
            em.wait("tensor", "tACT", fa)
            em.wait("tensor", "tPOOL", fp)
            for lc in range(NOC):
                em.op("tensor", "matmul", py[:, :],
                      wslot[wsl][:, lc * 128:(lc + 1) * 128],
                      abuf.ap()[:, lc * TQ:(lc + 1) * TQ],
                      start=(lc == 0), stop=(lc == NOC - 1), inc=("tPE", 1))
            mm = em.cnt["tPE"]
            slot_free[wsl] = mm
            em.wait("scalar", "tPE", mm)
            if prev in out_st:
                em.wait("scalar", f"qO{slot}", out_st[prev])
            ev = em.op("scalar", "activation", out=outb[:, slot * TQ: slot * TQ + TQ],
                       in_=py[:, :], func=AF.Identity, scale=w3sv[:, cc:cc + 1],
                       inc=("tACT", 1))
            em.wait("gpsimd", "tACT", ev)
            em.wait("gpsimd", f"qX{slot}", fl)
            em.op("gpsimd", TT, out=outb[:, slot * TQ: slot * TQ + TQ],
                  in0=outb[:, slot * TQ: slot * TQ + TQ],
                  in1=xq[:, slot * TQ: slot * TQ + TQ], op=ALU.add,
                  inc=None if with_ao else ("tPOOL", 1))
            if with_ao:
                em.op("gpsimd", TT, out=outb[:, slot * TQ: slot * TQ + TQ],
                      in0=outb[:, slot * TQ: slot * TQ + TQ],
                      in1=aoq[:, slot * TQ: slot * TQ + TQ], op=ALU.add,
                      inc=("tPOOL", 1))
            asm_done[ki] = em.cnt["tPOOL"]
            pfree[cc % 4] = (ev, asm_done[ki])
            pending_store[0] = (ki, slot, cc, qc0, qc1)
        pk, pslot, pcc, pq0, pq1 = pending_store[0]
        em.wait("sync", "tPOOL", asm_done[pk])
        out_st[pk] = em.dma("sync", yT[pcc * 128:(pcc + 1) * 128, pq0:pq1],
                            outb[:, pslot * TQ: pslot * TQ + TQ], f"qO{pslot}")
        pending_store[0] = None

    a_all0 = stage1(0)
    b_all0 = stage23(0, 0, h1sv, lambda oc: abuf.ap()[:, oc * TQ:(oc + 1) * TQ],
                     lambda jc: bbuf.ap()[:, jc * TQ:(jc + 1) * TQ], a_all0, None)

    for cc in range(NCH):
        l2c = low2v[:, cc * 512: cc * 512 + 512]
        m0 = em.op("vector", TS, out=aof[:, 0:512], in0=l2c, scalar1=0.0, scalar2=None,
                   op0=ALU.min, inc=("tDVE", 1))
        em.wait("scalar", "tDVE", m0)
        e0 = em.op("scalar", "activation", out=aof[:, 512:1024], in_=aof[:, 0:512],
                   func=AF.Exp, inc=("tACT", 1))
        em.op("vector", TS, out=aof[:, 1024:1536], in0=l2c, scalar1=0.0, scalar2=None,
              op0=ALU.max)
        em.wait("vector", "tACT", e0)
        em.op("vector", TT, out=qfmv[:, cc * 512: cc * 512 + 512], in0=aof[:, 512:1024],
              in1=aof[:, 1024:1536], op=ALU.add, inc=("tDVE", 1))
    for i in range(4):
        em.op("vector", "memset", vtb[i][:, 64:65], 1.0)
        em.op("vector", "memset", vtb[i][:, 129:130], 1.0, inc=("tDVE", 1))

    kv_ev, tr_done = {}, {}
    kvst = {"prev_kvmm": 0, "prev_trev": 0, "first": True}
    nst = {"att_tt": 0, "numev": 0, "dens": 0}
    dr_cnt = {}
    elu_all = em.cnt["tDVE"]

    def kv_T(cc):
        if kvst["first"]:
            em.wait("tensor", "tACT", identb_done)
            em.wait("tensor", "tDVE", elu_all)
            kvst["first"] = False
        em.wait("tensor", "tDVE", kvst["prev_trev"])
        em.wait("vector", "tPE", kvst["prev_kvmm"])
        mmt = {}
        for j in range(8):
            tn, which = j // 2, j % 2
            srcv = (low2v, qfmv)[which]
            psl = ptr[j // 4][:, (j % 4) * 128:(j % 4) * 128 + 128]
            mmt[j] = em.op("tensor", "transpose", psl,
                           srcv[:, cc * 512 + tn * 128: cc * 512 + tn * 128 + 128],
                           identb[:, :], inc=("tPE", 1))
        for j in range(8):
            tn, which = j // 2, j % 2
            psl = ptr[j // 4][:, (j % 4) * 128:(j % 4) * 128 + 128]
            if j % 4 == 0:
                em.wait("vector", "tPE", mmt[j + 3])
            if which == 0:
                em.op("vector", "tensor_copy", out=vtb[tn][:, 0:64], in_=psl[:, 0:64])
                em.op("vector", "tensor_copy", out=vtb[tn][:, 65:129],
                      in_=psl[:, 64:128], inc=("tDVE", 1))
            else:
                em.op("vector", "tensor_copy", out=qtb[tn][:, 0:128],
                      in_=psl[:, :], inc=("tDVE", 1))
        tr_done[cc] = em.cnt["tDVE"]
        kvst["prev_trev"] = tr_done[cc]

    def kv_M(cc):
        em.wait("tensor", "tDVE", tr_done[cc])
        if cc > 0:
            em.wait("tensor", "tACT", kv_ev[2 * cc - 1])
        for tn in range(4):
            em.op("tensor", "matmul", p[0][0:64, 0:65], qtb[tn][:, 0:64],
                  vtb[tn][:, 0:65], start=(tn == 0), stop=(tn == 3), inc=("tPE", 1))
        for tn in range(4):
            em.op("tensor", "matmul", p[1][0:64, 0:65], qtb[tn][:, 64:128],
                  vtb[tn][:, 65:130], start=(tn == 0), stop=(tn == 3), inc=("tPE", 1))
        kvst["prev_kvmm"] = em.cnt["tPE"]
        em.wait("scalar", "tPE", kvst["prev_kvmm"])
        kv_ev[2 * cc] = em.op("scalar", "copy", out=kvs[0:64, 65 * cc: 65 * cc + 65],
                              in_=p[0][0:64, 0:65], inc=("tACT", 1))
        kv_ev[2 * cc + 1] = em.op("scalar", "copy",
                                  out=kvs[64:128, 65 * cc: 65 * cc + 65],
                                  in_=p[1][0:64, 0:65], inc=("tACT", 1))

    def num_half(wv, half):
        for i in (2 * half, 2 * half + 1):
            h = 4 * wv + i
            pi = 4 + i % 2
            em.wait("tensor", "tACT", kv_ev[h])
            if wv > 0 or half > 0:
                em.wait("tensor", "tDVE",
                        nst["att_tt"] if half == 0 else nst["dens"])
                if half > 0:
                    em.wait("tensor", "tACT", nst["numev"])
            cc = h // 2
            rbase = 64 * (h % 2)
            mm = em.op("tensor", "matmul", p[pi][0:65, 0:512],
                       kvs[rbase:rbase + 64, 65 * cc: 65 * cc + 65],
                       qfmv[rbase:rbase + 64, cc * 512: cc * 512 + 512],
                       start=True, stop=True, inc=("tPE", 1))
            em.wait("scalar", "tPE", mm)
            nst["numev"] = em.op("scalar", "copy",
                                 out=attv[rbase:rbase + 64, cc * 514 + 1: cc * 514 + 513],
                                 in_=p[pi][0:64, 0:512], inc=("tACT", 1))
            em.wait("vector", "tPE", mm)
            nst["dens"] = em.op("vector", TS, out=bcB[0:1, i * 512:(i + 1) * 512],
                                in0=p[pi][64:65, 0:512], scalar1=EPS_ATTN,
                                scalar2=None, op0=ALU.add, inc=("tDVE", 1))
        if half == 1:
            dr_cnt[wv] = em.op("vector", "reciprocal", out=bcR[0:1, 0:2048],
                               in_=bcB[0:1, 0:2048], inc=("tDVE", 1))

    def bc_half(wv, half):
        if half == 0:
            em.wait("tensor", "tDVE", dr_cnt[wv])
            em.wait("tensor", "tACT", nst["numev"])
        else:
            em.wait("tensor", "tDVE", nst["att_tt"])
        bcs = {}
        for i in (2 * half, 2 * half + 1):
            bcs[i] = em.op("tensor", "matmul", p[4 + i % 2][0:64, 0:512],
                           onesr[0:1, 0:64], bcR[0:1, i * 512:(i + 1) * 512],
                           start=True, stop=True, inc=("tPE", 1))
        for i in (2 * half, 2 * half + 1):
            h = 4 * wv + i
            cc = h // 2
            rbase = 64 * (h % 2)
            em.wait("vector", "tPE", bcs[i])
            nst["att_tt"] = em.op(
                "vector", TT,
                out=attv[rbase:rbase + 64, cc * 514 + 1: cc * 514 + 513],
                in0=attv[rbase:rbase + 64, cc * 514 + 1: cc * 514 + 513],
                in1=p[4 + i % 2][0:64, 0:512], op=ALU.mult, inc=("tDVE", 1))

    s3_tasks = {}
    for k in range(8):
        s3_tasks[2 + 4 * k] = (lambda c=k: kv_T(c))
        if k < 7:
            s3_tasks[4 + 4 * k] = (lambda c=k: kv_M(c))
    for sl, w, hf in ((5, 0, 0), (9, 0, 1), (11, 0, 2), (13, 0, 3),
                      (15, 1, 0), (17, 1, 1), (19, 1, 2), (21, 1, 3),
                      (23, 2, 0), (25, 2, 1), (27, 2, 2), (29, 2, 3),
                      (31, 3, 0)):
        fn = (num_half if hf < 2 else bc_half)
        s3_tasks[sl] = (lambda f=fn, a=w, b=hf: f(a, b % 2))

    def s3_inter(jc):
        if jc in s3_tasks:
            s3_tasks[jc]()

    def s4_inter(cc):
        if cc == 1:
            num_half(3, 1)
        elif cc == 2:
            bc_half(3, 0)
        elif cc == 3:
            bc_half(3, 1)

    c_all0 = stage23(0, 1, h2sv, lambda oc: bbuf.ap()[:, oc * TQ:(oc + 1) * TQ],
                     lambda jc: abuf.ap()[:, jc * TQ:(jc + 1) * TQ], None, b_all0,
                     interleave=s3_inter)
    kv_M(7)
    pfree[0] = (kv_ev[14], pfree[0][1])
    pfree[1] = (kv_ev[15], pfree[1][1])
    stage4(0, c_all0, with_ao=False, interleave=s4_inter)
    em.op("vector", "memset", attv.rearrange("p (c n) -> p c n", c=8)[:, :, 0:1], 0.0)
    em.op("vector", "memset", attv.rearrange("p (c n) -> p c n", c=8)[:, :, 513:514],
          0.0, inc=("tDVE", 1))

    def recon(cc, sa_ap, sb_ap, dst_ap, dst_off, nh):
        def sa(d):
            return sa_ap[:, d: d + nh]
        def sbv(d):
            return sb_ap[:, d: d + nh]
        def hcm(j, c=cc):
            return fcmb[:, 5 * c + j: 5 * c + j + 1]
        def gcm(j, c=cc):
            return fcmb[:, 40 + 5 * c + j: 40 + 5 * c + j + 1]
        a5 = aof[:, 0:nh]
        em.op("vector", TS, out=a5, in0=sa(0), scalar1=hcm(0), scalar2=None, op0=ALU.mult)
        em.op("vector", STT, out=a5, in0=sa(1), scalar=hcm(1), in1=a5, op0=ALU.mult, op1=ALU.add)
        em.op("vector", STT, out=a5, in0=sa(2), scalar=hcm(2), in1=a5, op0=ALU.mult, op1=ALU.add)
        em.op("vector", STT, out=a5, in0=sbv(0), scalar=gcm(0), in1=a5, op0=ALU.mult, op1=ALU.add)
        em.op("vector", STT, out=a5, in0=sbv(1), scalar=gcm(1), in1=a5, op0=ALU.mult, op1=ALU.add)
        em.op("vector", STT, out=dst_ap[:, dst_off: dst_off + 2 * nh: 2], in0=sbv(2),
              scalar=gcm(2), in1=a5, op0=ALU.mult, op1=ALU.add, inc=("tDVE", 1))
        em.op("vector", TS, out=a5, in0=sa(1), scalar1=hcm(3), scalar2=None, op0=ALU.mult)
        em.op("vector", STT, out=a5, in0=sa(2), scalar=hcm(4), in1=a5, op0=ALU.mult, op1=ALU.add)
        em.op("vector", STT, out=a5, in0=sbv(1), scalar=gcm(3), in1=a5, op0=ALU.mult, op1=ALU.add)
        return em.op("vector", STT, out=dst_ap[:, dst_off + 1: dst_off + 2 * nh: 2],
                     in0=sbv(2), scalar=gcm(4), in1=a5, op0=ALU.mult, op1=ALU.add,
                     inc=("tDVE", 1))

    r1_st = {}
    for cc in range(NCH):
        tsl = cc % 2
        slot = sb4[tsl]
        if cc >= 2:
            em.wait("vector", f"qT{tsl}", r1_st[cc - 2])
        d = recon(cc, attv[:, cc * 514: cc * 514 + 514],
                  h1v[:, cc * 514: cc * 514 + 514], slot, 1, 512)
        if cc < 2:
            em.op("vector", "memset", slot[:, 0:1], 0.0)
            em.op("vector", "memset", slot[:, 1025:1028], 0.0, inc=("tDVE", 1))
            d = em.cnt["tDVE"]
        em.wait("sync", "tDVE", d)
        r1_st[cc] = em.dma("sync", r1_scr[cc], slot[:, 0:1026], f"qT{tsl}")

    low_st, f_done = {}, {}
    for cc in range(NCH):
        rtsl = cc % 2
        htsl = 2 + cc % 2
        rsl = sb4[rtsl]
        hsl = sb4[htsl]
        osl = lowsl[cc % 2]
        if cc >= 2:
            em.wait("sync", "tDVE", f_done[cc - 2])
        em.wait("sync", f"qT{rtsl}", r1_st[cc])
        em.wait("sync", f"qT{htsl}", em.cnt[f"qT{htsl}"])
        rl = em.dma("sync", rsl[:, 0:1026], r1_scr[cc], f"qT{rtsl}")
        hl = em.dma("sync", hsl[:, 0:1026], h0_scr[cc], f"qT{htsl}")
        if cc >= 2:
            em.wait("vector", f"qLs{cc % 2}", low_st[cc - 2])
        em.wait("vector", f"qT{rtsl}", rl)
        em.wait("vector", f"qT{htsl}", hl)
        f_done[cc] = recon(cc, rsl[:, 0:1026], hsl[:, 0:1026], osl, 0, 1024)
        em.op("vector", TT, out=azB[:, 0:2048], in0=xn[:, cc * S:(cc + 1) * S],
              in1=osl[:, :], op=ALU.add)
        em.op("vector", TT, out=aof[:, 0:2048], in0=azB[:, 0:2048], in1=azB[:, 0:2048],
              op=ALU.mult)
        if cc == 0:
            em.op("vector", "tensor_copy", out=bcR[:, :], in_=azB[:, 0:2048])
            em.op("vector", "tensor_copy", out=bcB[:, :], in_=aof[:, 0:2048],
                  inc=("tDVE", 1))
        else:
            em.op("vector", TT, out=bcR[:, :], in0=bcR[:, :], in1=azB[:, 0:2048],
                  op=ALU.add)
            em.op("vector", TT, out=bcB[:, :], in0=bcB[:, :], in1=aof[:, 0:2048],
                  op=ALU.add, inc=("tDVE", 1))
        f_done[cc] = em.cnt["tDVE"]
        em.wait("sync", "tDVE", f_done[cc])
        low_st[cc] = em.dma("sync", low_scr[cc], osl[:, :], f"qLs{cc % 2}")

    ln2_acc = em.cnt["tDVE"]

    a_all1 = stage1(1)
    b_all1 = stage23(1, 0, h1sv, lambda oc: abuf.ap()[:, oc * TQ:(oc + 1) * TQ],
                     lambda jc: bbuf.ap()[:, jc * TQ:(jc + 1) * TQ], a_all1, None)

    c_all1 = stage23(1, 1, h2sv, lambda oc: bbuf.ap()[:, oc * TQ:(oc + 1) * TQ],
                     lambda jc: abuf.ap()[:, jc * TQ:(jc + 1) * TQ], None, b_all1)
    stage4(1, c_all1, with_ao=False)
    ln2_done = ln_reduce_bcast(bcR[:, :], bcB[:, :], aof[:, 0:2048],
                               azB[0:1, 0:2048], aof[0:1, 0:2048], ln2_acc)

    ao_st, ao_dve = {}, {}
    for cc in range(NCH):
        isl = lowsl[cc % 2]
        osl = aosl[cc % 2]
        if cc < 2:
            em.wait("sync", "tDVE", ln2_acc)
        else:
            em.wait("sync", "tDVE", ao_dve[cc - 2])
        ll = em.dma("sync", isl[:, :], low_scr[cc], f"qLs{cc % 2}")
        em.wait("vector", f"qLs{cc % 2}", ll)
        if cc >= 2:
            em.wait("vector", f"qAO{cc % 2}", ao_st[cc - 2])
        em.op("vector", TT, out=azB[:, 0:2048], in0=xn[:, cc * S:(cc + 1) * S],
              in1=isl[:, :], op=ALU.add)
        em.op("vector", TT, out=azB[:, 0:2048], in0=azB[:, 0:2048], in1=bcB[:, :],
              op=ALU.mult)
        em.op("vector", TT, out=azB[:, 0:2048], in0=azB[:, 0:2048], in1=bcR[:, :],
              op=ALU.add)
        ao_dve[cc] = em.op("vector", TS, out=osl, in0=azB[:, 0:2048],
                           scalar1=gamv[:, cc:cc + 1], scalar2=betv[:, cc:cc + 1],
                           op0=ALU.mult, op1=ALU.add, inc=("tDVE", 1))
        em.wait("sync", "tDVE", ao_dve[cc])
        ao_st[cc] = em.dma("sync", ao_scr[cc], osl, f"qAO{cc % 2}")
    ao_gate[0] = (em.cnt["qAO0"], em.cnt["qAO1"])

    fx_dve = {}
    fi = 0
    for fq in (0, 1):
        for cc in range(NCH):
            if fi >= 2:
                em.wait("sync", "tDVE", fx_dve[fi - 2])
            em.wait("sync", f"qO{cc % 2}", out_st[(fq, cc)])
            if fi == 0:
                em.wait("sync", "qAO0", ao_gate[0][0])
                em.wait("sync", "qAO1", ao_gate[0][1])
            em.dma("sync", fyb[fi % 2][:, 0:512],
                   yT[cc * 128:(cc + 1) * 128, fq * TQ:(fq + 1) * TQ], f"qF{fi % 2}")
            fl = em.dma("sync", fab[fi % 2], ao_scr[cc][:, fq * TQ:(fq + 1) * TQ],
                        f"qF{fi % 2}")
            em.wait("vector", f"qF{fi % 2}", fl)
            fx_dve[fi] = em.op("vector", TT, out=fyb[fi % 2][:, 0:512],
                               in0=fyb[fi % 2][:, 0:512], in1=fab[fi % 2], op=ALU.add,
                               inc=("tDVE", 1))
            em.wait("sync", "tDVE", fx_dve[fi])
            em.dma("sync", yT[cc * 128:(cc + 1) * 128, fq * TQ:(fq + 1) * TQ],
                   fyb[fi % 2][:, 0:512], f"qF{fi % 2}")
            fi += 1
    for q in range(2, nq):
        a_all = stage1(q)
        b_all = stage23(q, 0, h1sv, lambda oc: abuf.ap()[:, oc * TQ:(oc + 1) * TQ],
                        lambda jc: bbuf.ap()[:, jc * TQ:(jc + 1) * TQ], a_all, None)
        c_all = stage23(q, 1, h2sv, lambda oc: bbuf.ap()[:, oc * TQ:(oc + 1) * TQ],
                        lambda jc: abuf.ap()[:, jc * TQ:(jc + 1) * TQ], None, b_all)
        stage4(q, c_all, with_ao=True)

    em.wait("sync", "qO0", em.cnt["qO0"])
    em.wait("sync", "qO1", em.cnt["qO1"])
    em.wait("sync", "qF0", em.cnt["qF0"])
    em.wait("sync", "qF1", em.cnt["qF1"])
    em.resolve_dwaits()

    if stop_after < 99:
        total = sum(len(em.q[e]) for e in ENGINES)
        budget = int(total * stop_after / 100.0)
        pos = {e: 0 for e in ENGINES}
        cnt = {k: 0 for k in em.cnt}
        done = 0
        progress = True
        while progress and done < budget:
            progress = False
            for e in ENGINES:
                while pos[e] < len(em.q[e]) and done < budget:
                    it = em.q[e][pos[e]]
                    if it[0] == "wait":
                        if cnt[it[1]] < it[2]:
                            break
                    else:
                        if it[4] is not None:
                            cnt[it[4][0]] += it[4][1]
                    pos[e] += 1
                    done += 1
                    progress = True
        for e in ENGINES:
            em.q[e] = em.q[e][:pos[e]]

    with contextlib.ExitStack() as stack:
        sems = {name: stack.enter_context(nc.semaphore(name)) for name in em.cnt}
        blk = stack.enter_context(nc.Block())

        def mk(engname):
            def f(eng):
                em.replay(eng, engname, sems)
            return f
        blk.sync(mk("sync"))
        blk.tensor(mk("tensor"))
        blk.vector(mk("vector"))
        blk.scalar(mk("scalar"))
        blk.gpsimd(mk("gpsimd"))
    for t in reversed(ctxs):
        t.__exit__(None, None, None)
    return nc, em



def _prep_shared(inputs):
    d = {}
    w1t = np.ascontiguousarray(inputs["L0_ffn_w1_w"].T)
    w2t = np.ascontiguousarray(inputs["L0_ffn_w2_w"].T)
    h1t = np.ascontiguousarray(inputs["L0_ffn_h1_w"].T)
    h2t = np.ascontiguousarray(inputs["L0_ffn_h2_w"].T)
    w3t = np.ascontiguousarray(inputs["L0_ffn_w3_w"].T)
    d["w1g"] = np.ascontiguousarray(
        w1t.reshape(8, 128, 8, 512).transpose(2, 1, 0, 3).reshape(8, 128, 4096))
    d["w2g"] = np.ascontiguousarray(
        w2t.reshape(8, 128, 8, 512).transpose(2, 1, 0, 3).reshape(8, 128, 4096))
    d["h1g"] = np.ascontiguousarray(
        h1t.reshape(32, 128, 32, 128).transpose(2, 1, 0, 3).reshape(32, 128, 4096))
    d["h2g"] = np.ascontiguousarray(
        h2t.reshape(32, 128, 32, 128).transpose(2, 1, 0, 3).reshape(32, 128, 4096))
    d["w3g"] = np.ascontiguousarray(
        w3t.reshape(32, 128, 8, 128).transpose(2, 1, 0, 3).reshape(8, 128, 4096))
    d["s1"] = np.ascontiguousarray(inputs["L0_ffn_w1_s"].reshape(32, 128).T)
    d["s2"] = np.ascontiguousarray(inputs["L0_ffn_w2_s"].reshape(32, 128).T)
    d["h1s"] = np.ascontiguousarray(inputs["L0_ffn_h1_s"].reshape(32, 128).T)
    d["h2s"] = np.ascontiguousarray(inputs["L0_ffn_h2_s"].reshape(32, 128).T)
    d["w3s"] = np.ascontiguousarray(inputs["L0_ffn_w3_s"].reshape(8, 128).T)
    d["gam"] = np.ascontiguousarray(inputs["L0_attn_norm_gamma"].reshape(8, 128).T)
    d["bet"] = np.ascontiguousarray(inputs["L0_attn_norm_beta"].reshape(8, 128).T)
    for nm, key in (("hfil", "L0_attn_h_filter"), ("gfil", "L0_attn_g_filter"),
                    ("hsyn", "L0_attn_h_synth"), ("gsyn", "L0_attn_g_synth")):
        f = inputs[key]
        d[nm] = np.ascontiguousarray(f.T.reshape(8, 128, 4).transpose(1, 0, 2))
    d["ident"] = np.eye(128, dtype=np.float32)
    return d


def kernel(**inputs):
    x = np.asarray(inputs["L0_attn_x"], dtype=np.float32)
    shared = _prep_shared({k: np.asarray(v, dtype=np.float32)
                           for k, v in inputs.items() if k != "L0_attn_x"})
    nc, _ = build_program()
    in_maps = []
    for b in range(8):
        m = dict(shared)
        m["xT"] = np.ascontiguousarray(x[b].T)
        in_maps.append(m)
    res = run_bass_kernel_spmd(nc, in_maps, core_ids=list(range(8)))
    out = np.stack([np.asarray(res.results[b]["yT"]).T for b in range(8)], axis=0)
    return np.ascontiguousarray(out.astype(np.float32))
